# version 01
# baseline (speedup 1.0000x reference)
import numpy as np
import ml_dtypes
import concourse.bass as bass
import concourse.mybir as mybir
from concourse.bass_utils import run_bass_kernel_spmd
from contextlib import ExitStack

F32 = mybir.dt.float32
BF16 = mybir.dt.bfloat16
AF = mybir.ActivationFunctionType
ALU = mybir.AluOpType
AX = mybir.AxisListType

B_LOC, S, D, H, HD = 2, 2048, 2048, 16, 128
TOK = B_LOC * S
ST = 512
NST = S // ST
NDC = D // 128
EPS_LN = 1e-5
EPS_RMS = float(np.finfo(np.float32).eps)
MAGIC = 12582912.0

ENGINES = ("sync", "tensor", "vector", "scalar", "gpsimd")

ALL_PHASES = ("prep", "a1", "a2", "b1", "b2", "c2a", "c2b", "c3")

bfnp = ml_dtypes.bfloat16


def host_consts():
    i = np.arange(128)
    ang = 2.0 * np.pi * np.outer(i, i) / 128.0
    onescol = np.zeros((128, 16, 16), np.float32)
    for ec in range(16):
        onescol[:, ec, ec] = 1.0
    hsel = np.zeros((16, 16, 128), np.float32)
    for h in range(16):
        hsel[h, h, :] = 1.0
    c = {
        "c_fre": np.cos(ang),
        "c_fim": -np.sin(ang),
        "c_ifre": np.cos(ang) / 128.0,
        "c_ifim": -np.sin(ang) / 128.0,
        "c_ident": np.eye(128, dtype=np.float32),
        "c_onescol": onescol.reshape(128, 256),
        "c_hsel": hsel.reshape(16, 2048),
        "c_ones": np.ones((128, 512), np.float32),
        "c_onesr": np.ones((1, 128), np.float32),
    }
    out = {}
    for k, v in c.items():
        out[k] = v.astype(bfnp)
    out["c_identf"] = np.eye(128, dtype=np.float32)
    out["c_onesf"] = np.ones((128, 512), np.float32)
    out["c_onesrf"] = np.ones((1, 128), np.float32)
    return out


CONST_SPECS = {k: (v.shape, BF16 if v.dtype == bfnp else F32)
               for k, v in host_consts().items()}

SCRATCH = {
    "xbf":    ((B_LOC, S, D), BF16, "prep", ("a1",)),
    "wqbf":   ((D, D), BF16, "prep", ("a1",)),
    "wkvbf":  ((D, D), BF16, "prep", ("a1",)),
    "wobf":   ((D, D), BF16, "prep", ("a2",)),
    "wqkvbf": ((3 * D, D), BF16, "prep", ("b1",)),
    "wo1bf":  ((D, D), BF16, "prep", ("b2",)),
    "wfcbf":  ((8192, D), BF16, "prep", ("prep",)),
    "wprojbf": ((D, 8192), BF16, "prep", ("prep",)),
    "w1bf":   ((8192, D), BF16, "prep", ("c2a",)),
    "w2bf":   ((D, 8192), BF16, "prep", ("c2b",)),
    "ffnw_s": ((128, 16 * 208), BF16, "prep", ("a2",)),
    "dgn_s":  ((72, D), BF16, "prep", ("a2",)),
    "k1_s":   ((1, 136), BF16, "prep", ("a2",)),
    "gate_s": ((2, 8), F32, "prep", ("a2",)),
    "sn_s":   ((8, 8), F32, "prep", ("prep",)),
    "cc_s":   ((8, 8), F32, "prep", ("prep",)),
    "mn2_s":  ((8, 1), F32, "prep", ("prep",)),
    "vhT":    ((B_LOC, NST, D, ST), BF16, "a1", ("a2",)),
    "h2f":    ((B_LOC, S, D), F32, "a2", ("b2",)),
    "h2bf":   ((B_LOC, S, D), BF16, "a2", ("b1",)),
    "qkvbf":  ((B_LOC, S, 3 * D), BF16, "b1", ("b2",)),
    "h3f":    ((B_LOC, S, D), F32, "b2", ("c3",)),
    "h3bf":   ((B_LOC, S, D), BF16, "b2", ("c2a",)),
    "hrs":    ((64, 128, TOK), BF16, "c2a", ("c2b",)),
    "rms_s":  ((128, 32), F32, "c2a", ("c2b",)),
    "ysb":    ((TOK, D), BF16, "c2b", ("c3",)),
}

INPUT_SPECS = {
    "x": ((B_LOC, S, D), F32),
    "L0_attn_W_q": ((D, D), F32),
    "L0_attn_W_kv": ((D, D), F32),
    "L0_attn_W_o": ((D, D), F32),
    "L0_ffn_select_weight": ((8, D, 8), F32),
    "L0_ffn_select_mean_diff": ((8, D), F32),
    "L0_ffn_imp_center": ((8,), F32),
    "L0_ffn_imp_slope": ((8,), F32),
    "L0_ffn_lora_update": ((8, D, 8), F32),
    "L0_ffn_lora_degen": ((8, D, 8), F32),
    "L0_ffn_bias_w": ((8, D), F32),
    "L0_ffn_debias_w": ((8, D), F32),
    "L1_attn_W_qkv": ((3 * D, D), F32),
    "L1_attn_W_o": ((D, D), F32),
    "L1_ffn_W_fc": ((8192, D), F32),
    "L1_ffn_W_proj": ((D, 8192), F32),
}

PHASE_INPUTS = {
    "prep": ("x", "L0_attn_W_q", "L0_attn_W_kv", "L0_attn_W_o",
             "L0_ffn_select_weight", "L0_ffn_select_mean_diff",
             "L0_ffn_imp_center", "L0_ffn_imp_slope", "L0_ffn_lora_update",
             "L0_ffn_lora_degen", "L0_ffn_bias_w", "L0_ffn_debias_w",
             "L1_attn_W_qkv", "L1_attn_W_o", "L1_ffn_W_fc", "L1_ffn_W_proj"),
    "a1": (), "a2": ("x",), "b1": (), "b2": (), "c2a": (), "c2b": (), "c3": (),
}
PHASE_CONSTS = {
    "prep": ("c_ident", "c_identf"),
    "a1": ("c_fre", "c_fim", "c_ifre", "c_ifim", "c_onescol", "c_hsel", "c_onesf"),
    "a2": ("c_onesr", "c_onesrf", "c_identf", "c_ident", "c_onesf"),
    "b1": (),
    "b2": ("c_ones", "c_onesr"),

    "c2a": ("c_onesf", "c_identf"),
    "c2b": (),
    "c3": (),
}


class Sem:
    def __init__(self, nc, es, name):
        self.h = es.enter_context(nc.semaphore(name))
        self.n = 0


class Prog:

    def __init__(self, ctx=None):
        self.ops = []
        self.ctx = ctx
        self.last = {}

    def add(self, eng, fn, waits=(), inc=None, inck=1, chain=None):
        if chain is None:
            chain = eng in ("vector", "scalar")
        waits = list(waits)
        if eng in ("vector", "scalar"):
            dflt = self.ctx.sem("v" if eng == "vector" else "s")
            if chain and eng in self.last:
                waits.append(self.last[eng])
            if inc is None:
                inc = dflt
                inck = 1
        self.ops.append((eng, fn, tuple(waits), inc, inck))
        if inc is not None:
            inc.n += inck
            if eng in ("vector", "scalar"):
                self.last[eng] = (inc, inc.n)

    def barrier(self, bar):
        target = bar.n + 5
        cur = [(self.ctx.sem(n), self.ctx.sem(n).n)
               for n in ("ld", "t", "v", "s", "g")]
        for g in ENGINES:
            self.ops.append((g, lambda e: e.nop(), tuple(cur), bar, 1))
            bar.n += 1
            self.ops.append((g, None, ((bar, target),), None, 1))
        assert bar.n == target
        self.last = {}

    def check(self):
        import collections
        qs = {g: collections.deque() for g in ENGINES}
        for op in self.ops:
            qs[op[0]].append(op)
        counts = collections.defaultdict(int)
        progress = True
        while progress:
            progress = False
            for g in ENGINES:
                while qs[g]:
                    (eng, fn, waits, inc, inck) = qs[g][0]
                    if any(counts[id(sem)] < val for (sem, val) in waits):
                        break
                    if inc is not None:
                        counts[id(inc)] += inck
                    qs[g].popleft()
                    progress = True
        stuck = {g: len(qs[g]) for g in ENGINES if qs[g]}
        if stuck:
            msgs = []
            for g in ENGINES:
                if not qs[g]:
                    continue
                (eng, fn, waits, inc, inck) = qs[g][0]
                bad = [(self._semname(sem), val, counts[id(sem)])
                       for (sem, val) in waits if counts[id(sem)] < val]
                msgs.append(f"{g}: {len(qs[g])} ops stuck; head waits {bad}")
            raise RuntimeError("DEADLOCK in schedule:\n" + "\n".join(msgs))

    def _semname(self, sem):
        for k, v in self.ctx.sems.items():
            if v is sem:
                return k
        return "?"

    def emit(self, block):
        self.check()
        for engname in ENGINES:
            items = [(f, w, i, k) for (g, f, w, i, k) in self.ops if g == engname]
            if not items:
                continue

            def mk(items):
                def run(e):
                    for (f, w, inc, k) in items:
                        for (sem, val) in w:
                            e.wait_ge(sem.h, val)
                        if f is None:
                            continue
                        ins = f(e)
                        if inc is not None:
                            assert ins is not None
                            ins.then_inc(inc.h, k)
                return run
            getattr(block, engname)(mk(items))


class Ctx:
    def __init__(self, nc, es, phases):
        self.nc = nc
        self.es = es
        self.p = Prog(self)
        self.phases = phases
        self.dram = {}
        self.sems = {}
        for name in ("ld", "ld2", "t", "v", "s", "g", "bar"):
            self.sems[name] = Sem(nc, es, "sem_" + name)

    def sem(self, n):
        return self.sems[n]

    def sb(self, name, shape, dtype, stack=None):
        return (stack or self.es).enter_context(self.nc.sbuf_tensor(name, list(shape), dtype))

    def ps(self, name, shape, dtype=F32, stack=None):
        return (stack or self.es).enter_context(self.nc.psum_tensor(name, list(shape), dtype))


def declare_dram(ctx, group_phases):
    nc = ctx.nc
    phs = set(group_phases)
    needed_inputs = set()
    needed_consts = set()
    for ph in phs:
        needed_inputs.update(PHASE_INPUTS[ph])
        needed_consts.update(PHASE_CONSTS[ph])
    for name in sorted(needed_inputs):
        shape, dt = INPUT_SPECS[name]
        ctx.dram[name] = nc.dram_tensor(name, list(shape), dt, kind="ExternalInput").ap()
    for name in sorted(needed_consts):
        shape, dt = CONST_SPECS[name]
        ctx.dram[name] = nc.dram_tensor(name, list(shape), dt, kind="ExternalInput").ap()
    ext_in, ext_out = [], []
    for name, (shape, dt, producer, consumers) in SCRATCH.items():
        prod_here = producer in phs
        cons_here = any(c in phs for c in consumers)
        cons_later = any(c not in phs for c in consumers)
        if not prod_here and not cons_here:
            continue
        if prod_here and not cons_later:
            kind = "Internal"
        elif prod_here:
            kind = "ExternalOutput"
            ext_out.append(name)
        else:
            kind = "ExternalInput"
            ext_in.append(name)
        ctx.dram[name] = nc.dram_tensor(name, list(shape), dt, kind=kind).ap()
    if "c3" in phs:
        ctx.dram["out"] = nc.dram_tensor("out", [B_LOC, S, D], F32, kind="ExternalOutput").ap()
        ext_out.append("out")
    return sorted(needed_inputs | needed_consts) + ext_in, ext_out



def ph_prep(ctx):
    nc, p, dr = ctx.nc, ctx.p, ctx.dram
    g = ctx.sem("g")
    ld = ctx.sem("ld")
    v = ctx.sem("v")
    t = ctx.sem("t")
    s = ctx.sem("s")

    casts = [
        (dr["xbf"][0], dr["x"][0]), (dr["xbf"][1], dr["x"][1]),
        (dr["wqbf"], dr["L0_attn_W_q"]), (dr["wkvbf"], dr["L0_attn_W_kv"]),
        (dr["wobf"], dr["L0_attn_W_o"]), (dr["wo1bf"], dr["L1_attn_W_o"]),
        (dr["wqkvbf"][0:3072], dr["L1_attn_W_qkv"][0:3072]),
        (dr["wqkvbf"][3072:6144], dr["L1_attn_W_qkv"][3072:6144]),
        (dr["wfcbf"][0:4096], dr["L1_ffn_W_fc"][0:4096]),
        (dr["wfcbf"][4096:8192], dr["L1_ffn_W_fc"][4096:8192]),
        (dr["wprojbf"], dr["L1_ffn_W_proj"]),
        (dr["dgn_s"][64:72], dr["L0_ffn_bias_w"]),
    ]
    for (o, i) in casts:
        p.add("gpsimd", lambda e, o=o, i=i: e.dma_start(out=o, in_=i), inc=g, inck=16)
    cast_done = g.n

    tern = ctx.sb("tern_in", (128, 2 * 8192), BF16)
    tred = ctx.sb("tred", (128, 128), F32)
    tsc = ctx.sb("tsc", (128, 128), BF16)
    trec = ctx.sb("trec", (128, 128), F32)
    trat = ctx.sb("trat", (128, 8192), BF16)
    ternf = ctx.sb("ternf", (128, 8192), F32)
    tscf = ctx.sb("tscf", (128, 128), F32)
    tq = ctx.sb("tq", (128, 8192), BF16)
    tw1 = ctx.sb("tw1", (128, 2 * 8192), BF16)

    jobs = []
    wfc3 = dr["wfcbf"].rearrange("(n p) d -> n p d", p=128)
    w1b3 = dr["w1bf"].rearrange("(n p) d -> n p d", p=128)
    for i in range(64):
        jobs.append((wfc3[i], w1b3[i], 2048))
    wpr3 = dr["wprojbf"].rearrange("(n p) d -> n p d", p=128)
    w2b3 = dr["w2bf"].rearrange("(n p) d -> n p d", p=128)
    for i in range(16):
        jobs.append((wpr3[i], w2b3[i], 8192))

    v_per_tile = 11
    tern_v0 = v.n
    for i, (src, dst, F) in enumerate(jobs):
        bi = i % 2
        ti = tern[:, bi * 8192: bi * 8192 + F]
        to = tw1[:, bi * 8192: bi * 8192 + F]
        G = F // 64
        w_in = [(g, cast_done)]
        if i >= 2:
            w_in.append((v, tern_v0 + (i - 1) * v_per_tile))
        p.add("sync", lambda e, ti=ti, src=src: e.dma_start(out=ti, in_=src),
              waits=w_in, inc=ld, inck=16)
        w_v = [(ld, ld.n)]
        if i >= 2:
            w_v.append((g, cast_done + (i - 1) * 16))
        ti3 = ti.rearrange("p (gg o) -> p gg o", o=64)
        tf3 = ternf[:, 0:F].rearrange("p (gg o) -> p gg o", o=64)
        p.add("vector", lambda e, ti3=ti3, G=G: e.tensor_reduce(
            tred[:, 0:G].unsqueeze(2), ti3, AX.X, ALU.add, apply_absolute_value=True),
            waits=w_v)
        p.add("vector", lambda e, G=G: e.tensor_scalar(
            tsc[:, 0:G], tred[:, 0:G], 1.0 / 64.0, 1e-8, ALU.mult, ALU.max))
        p.add("vector", lambda e, G=G: e.reciprocal(trec[:, 0:G], tsc[:, 0:G]))
        p.add("vector", lambda e, ti=ti, F=F: e.tensor_copy(ternf[:, 0:F], ti))
        p.add("vector", lambda e, tf3=tf3, G=G: e.tensor_tensor(
            tf3, tf3, trec[:, 0:G].unsqueeze(2).broadcast_to([128, G, 64]), ALU.mult))
        p.add("vector", lambda e, F=F: e.tensor_copy(trat[:, 0:F], ternf[:, 0:F]))
        p.add("vector", lambda e, F=F: e.tensor_scalar(
            ternf[:, 0:F], trat[:, 0:F], MAGIC, MAGIC, ALU.add, ALU.subtract))
        p.add("vector", lambda e, F=F: e.tensor_scalar(
            ternf[:, 0:F], ternf[:, 0:F], 1.0, -1.0, ALU.min, ALU.max))
        p.add("vector", lambda e, G=G: e.tensor_copy(tscf[:, 0:G], tsc[:, 0:G]))
        p.add("vector", lambda e, tf3=tf3, G=G: e.tensor_tensor(
            tf3, tf3, tscf[:, 0:G].unsqueeze(2).broadcast_to([128, G, 64]), ALU.mult))
        p.add("vector", lambda e, to=to, F=F: e.tensor_copy(to, ternf[:, 0:F]))
        p.add("gpsimd", lambda e, to=to, dst=dst: e.dma_start(out=dst, in_=to),
              waits=[(v, v.n)], inc=g, inck=16)
    tern_v_end = v.n

    mu_f = ctx.sb("mu_f", (8, D), F32)
    deb_f = ctx.sb("deb_f", (8, D), F32)
    musq = ctx.sb("musq", (8, D), F32)
    mn2 = ctx.sb("mn2", (8, 1), F32)
    slp0 = ctx.sb("slp0", (1, 8), F32)
    slpc = ctx.sb("slpc", (1, 8), F32)
    p.add("sync", lambda e: e.dma_start(out=mu_f[:], in_=dr["L0_ffn_select_mean_diff"][:, :]), inc=ld, inck=16)
    p.add("sync", lambda e: e.dma_start(out=deb_f[:], in_=dr["L0_ffn_debias_w"][:, :]), inc=ld, inck=16)
    p.add("sync", lambda e: e.dma_start(out=slp0[:], in_=dr["L0_ffn_imp_slope"].unsqueeze(0)), inc=ld, inck=16)
    p.add("sync", lambda e: e.dma_start(out=slpc[:], in_=dr["L0_ffn_imp_center"].unsqueeze(0)), inc=ld, inck=16)
    small_ld = ld.n
    p.add("vector", lambda e: e.tensor_tensor(musq[:], mu_f[:], mu_f[:], ALU.mult),
          waits=[(ld, small_ld)], inc=v)
    p.add("vector", lambda e: e.tensor_reduce(mn2[:], musq[:], AX.X, ALU.add), inc=v)
    p.add("vector", lambda e: e.tensor_tensor(slpc[:], slp0[:], slpc[:], ALU.mult), inc=v)
    p.add("gpsimd", lambda e: e.dma_start(out=dr["gate_s"][0:1, :], in_=slp0[:]),
          waits=[(v, v.n)], inc=g, inck=16)
    p.add("gpsimd", lambda e: e.dma_start(out=dr["gate_s"][1:2, :], in_=slpc[:]),
          waits=[(v, v.n)], inc=g, inck=16)
    p.add("gpsimd", lambda e: e.dma_start(out=dr["mn2_s"][:, :], in_=mn2[:]),
          waits=[(v, v.n)], inc=g, inck=16)
    small_v = v.n

    ffnw = ctx.sb("ffnw", (128, 16 * 208), BF16)
    muhi = ctx.sb("muhi", (128, 16 * 8), BF16)
    debhi = ctx.sb("debhi", (128, 16 * 8), BF16)
    ident_f = ctx.sb("ident_f", (128, 128), F32)
    ident_b = ctx.sb("ident_b", (128, 128), BF16)
    p.add("sync", lambda e: e.dma_start(out=ident_f[:], in_=dr["c_identf"][:, :]), inc=ld, inck=16)
    p.add("sync", lambda e: e.dma_start(out=ident_b[:], in_=dr["c_ident"][:, :]), inc=ld, inck=16)
    ident_ld = ld.n
    ps_a = ctx.ps("pp_a", (128, 512))
    ps_b = ctx.ps("pp_b", (128, 512))
    mulo_t = ctx.sb("mulo_t", (128, 8), BF16)
    muf32 = ctx.sb("muf32", (128, 8), F32)
    hif32 = ctx.sb("hif32", (128, 8), F32)
    for which, srcf, hidst in (("mu", mu_f, muhi), ("deb", deb_f, debhi)):
        for dc in range(16):
            pst = ps_a[:, 0:8] if dc % 2 == 0 else ps_b[:, 0:8]
            p.add("tensor", lambda e, srcf=srcf, dc=dc, pst=pst: e.transpose(
                pst, srcf[:, dc * 128:(dc + 1) * 128], ident_f[0:8, 0:8]),
                waits=[(ld, ident_ld), (v, v.n)], inc=t)
            p.add("vector", lambda e, pst=pst: e.tensor_copy(muf32[:], pst),
                  waits=[(t, t.n)], inc=v)
            p.add("vector", lambda e, hidst=hidst, dc=dc: e.tensor_copy(
                hidst[:, dc * 8:(dc + 1) * 8], muf32[:]), inc=v)
            if which == "mu":
                p.add("vector", lambda e, dc=dc: e.tensor_copy(
                    hif32[:], muhi[:, dc * 8:(dc + 1) * 8]), inc=v)
                p.add("vector", lambda e: e.tensor_tensor(
                    mulo_t[:], muf32[:], hif32[:], ALU.subtract), inc=v)
                p.add("vector", lambda e, dc=dc: e.tensor_scalar_mul(
                    ffnw[:, dc * 208 + 64: dc * 208 + 72], muhi[:, dc * 8:(dc + 1) * 8], -2.0), inc=v)
                p.add("vector", lambda e, dc=dc: e.tensor_scalar_mul(
                    ffnw[:, dc * 208 + 200: dc * 208 + 208], mulo_t[:], -2.0), inc=v)
    mu_v = v.n

    wsl_t = ctx.sb("wsl_t", (128, 8), F32)
    for n in range(8):
        for dc in range(16):
            src = dr["L0_ffn_select_weight"][n, dc * 128:(dc + 1) * 128, :]
            p.add("sync", lambda e, src=src: e.dma_start(out=wsl_t[:], in_=src),
                  waits=[(v, v.n)], inc=ld, inck=16)
            p.add("vector", lambda e, dc=dc, n=n: e.tensor_copy(
                ffnw[:, dc * 208 + n * 8: dc * 208 + n * 8 + 8], wsl_t[:]),
                waits=[(ld, ld.n)], inc=v)
            p.add("vector", lambda e, dc=dc, n=n: e.tensor_copy(
                hif32[:], ffnw[:, dc * 208 + n * 8: dc * 208 + n * 8 + 8]), inc=v)
            p.add("vector", lambda e, dc=dc, n=n: e.tensor_tensor(
                ffnw[:, dc * 208 + 136 + n * 8: dc * 208 + 136 + n * 8 + 8], wsl_t[:],
                hif32[:], ALU.subtract), inc=v)
    for n in range(8):
        for dc in range(16):
            src = dr["L0_ffn_lora_update"][n, dc * 128:(dc + 1) * 128, :]
            p.add("sync", lambda e, src=src: e.dma_start(out=wsl_t[:], in_=src),
                  waits=[(v, v.n)], inc=ld, inck=16)
            p.add("vector", lambda e, dc=dc, n=n: e.tensor_copy(
                ffnw[:, dc * 208 + 72 + n * 8: dc * 208 + 72 + n * 8 + 8], wsl_t[:]),
                waits=[(ld, ld.n)], inc=v)
    ffnw_v = v.n
    p.add("gpsimd", lambda e: e.dma_start(out=dr["ffnw_s"][:, :], in_=ffnw[:]),
          waits=[(v, ffnw_v)], inc=g, inck=16)

    snneg = ctx.sb("snneg", (8, 8), F32)
    ccneg = ctx.sb("ccneg", (8, 8), F32)
    for dstbuf, colbase, vecbuf in ((snneg, 0, muhi), (ccneg, 72, debhi)):
        for n in range(8):
            for dc in range(16):
                lhsT = ffnw[:, dc * 208 + colbase + n * 8: dc * 208 + colbase + n * 8 + 8]
                rhs = vecbuf[:, dc * 8 + n: dc * 8 + n + 1]
                p.add("tensor", lambda e, lhsT=lhsT, rhs=rhs, dc=dc: e.matmul(
                    ps_a[0:8, 0:1], lhsT, rhs, start=(dc == 0), stop=(dc == 15)),
                    waits=[(v, ffnw_v)] if dc == 0 else (), inc=t if dc == 15 else None)
            p.add("vector", lambda e, dstbuf=dstbuf, n=n: e.tensor_scalar_mul(
                dstbuf[:, n:n + 1], ps_a[0:8, 0:1], -1.0), waits=[(t, t.n)], inc=v)
            p.add("tensor", lambda e: e.nop(), waits=[(v, v.n)])
    p.add("gpsimd", lambda e: e.dma_start(out=dr["sn_s"][:, :], in_=snneg[:]),
          waits=[(v, v.n)], inc=g, inck=16)
    p.add("gpsimd", lambda e: e.dma_start(out=dr["cc_s"][:, :], in_=ccneg[:]),
          waits=[(v, v.n)], inc=g, inck=16)
    sncc_g = g.n
    p.add("gpsimd", lambda e: e.dma_start(
        out=dr["k1_s"][0:1, 0:64].rearrange("o (n s) -> o n s", s=8),
        in_=dr["sn_s"].transpose([1, 0]).unsqueeze(0),
        allow_slow_non_contiguous=True),
        waits=[(g, sncc_g)], inc=g, inck=16)
    p.add("gpsimd", lambda e: e.dma_start(
        out=dr["k1_s"][0:1, 64:72], in_=dr["mn2_s"].rearrange("n o -> (n o)").unsqueeze(0)),
        inc=g, inck=16)
    p.add("gpsimd", lambda e: e.dma_start(
        out=dr["k1_s"][0:1, 72:136].rearrange("o (n h) -> o n h", h=8),
        in_=dr["cc_s"].transpose([1, 0]).unsqueeze(0),
        allow_slow_non_contiguous=True),
        inc=g, inck=16)

    dgrow = ctx.sb("dgrow", (8, 16 * 128), BF16)
    ldg_f = ctx.sb("ldg_f", (128, 8), F32)
    ldg_b = ctx.sb("ldg_b", (128, 8), BF16)
    ps_tb = ctx.ps("pp_tb", (128, 1024), BF16)
    for n in range(8):
        for dc in range(16):
            src = dr["L0_ffn_lora_degen"][n, dc * 128:(dc + 1) * 128, :]
            p.add("sync", lambda e, src=src: e.dma_start(out=ldg_f[:], in_=src),
                  waits=[(v, v.n)], inc=ld, inck=16)
            p.add("vector", lambda e: e.tensor_copy(ldg_b[:], ldg_f[:]),
                  waits=[(ld, ld.n)], inc=v)
            pst_b = ps_tb[0:8, (dc % 2) * 512:(dc % 2) * 512 + 128]
            p.add("tensor", lambda e, pst_b=pst_b: e.transpose(pst_b, ldg_b[:], ident_b[:]),
                  waits=[(v, v.n)], inc=t)
            p.add("vector", lambda e, dc=dc, pst_b=pst_b: e.tensor_copy(
                dgrow[:, dc * 128:(dc + 1) * 128], pst_b), waits=[(t, t.n)], inc=v)
        p.add("gpsimd", lambda e, n=n: e.dma_start(
            out=dr["dgn_s"][n * 8:(n + 1) * 8, :], in_=dgrow[:]),
            waits=[(v, v.n)], inc=g, inck=16)
    p.add("gpsimd", lambda e, gn=g.n: e.wait_ge(g.h, gn))



def ph_a1(ctx):
    nc, p, dr = ctx.nc, ctx.p, ctx.dram
    ld, t, v, s, g = (ctx.sem(n) for n in ("ld", "t", "v", "s", "g"))

    wq = ctx.sb("a1_wq", (128, 16 * 2048), BF16)
    wkv = ctx.sb("a1_wkv", (128, 16 * 2048), BF16)
    xT = ctx.sb("a1_xT", (128, 16 * 512), BF16)
    qsb = ctx.sb("a1_q", (128, 16 * 512), BF16)
    kvsb = ctx.sb("a1_kv", (128, 16 * 512), BF16)
    sqb = ctx.sb("a1_sq", (128, 2 * 512), BF16)
    rstdq = ctx.sb("a1_rstdq", (16, 512), BF16)
    rstdkv = ctx.sb("a1_rstdkv", (16, 512), BF16)
    rtmp = ctx.sb("a1_rtmp", (16, 512), F32)
    qn = ctx.sb("a1_qn", (128, 512), BF16)
    kvn = ctx.sb("a1_kvn", (128, 512), F32)
    kvcs = ctx.sb("a1_kvcs", (128, 512), F32)
    kvcsb = ctx.sb("a1_kvcsb", (128, 512), BF16)
    carry = ctx.sb("a1_carry", (128, 16), F32)
    ffre = ctx.sb("a1_ffre", (128, 128), BF16)
    ffim = ctx.sb("a1_ffim", (128, 128), BF16)
    fire = ctx.sb("a1_fire", (128, 128), BF16)
    fiim = ctx.sb("a1_fiim", (128, 128), BF16)
    onescol = ctx.sb("a1_ones", (128, 256), BF16)
    hsel = ctx.sb("a1_hsel", (16, 2048), BF16)
    onesd = ctx.sb("a1_onesd", (128, 512), F32)
    bcb = ctx.sb("a1_bcb", (128, 512), BF16)
    qfre = ctx.sb("a1_qfre", (128, 512), BF16)
    qfim = ctx.sb("a1_qfim", (128, 512), BF16)
    kfre = ctx.sb("a1_kfre", (128, 512), BF16)
    kfim = ctx.sb("a1_kfim", (128, 512), BF16)
    qvre = ctx.sb("a1_qvre", (128, 512), BF16)
    qvim = ctx.sb("a1_qvim", (128, 512), BF16)
    tmpc = ctx.sb("a1_tmpc", (128, 512), F32)
    vhsb = ctx.sb("a1_vh", (128, 2 * 512), BF16)

    P = [ctx.ps(f"a1_ps{i}", (128, 512)) for i in range(8)]

    for dst, nmc in ((ffre, "c_fre"), (ffim, "c_fim"), (fire, "c_ifre"), (fiim, "c_ifim"),
                     (onescol, "c_onescol"), (hsel, "c_hsel"), (onesd, "c_onesf")):
        p.add("sync", lambda e, dst=dst, nmc=nmc: e.dma_start(out=dst[:], in_=dr[nmc][:, :]),
              inc=ld, inck=16)
    for wsb, wd in ((wq, "wqbf"), (wkv, "wkvbf")):
        for dc in range(16):
            for eb in range(4):
                src = dr[wd][eb * 512:(eb + 1) * 512, dc * 128:(dc + 1) * 128]
                dst = wsb[:, dc * 2048 + eb * 512: dc * 2048 + (eb + 1) * 512]
                p.add("sync", lambda e, dst=dst, src=src: e.dma_start_transpose(dst, src),
                      inc=ld, inck=16)
    const_ld = ld.n

    def xT_ap(buf_i, dc):
        return xT[:, dc * 512: (dc + 1) * 512]

    n_steps = B_LOC * NST
    evq_s = {}
    step_t_end = {}

    for step in range(n_steps):
        b, st = divmod(step, NST)
        bi = step % 2
        w_l = [(ld, const_ld)]
        if step >= 1:
            w_l.append((t, step_t_end[step - 1]))
        for dc in range(16):
            src = dr["xbf"][b, st * 512:(st + 1) * 512, dc * 128:(dc + 1) * 128]
            p.add("sync", lambda e, d=xT_ap(bi, dc), src=src: e.dma_start_transpose(d, src),
                  waits=w_l if dc == 0 else (), inc=ld, inck=16)
        ld_step = ld.n

        if st == 0:
            p.add("vector", lambda e: e.memset(carry[:], 0.0), inc=v)

        if step == 0:
            bank_evac = {}
            nsq_t = {}
            vh_store = {}
        for wi, (wsb, dstq, nsqbank) in enumerate(((wq, qsb, 4), (wkv, kvsb, 5))):
            for ec in range(16):
                bidx = wi * 2 + (ec % 2)
                bank = P[bidx]
                w_mm = [(ld, ld_step)]
                if bidx in bank_evac:
                    w_mm.append((s, bank_evac[bidx]))
                for dc in range(16):
                    lhsT = wsb[:, dc * 2048 + ec * 128: dc * 2048 + (ec + 1) * 128]
                    p.add("tensor", lambda e, bank=bank, lhsT=lhsT, dc=dc, bi=bi: e.matmul(
                        bank[:], lhsT, xT_ap(bi, dc), start=(dc == 0), stop=(dc == 15)),
                        waits=w_mm if dc == 0 else (), inc=t if dc == 15 else None)
                mm_done = t.n
                p.add("scalar", lambda e, bank=bank, dstq=dstq, ec=ec: e.copy(
                    dstq[:, ec * 512:(ec + 1) * 512], bank[:]), waits=[(t, mm_done)], inc=s)
                w_sq = [(t, nsq_t[ec % 2])] if (ec % 2) in nsq_t else []
                p.add("scalar", lambda e, bank=bank, ec=ec: e.activation(
                    sqb[:, (ec % 2) * 512:(ec % 2 + 1) * 512], bank[:], AF.Square),
                    waits=w_sq, inc=s)
                sq_done = s.n
                bank_evac[bidx] = sq_done
                p.add("tensor", lambda e, nb=P[nsqbank], ec=ec: e.matmul(
                    nb[0:16, :], onescol[:, ec * 16:(ec + 1) * 16],
                    sqb[:, (ec % 2) * 512:(ec % 2 + 1) * 512],
                    start=(ec == 0), stop=(ec == 15)),
                    waits=[(s, sq_done)], inc=t)
                nsq_t[ec % 2] = t.n
        nsq_done = t.n

        for nsqbank, rdst in ((4, rstdq), (5, rstdkv)):
            p.add("vector", lambda e, nb=P[nsqbank]: e.tensor_scalar_mul(
                rtmp[:], nb[0:16, :], 128.0), waits=[(t, nsq_done)], inc=v)
            p.add("scalar", lambda e: e.activation(rtmp[:], rtmp[:], AF.Ln),
                  waits=[(v, v.n)], inc=s)
            p.add("vector", lambda e: e.tensor_scalar_mul(rtmp[:], rtmp[:], -0.5),
                  waits=[(s, s.n)], inc=v)
            p.add("scalar", lambda e: e.activation(rtmp[:], rtmp[:], AF.Exp),
                  waits=[(v, v.n)], inc=s)
            p.add("vector", lambda e, rdst=rdst: e.tensor_copy(rdst[:], rtmp[:]),
                  waits=[(s, s.n)], inc=v)
        rstd_done = v.n

        for ec in range(16):
            bcp = P[0]
            p.add("tensor", lambda e, ec=ec: e.matmul(
                bcp[:], hsel[:, ec * 128:(ec + 1) * 128], rstdq[:], start=True, stop=True),
                waits=[(v, v.n)], inc=t)
            p.add("vector", lambda e: e.tensor_copy(bcb[:], bcp[:]), waits=[(t, t.n)])
            p.add("vector", lambda e, ec=ec: e.tensor_tensor(
                qn[:], qsb[:, ec * 512:(ec + 1) * 512], bcb[:], ALU.mult))
            qn_done = v.n
            p.add("tensor", lambda e: e.matmul(P[1][:], ffre[:], qn[:], start=True, stop=True),
                  waits=[(v, qn_done)], inc=t)
            p.add("tensor", lambda e: e.matmul(P[2][:], ffim[:], qn[:], start=True, stop=True),
                  inc=t)
            qf_mm = t.n
            p.add("scalar", lambda e: e.copy(qfre[:], P[1][:]), waits=[(t, qf_mm)], inc=s)
            p.add("scalar", lambda e: e.copy(qfim[:], P[2][:]), inc=s)
            qf_ev = s.n
            p.add("tensor", lambda e, ec=ec: e.matmul(
                bcp[:], hsel[:, ec * 128:(ec + 1) * 128], rstdkv[:], start=True, stop=True),
                waits=[(v, qn_done)], inc=t)
            p.add("vector", lambda e: e.tensor_copy(bcb[:], bcp[:]), waits=[(t, t.n)])
            p.add("vector", lambda e, ec=ec: e.tensor_tensor(
                kvn[:], kvsb[:, ec * 512:(ec + 1) * 512], bcb[:], ALU.mult))
            p.add("vector", lambda e, ec=ec: e.tensor_tensor_scan(
                kvcs[:], onesd[:], kvn[:], carry[:, ec:ec + 1], ALU.mult, ALU.add), inc=v)
            p.add("vector", lambda e, ec=ec: e.tensor_copy(carry[:, ec:ec + 1], kvcs[:, 511:512]), inc=v)
            p.add("vector", lambda e: e.tensor_copy(kvcsb[:], kvcs[:]), inc=v)
            kvcs_done = v.n
            p.add("tensor", lambda e: e.matmul(P[3][:], ffre[:], kvcsb[:], start=True, stop=True),
                  waits=[(v, kvcs_done)], inc=t)
            p.add("tensor", lambda e: e.matmul(P[4][:], ffim[:], kvcsb[:], start=True, stop=True),
                  inc=t)
            kf_mm = t.n
            p.add("scalar", lambda e: e.copy(kfre[:], P[3][:]), waits=[(t, kf_mm)], inc=s)
            p.add("scalar", lambda e: e.copy(kfim[:], P[4][:]), inc=s)
            kf_ev = s.n
            p.add("vector", lambda e: e.tensor_tensor(tmpc[:], qfre[:], kfre[:], ALU.mult),
                  waits=[(s, kf_ev)], inc=v)
            p.add("vector", lambda e: e.tensor_tensor(qvre[:], qfim[:], kfim[:], ALU.mult), inc=v)
            p.add("vector", lambda e: e.tensor_tensor(qvre[:], tmpc[:], qvre[:], ALU.subtract), inc=v)
            p.add("vector", lambda e: e.tensor_tensor(tmpc[:], qfre[:], kfim[:], ALU.mult), inc=v)
            p.add("vector", lambda e: e.tensor_tensor(qvim[:], qfim[:], kfre[:], ALU.mult), inc=v)
            p.add("vector", lambda e: e.tensor_tensor(qvim[:], tmpc[:], qvim[:], ALU.add), inc=v)
            qv_done = v.n
            p.add("tensor", lambda e: e.matmul(P[5][:], fire[:], qvre[:], start=True, stop=False),
                  waits=[(v, qv_done)], inc=None)
            p.add("tensor", lambda e: e.matmul(P[5][:], fiim[:], qvim[:], start=False, stop=True),
                  inc=t)
            vh_mm = t.n
            w_vh = [(t, vh_mm)]
            if (ec % 2) in vh_store:
                w_vh.append((g, vh_store[ec % 2]))
            p.add("scalar", lambda e, ec=ec: e.copy(
                vhsb[:, (ec % 2) * 512:(ec % 2 + 1) * 512], P[5][:]),
                waits=w_vh, inc=s)
            vh_ev = s.n
            dst = dr["vhT"][b, st, ec * 128:(ec + 1) * 128, :]
            p.add("gpsimd", lambda e, dst=dst, ec=ec: e.dma_start(
                out=dst, in_=vhsb[:, (ec % 2) * 512:(ec % 2 + 1) * 512]),
                waits=[(s, vh_ev)], inc=g, inck=16)
            vh_store[ec % 2] = g.n
        step_t_end[step] = t.n
    p.add("gpsimd", lambda e, gn=g.n: e.wait_ge(g.h, gn))



def _ln_stats(ctx, pre_ap, nfree, tl, eps):
    p, v, s = ctx.p, ctx.sem("v"), ctx.sem("s")
    p.add("scalar", lambda e: e.activation(
        tl["dump"][:, 0:nfree], pre_ap, AF.Identity, accum_out=tl["sumb"][:, 0:1]),
        waits=[(v, v.n)])
    p.add("vector", lambda e: e.tensor_scalar_mul(
        tl["negm"][:, 0:1], tl["sumb"][:, 0:1], -1.0 / nfree), waits=[(s, s.n)])
    p.add("scalar", lambda e: e.activation(
        tl["dump"][:, 0:nfree], pre_ap, AF.Square, bias=tl["negm"][:, 0:1],
        accum_out=tl["ssq"][:, 0:1]), waits=[(v, v.n)])
    p.add("vector", lambda e, eps=eps: e.tensor_scalar(
        tl["vpe"][:, 0:1], tl["ssq"][:, 0:1], 1.0 / nfree, eps, ALU.mult, ALU.add),
        waits=[(s, s.n)])
    p.add("scalar", lambda e: e.activation(
        tl["rstd"][:, 0:1], tl["vpe"][:, 0:1], AF.Ln), waits=[(v, v.n)])
    p.add("vector", lambda e: e.tensor_scalar_mul(
        tl["rstd"][:, 0:1], tl["rstd"][:, 0:1], -0.5), waits=[(s, s.n)])
    p.add("scalar", lambda e: e.activation(
        tl["rstd"][:, 0:1], tl["rstd"][:, 0:1], AF.Exp), waits=[(v, v.n)])
    p.add("vector", lambda e: e.tensor_scalar_mul(
        tl["mean"][:, 0:1], tl["sumb"][:, 0:1], 1.0 / nfree), waits=[(s, s.n)])


def ph_a2(ctx):
    nc, p, dr = ctx.nc, ctx.p, ctx.dram
    ld, t, v, s, g = (ctx.sem(n) for n in ("ld", "t", "v", "s", "g"))

    wo = ctx.sb("a2_wo", (128, 16 * 2048), BF16)
    vh = ctx.sb("a2_vh", (128, 16 * 512), BF16)
    ffnw = ctx.sb("a2_ffnw", (128, 16 * 208), BF16)
    dgn = ctx.sb("a2_dgn", (72, 2048), BF16)
    k1 = ctx.sb("a2_k1", (1, 136), BF16)
    g0 = ctx.sb("a2_g0", (1, 8), F32)
    g1 = ctx.sb("a2_g1", (1, 8), F32)
    slope_b = ctx.sb("a2_slb", (128, 8), F32)
    slc_b = ctx.sb("a2_slc", (128, 8), F32)
    onesr = ctx.sb("a2_onesr", (1, 128), BF16)
    onesrf = ctx.sb("a2_onesrf", (1, 128), F32)
    identf = ctx.sb("a2_idf", (128, 128), F32)
    identb = ctx.sb("a2_idb", (128, 128), BF16)
    onesf = ctx.sb("a2_onesf", (128, 8), F32)
    xf = ctx.sb("a2_xf", (128, 2 * 2048), F32)
    pre = ctx.sb("a2_pre", (128, 2048), F32)
    h1f = ctx.sb("a2_h1", (128, 2048), F32)
    dump = ctx.sb("a2_dump", (128, 2048), F32)
    xhi = ctx.sb("a2_xhi", (128, 2048), BF16)
    xlo = ctx.sb("a2_xlo", (128, 2048), BF16)
    tpf = ctx.sb("a2_tpf", (128, 128), F32)
    tpf2 = ctx.sb("a2_tpf2", (128, 128), F32)
    sel = ctx.sb("a2_sel", (128, 136), F32)
    h2 = ctx.sb("a2_h2", (128, 2048), F32)
    h2b = ctx.sb("a2_h2b", (128, 2048), BF16)
    hext = ctx.sb("a2_hext", (128, 72), BF16)
    hexT = ctx.sb("a2_hexT", (72, 128), BF16)
    sm = {k: ctx.sb("a2_" + k, (128, 8), F32) for k in
          ("sumb", "negm", "ssq", "vpe", "rstd", "mean", "sumx2", "r2", "rr",
           "s1", "score", "z", "expz", "gate", "gmax", "ge", "pfx", "eq1",
           "oh", "s2m", "macc")}
    sm["dump"] = dump
    d2 = ctx.sb("a2_d2", (128, 64), F32)
    tmp512 = ctx.sb("a2_tmp512", (128, 512), F32)

    P = [ctx.ps(f"a2_ps{i}", (128, 512)) for i in range(7)]
    ps_tb = ctx.ps("a2_pstb", (128, 1024), BF16)

    for dst, nmc in ((onesr, "c_onesr"), (onesrf, "c_onesrf"), (identf, "c_identf"),
                     (identb, "c_ident")):
        p.add("sync", lambda e, dst=dst, nmc=nmc: e.dma_start(out=dst[:], in_=dr[nmc][:, :]),
              inc=ld, inck=16)
    p.add("sync", lambda e: e.dma_start(out=onesf[:], in_=dr["c_onesf"][:, 0:8],
                                        allow_slow_non_contiguous=True), inc=ld, inck=16)
    p.add("sync", lambda e: e.dma_start(out=ffnw[:], in_=dr["ffnw_s"][:, :]), inc=ld, inck=16)
    p.add("sync", lambda e: e.dma_start(out=dgn[:], in_=dr["dgn_s"][:, :]), inc=ld, inck=16)
    p.add("sync", lambda e: e.dma_start(out=k1[:], in_=dr["k1_s"][:, :]), inc=ld, inck=16)
    p.add("sync", lambda e: e.dma_start(out=g0[:], in_=dr["gate_s"][0:1, :]), inc=ld, inck=16)
    p.add("sync", lambda e: e.dma_start(out=g1[:], in_=dr["gate_s"][1:2, :]), inc=ld, inck=16)
    for dc in range(16):
        for eb in range(4):
            src = dr["wobf"][eb * 512:(eb + 1) * 512, dc * 128:(dc + 1) * 128]
            dst = wo[:, dc * 2048 + eb * 512: dc * 2048 + (eb + 1) * 512]
            p.add("sync", lambda e, dst=dst, src=src: e.dma_start_transpose(dst, src),
                  inc=ld, inck=16)
    const_ld = ld.n
    p.add("tensor", lambda e: e.matmul(P[4][:, 0:8], onesrf[:], g0[:], start=True, stop=True),
          waits=[(ld, const_ld)], inc=t)
    p.add("tensor", lambda e: e.matmul(P[5][:, 0:8], onesrf[:], g1[:], start=True, stop=True),
          inc=t)
    p.add("vector", lambda e: e.tensor_copy(slope_b[:], P[4][:, 0:8]), waits=[(t, t.n)])
    p.add("vector", lambda e: e.tensor_copy(slc_b[:], P[5][:, 0:8]))
    gates_v = v.n

    vh_t_guard = 0
    prev_store = {}
    for step in range(B_LOC * NST):
        b, st = divmod(step, NST)
        w_l = [(ld, const_ld)]
        if step > 0:
            w_l.append((t, vh_t_guard))
        for ec in range(16):
            src = dr["vhT"][b, st, ec * 128:(ec + 1) * 128, :]
            p.add("sync", lambda e, ec=ec, src=src: e.dma_start(
                out=vh[:, ec * 512:(ec + 1) * 512], in_=src),
                waits=w_l if ec == 0 else (), inc=ld, inck=16)
        vh_ld = ld.n
        for tt in range(4):
            row0 = st * 512 + tt * 128
            w_x = [(v, v.n)] if step + tt > 0 else []
            p.add("sync", lambda e, b=b, row0=row0, tt=tt: e.dma_start(
                out=xf[:, (tt % 2) * 2048:(tt % 2 + 1) * 2048],
                in_=dr["x"][b, row0:row0 + 128, :]), waits=w_x, inc=ld, inck=16)
            x_ld = ld.n
            for dc in range(4):
                for ec in range(16):
                    lhsT = vh[:, ec * 512 + tt * 128: ec * 512 + tt * 128 + 128]
                    rhs = wo[:, ec * 2048 + dc * 512: ec * 2048 + (dc + 1) * 512]
                    p.add("tensor", lambda e, dc=dc, lhsT=lhsT, rhs=rhs, ec=ec: e.matmul(
                        P[dc][:], lhsT, rhs, start=(ec == 0), stop=(ec == 15)),
                        waits=[(ld, vh_ld), (v, v.n)] if ec == 0 else (),
                        inc=t if ec == 15 else None)
            wo_done = t.n
            vh_t_guard = t.n
            for dc in range(4):
                p.add("vector", lambda e, dc=dc, tt=tt: e.tensor_tensor(
                    pre[:, dc * 512:(dc + 1) * 512],
                    xf[:, (tt % 2) * 2048 + dc * 512:(tt % 2) * 2048 + (dc + 1) * 512],
                    P[dc][:], ALU.add), waits=[(t, wo_done), (ld, x_ld)])
            _ln_stats(ctx, pre[:], 2048, sm, EPS_LN)
            p.add("vector", lambda e: e.reciprocal(sm["sumx2"][:, 0:1], sm["vpe"][:, 0:1]))
            p.add("vector", lambda e: e.tensor_tensor(
                sm["sumx2"][:, 0:1], sm["ssq"][:, 0:1], sm["sumx2"][:, 0:1], ALU.mult))
            p.add("vector", lambda e: e.tensor_scalar(
                h1f[:], pre[:], sm["mean"][:, 0:1], sm["rstd"][:, 0:1],
                ALU.subtract, ALU.mult))
            h1_done = v.n
            for dc in range(16):
                bank = P[4 + dc % 2]
                p.add("tensor", lambda e, dc=dc, bank=bank: e.transpose(
                    bank[:, 0:128], h1f[:, dc * 128:(dc + 1) * 128], identf[:]),
                    waits=[(v, v.n)], inc=t)
                p.add("vector", lambda e, bank=bank: e.tensor_copy(tpf[:], bank[:, 0:128]),
                      waits=[(t, t.n)])
                p.add("vector", lambda e, dc=dc: e.tensor_copy(
                    xhi[:, dc * 128:(dc + 1) * 128], tpf[:]))
                p.add("vector", lambda e, dc=dc: e.tensor_copy(
                    tpf2[:], xhi[:, dc * 128:(dc + 1) * 128]))
                p.add("vector", lambda e, dc=dc: e.tensor_tensor(
                    xlo[:, dc * 128:(dc + 1) * 128], tpf[:], tpf2[:], ALU.subtract))
            xsplit_done = v.n
            nsel = 0
            for pas, (xsrc, c0, c1) in enumerate(
                    ((xhi, 0, 136), (xhi, 136, 208), (xlo, 0, 72))):
                for dc in range(16):
                    lhsT = xsrc[:, dc * 128:(dc + 1) * 128]
                    rhs = ffnw[:, dc * 208 + c0: dc * 208 + c1]
                    odst = P[6][:, 0:c1 - c0]
                    p.add("tensor", lambda e, lhsT=lhsT, rhs=rhs, odst=odst,
                          first=(nsel == 0): e.matmul(
                        odst, lhsT, rhs, start=first, stop=False),
                        waits=[(v, xsplit_done)] if nsel == 0 else ())
                    nsel += 1
            p.add("tensor", lambda e: e.matmul(
                P[6][:, 0:136], onesr[:], k1[:], start=False, stop=True), inc=t)
            sel_done = t.n
            p.add("vector", lambda e: e.tensor_copy(sel[:], P[6][:, 0:136]),
                  waits=[(t, sel_done)])
            p.add("vector", lambda e: e.tensor_scalar(
                sm["r2"][:], sel[:, 64:72], sm["sumx2"][:, 0:1], None, ALU.add))
            p.add("vector", lambda e: e.reciprocal(sm["rr"][:], sm["r2"][:]))
            p.add("vector", lambda e: e.tensor_tensor(d2[:], sel[:, 0:64], sel[:, 0:64], ALU.mult))
            p.add("vector", lambda e: e.tensor_reduce(
                sm["s1"][:].unsqueeze(2), d2[:].rearrange("p (n s_) -> p n s_", s_=8),
                AX.X, ALU.add))
            p.add("vector", lambda e: e.tensor_tensor(sm["score"][:], sm["s1"][:], sm["rr"][:], ALU.mult))
            p.add("vector", lambda e: e.tensor_tensor(sm["z"][:], sm["score"][:], slope_b[:], ALU.mult),
                  waits=[(v, gates_v)])
            p.add("vector", lambda e: e.tensor_tensor(sm["z"][:], sm["z"][:], slc_b[:], ALU.subtract))
            p.add("scalar", lambda e: e.activation(sm["expz"][:], sm["z"][:], AF.Exp,
                                                   bias=0.0, scale=-1.0), waits=[(v, v.n)])
            p.add("vector", lambda e: e.tensor_scalar_add(sm["gate"][:], sm["expz"][:], 1.0),
                  waits=[(s, s.n)])
            p.add("vector", lambda e: e.reciprocal(sm["gate"][:], sm["gate"][:]))
            p.add("vector", lambda e: e.reduce_max(sm["gmax"][:, 0:1], sm["gate"][:], AX.X))
            p.add("vector", lambda e: e.tensor_single_scalar(
                sm["ge"][:], sm["gate"][:], sm["gmax"][:, 0:1], ALU.is_ge))
            p.add("vector", lambda e: e.tensor_tensor_scan(
                sm["pfx"][:], onesf[:], sm["ge"][:], 0.0, ALU.mult, ALU.add))
            p.add("vector", lambda e: e.tensor_single_scalar(
                sm["eq1"][:], sm["pfx"][:], 1.0, ALU.is_equal))
            p.add("vector", lambda e: e.tensor_tensor(sm["oh"][:], sm["ge"][:], sm["eq1"][:], ALU.mult))
            p.add("vector", lambda e: e.tensor_scalar(
                sm["s2m"][:, 0:1], sm["gmax"][:, 0:1], -1.0, 2.0, ALU.mult, ALU.add))
            p.add("vector", lambda e: e.tensor_scalar(
                sm["macc"][:], sel[:, 72:80], sm["oh"][:, 0:1], None, ALU.mult))
            for n in range(1, 8):
                p.add("vector", lambda e, n=n: e.scalar_tensor_tensor(
                    sm["macc"][:], sel[:, 72 + 8 * n: 80 + 8 * n], sm["oh"][:, n:n + 1],
                    sm["macc"][:], ALU.mult, ALU.add))
            for n in range(8):
                p.add("vector", lambda e, n=n: e.tensor_scalar(
                    hext[:, 8 * n:8 * n + 8], sm["macc"][:], sm["oh"][:, n:n + 1],
                    None, ALU.mult))
            p.add("vector", lambda e: e.tensor_copy(hext[:, 64:72], sm["oh"][:]))
            hext_done = v.n
            p.add("tensor", lambda e: e.transpose(ps_tb[0:72, 0:128], hext[:], identb[:]),
                  waits=[(v, hext_done)], inc=t)
            p.add("vector", lambda e: e.tensor_copy(hexT[:], ps_tb[0:72, 0:128]),
                  waits=[(t, t.n)])
            hexT_done = v.n
            for dc in range(4):
                p.add("tensor", lambda e, dc=dc: e.matmul(
                    P[dc][:], hexT[:], dgn[:, dc * 512:(dc + 1) * 512],
                    start=True, stop=True),
                    waits=[(v, hexT_done)], inc=t)
            mod_done = t.n
            for dc in range(4):
                p.add("vector", lambda e, dc=dc: e.tensor_scalar(
                    tmp512[:], h1f[:, dc * 512:(dc + 1) * 512], sm["s2m"][:, 0:1],
                    None, ALU.mult), waits=[(t, mod_done)] if dc == 0 else ())
                p.add("vector", lambda e, dc=dc: e.scalar_tensor_tensor(
                    pre[:, dc * 512:(dc + 1) * 512], P[dc][:], sm["gmax"][:, 0:1],
                    tmp512[:], ALU.mult, ALU.add))
            _ln_stats(ctx, pre[:], 2048, sm, EPS_LN)
            p.add("vector", lambda e: e.tensor_scalar(
                h2[:], pre[:], sm["mean"][:, 0:1], sm["rstd"][:, 0:1],
                ALU.subtract, ALU.mult))
            p.add("vector", lambda e: e.tensor_copy(h2b[:], h2[:]))
            h2_done = v.n
            w_st = [(v, h2_done)]
            p.add("gpsimd", lambda e, b=b, row0=row0: e.dma_start(
                out=dr["h2f"][b, row0:row0 + 128, :], in_=h2[:]), waits=w_st, inc=g, inck=16)
            p.add("gpsimd", lambda e, b=b, row0=row0: e.dma_start(
                out=dr["h2bf"][b, row0:row0 + 128, :], in_=h2b[:]), waits=[(v, h2_done)],
                inc=g, inck=16)
            p.add("vector", lambda e, gn=g.n: e.nop(), waits=[(g, g.n)])
    p.add("gpsimd", lambda e, gn=g.n: e.wait_ge(g.h, gn))



def ph_b1(ctx):
    nc, p, dr = ctx.nc, ctx.p, ctx.dram
    ld, t, v, s, g = (ctx.sem(n) for n in ("ld", "t", "v", "s", "g"))
    wk = ctx.sb("b1_wk", (128, 16 * 3072), BF16)
    h2T = ctx.sb("b1_h2T", (128, 16 * 512), BF16)
    ev = ctx.sb("b1_ev", (128, 4 * 512), BF16)
    P = [ctx.ps(f"b1_ps{i}", (128, 512)) for i in range(4)]

    ev_guard = {}
    ev_g = {}
    for half in range(2):
        e0 = half * 3072
        w_l = [(t, t.n)]
        for dc in range(16):
            for eb in range(6):
                src = dr["wqkvbf"][e0 + eb * 512: e0 + (eb + 1) * 512,
                                   dc * 128:(dc + 1) * 128]
                dst = wk[:, dc * 3072 + eb * 512: dc * 3072 + (eb + 1) * 512]
                p.add("sync", lambda e, dst=dst, src=src: e.dma_start_transpose(dst, src),
                      waits=w_l if (dc == 0 and eb == 0) else (), inc=ld, inck=16)
        wk_ld = ld.n
        for step in range(B_LOC * NST):
            b, st = divmod(step, NST)
            w_h = [(ld, wk_ld)]
            if (half, step) != (0, 0):
                w_h.append((t, t.n))
            for dc in range(16):
                src = dr["h2bf"][b, st * 512:(st + 1) * 512, dc * 128:(dc + 1) * 128]
                p.add("sync", lambda e, dc=dc, src=src: e.dma_start_transpose(
                    h2T[:, dc * 512:(dc + 1) * 512], src),
                    waits=w_h if dc == 0 else (), inc=ld, inck=16)
            h2_ld = ld.n
            for tt in range(4):
                row0 = st * 512 + tt * 128
                for eb in range(6):
                    bank = P[eb % 4]
                    w_mm = [(ld, h2_ld)]
                    if (eb % 4) in ev_guard:
                        w_mm.append((s, ev_guard[eb % 4]))
                    for dc in range(16):
                        lhsT = h2T[:, dc * 512 + tt * 128: dc * 512 + tt * 128 + 128]
                        rhs = wk[:, dc * 3072 + eb * 512: dc * 3072 + (eb + 1) * 512]
                        p.add("tensor", lambda e, bank=bank, lhsT=lhsT, rhs=rhs, dc=dc: e.matmul(
                            bank[:], lhsT, rhs, start=(dc == 0), stop=(dc == 15)),
                            waits=w_mm if dc == 0 else (), inc=t if dc == 15 else None)
                    mm_done = t.n
                    w_ev = [(t, mm_done)]
                    p.add("scalar", lambda e, bank=bank, eb=eb: e.copy(
                        ev[:, (eb % 4) * 512:(eb % 4 + 1) * 512], bank[:]),
                        waits=w_ev + ([(g, ev_g[eb % 4])] if (eb % 4) in ev_g else []))
                    ev_guard[eb % 4] = s.n
                    dst = dr["qkvbf"][b, row0:row0 + 128,
                                      e0 + eb * 512: e0 + (eb + 1) * 512]
                    p.add("gpsimd", lambda e, dst=dst, eb=eb: e.dma_start(
                        out=dst, in_=ev[:, (eb % 4) * 512:(eb % 4 + 1) * 512]),
                        waits=[(s, s.n)], inc=g, inck=16)
                    ev_g[eb % 4] = g.n
    p.add("gpsimd", lambda e, gn=g.n: e.wait_ge(g.h, gn))



def ph_b2(ctx):
    nc, p, dr = ctx.nc, ctx.p, ctx.dram
    ld, t, v, s, g = (ctx.sem(n) for n in ("ld", "t", "v", "s", "g"))
    SC = 1.0 / np.sqrt(128.0)

    wo = ctx.sb("b2_wo", (128, 16 * 2048), BF16)
    atn = ctx.sb("b2_atn", (128, 16 * 2048), BF16)
    qT = ctx.sb("b2_qT", (128, 2 * 2048), BF16)
    kT = ctx.sb("b2_kT", (128, 2 * 2048), BF16)
    vv = ctx.sb("b2_vv", (128, 2 * 2048), BF16)
    pt = ctx.sb("b2_pt", (128, 1024), BF16)
    rsum = ctx.sb("b2_rs", (1, 512), F32)
    rsumb = ctx.sb("b2_rsb", (1, 512), BF16)
    rbc = ctx.sb("b2_rbc", (128, 512), F32)
    onesr = ctx.sb("b2_onesr", (1, 128), BF16)
    onesc = ctx.sb("b2_onesc", (128, 1), BF16)
    h2t = ctx.sb("b2_h2t", (128, 2048), F32)
    pre = ctx.sb("b2_pre", (128, 2048), F32)
    dump = ctx.sb("b2_dump", (128, 2048), F32)
    h3 = ctx.sb("b2_h3", (128, 2048), F32)
    h3b = ctx.sb("b2_h3b", (128, 2048), BF16)
    sm = {k: ctx.sb("b2s_" + k, (128, 1), F32) for k in
          ("sumb", "negm", "ssq", "vpe", "rstd", "mean")}
    sm["dump"] = dump

    P = [ctx.ps(f"b2_ps{i}", (128, 512)) for i in range(8)]

    p.add("sync", lambda e: e.dma_start(out=onesr[:], in_=dr["c_onesr"][:, :]), inc=ld, inck=16)
    p.add("sync", lambda e: e.dma_start(out=onesc[:], in_=dr["c_ones"][:, 0:1],
                                        allow_slow_non_contiguous=True), inc=ld, inck=16)
    for dc in range(16):
        for eb in range(4):
            src = dr["wo1bf"][eb * 512:(eb + 1) * 512, dc * 128:(dc + 1) * 128]
            dst = wo[:, dc * 2048 + eb * 512: dc * 2048 + (eb + 1) * 512]
            p.add("sync", lambda e, dst=dst, src=src: e.dma_start_transpose(dst, src),
                  inc=ld, inck=16)
    const_ld = ld.n

    qkv_flat = dr["qkvbf"].rearrange("b s e -> b (s e)").rearrange(
        "b (th hh l i) -> b th hh l i", th=3, hh=16, l=2048)

    for b in range(B_LOC):
        atn_t_guard = t.n
        for h in range(16):
            bi = h % 2
            w_l = [(ld, const_ld)]
            if (b, h) != (0, 0):
                w_l.append((t, t.n))
            p.add("sync", lambda e, b=b, h=h, bi=bi: e.dma_start_transpose(
                qT[:, bi * 2048:(bi + 1) * 2048], qkv_flat[b, 0, h]),
                waits=w_l, inc=ld, inck=16)
            p.add("sync", lambda e, b=b, h=h, bi=bi: e.dma_start_transpose(
                kT[:, bi * 2048:(bi + 1) * 2048], qkv_flat[b, 1, h]),
                inc=ld, inck=16)
            p.add("sync", lambda e, b=b, h=h, bi=bi: e.dma_start(
                out=vv[:, bi * 2048:(bi + 1) * 2048].rearrange("p (kc i) -> p kc i", i=128),
                in_=qkv_flat[b, 2, h].rearrange("(kc pp) i -> pp kc i", pp=128)),
                inc=ld, inck=16)
            qkv_ld = ld.n
            p.add("vector", lambda e, bi=bi: e.tensor_scalar_mul(
                qT[:, bi * 2048:(bi + 1) * 2048], qT[:, bi * 2048:(bi + 1) * 2048], SC),
                waits=[(ld, qkv_ld)])
            q_ready = v.n
            pt_guard = {}
            for qg in range(4):
                for kc in range(16):
                    sb_bank = P[kc % 2]
                    p.add("tensor", lambda e, kc=kc, qg=qg, bi=bi, sb_bank=sb_bank: e.matmul(
                        sb_bank[:], kT[:, bi * 2048 + kc * 128: bi * 2048 + kc * 128 + 128],
                        qT[:, bi * 2048 + qg * 512: bi * 2048 + (qg + 1) * 512],
                        start=True, stop=True),
                        waits=[(v, q_ready), (s, s.n)], inc=t)
                    sc_done = t.n
                    w_exp = [(t, sc_done)]
                    if (kc % 2) in pt_guard:
                        w_exp.append((t, pt_guard[kc % 2]))
                    p.add("scalar", lambda e, kc=kc, sb_bank=sb_bank: e.activation(
                        pt[:, (kc % 2) * 512:(kc % 2 + 1) * 512], sb_bank[:], AF.Exp),
                        waits=w_exp)
                    exp_done = s.n
                    w_pv = [(s, exp_done)]
                    if kc == 0:
                        w_pv.append((v, v.n))
                    p.add("tensor", lambda e, kc=kc, bi=bi: e.matmul(
                        P[2][:], vv[:, bi * 2048 + kc * 128: bi * 2048 + kc * 128 + 128],
                        pt[:, (kc % 2) * 512:(kc % 2 + 1) * 512],
                        start=(kc == 0), stop=(kc == 15)),
                        waits=w_pv, inc=t)
                    p.add("tensor", lambda e, kc=kc: e.matmul(
                        P[3][0:1, :], onesc[:], pt[:, (kc % 2) * 512:(kc % 2 + 1) * 512],
                        start=(kc == 0), stop=(kc == 15)),
                        inc=t)
                    pt_guard[kc % 2] = t.n
                pv_done = t.n
                p.add("vector", lambda e: e.reciprocal(rsum[:], P[3][0:1, :]),
                      waits=[(t, pv_done)])
                p.add("vector", lambda e: e.tensor_copy(rsumb[:], rsum[:]))
                rs_done = v.n
                p.add("tensor", lambda e: e.matmul(
                    P[4][:], onesr[:], rsumb[:], start=True, stop=True),
                    waits=[(v, rs_done)], inc=t)
                bc_done = t.n
                p.add("vector", lambda e: e.tensor_copy(rbc[:], P[4][:]),
                      waits=[(t, bc_done)])
                p.add("vector", lambda e, h=h, qg=qg: e.tensor_tensor(
                    atn[:, h * 2048 + qg * 512: h * 2048 + (qg + 1) * 512],
                    P[2][:], rbc[:], ALU.mult))
        for tt in range(16):
            row0 = tt * 128
            p.add("sync", lambda e, b=b, row0=row0: e.dma_start(
                out=h2t[:], in_=dr["h2f"][b, row0:row0 + 128, :]),
                waits=[(v, v.n)], inc=ld, inck=16)
            h2_ld = ld.n
            for dc in range(4):
                for ec in range(16):
                    lhsT = atn[:, ec * 2048 + tt * 128: ec * 2048 + tt * 128 + 128]
                    rhs = wo[:, ec * 2048 + dc * 512: ec * 2048 + (dc + 1) * 512]
                    p.add("tensor", lambda e, dc=dc, lhsT=lhsT, rhs=rhs, ec=ec: e.matmul(
                        P[4 + dc][:], lhsT, rhs, start=(ec == 0), stop=(ec == 15)),
                        waits=[(v, v.n), (s, s.n)] if ec == 0 else (),
                        inc=t if ec == 15 else None)
            wo_done = t.n
            for dc in range(4):
                p.add("vector", lambda e, dc=dc: e.tensor_tensor(
                    pre[:, dc * 512:(dc + 1) * 512], h2t[:, dc * 512:(dc + 1) * 512],
                    P[4 + dc][:], ALU.add), waits=[(t, wo_done), (ld, h2_ld)])
            _ln_stats(ctx, pre[:], 2048, sm, EPS_LN)
            p.add("vector", lambda e: e.tensor_scalar(
                h3[:], pre[:], sm["mean"][:, 0:1], sm["rstd"][:, 0:1],
                ALU.subtract, ALU.mult))
            p.add("vector", lambda e: e.tensor_copy(h3b[:], h3[:]))
            h3_done = v.n
            p.add("gpsimd", lambda e, b=b, row0=row0: e.dma_start(
                out=dr["h3f"][b, row0:row0 + 128, :], in_=h3[:]),
                waits=[(v, h3_done)], inc=g, inck=16)
            p.add("gpsimd", lambda e, b=b, row0=row0: e.dma_start(
                out=dr["h3bf"][b, row0:row0 + 128, :], in_=h3b[:]),
                waits=[(v, h3_done)], inc=g, inck=16)
            p.add("vector", lambda e: e.nop(), waits=[(g, g.n)])
    p.add("gpsimd", lambda e, gn=g.n: e.wait_ge(g.h, gn))



def ph_c2a(ctx):
    nc, p, dr = ctx.nc, ctx.p, ctx.dram
    ld, t, v, s, g = (ctx.sem(n) for n in ("ld", "t", "v", "s", "g"))
    h3T = ctx.sb("c2a_h3T", (128, 16 * 4096), BF16)
    w1T = ctx.sb("c2a_w1T", (128, 2 * 16 * 512), BF16)
    hrb = ctx.sb("c2a_hr", (128, 4 * 512), BF16)
    hsq = ctx.sb("c2a_hsq", (128, 512), F32)
    relu_t = ctx.sb("c2a_relu", (128, 512), BF16)
    sqf = ctx.sb("c2a_sqf", (128, 512), F32)
    acc = ctx.sb("c2a_acc", (128, 8 * 512), F32)
    ssr = ctx.sb("c2a_ssr", (1, 512), F32)
    rmsc = ctx.sb("c2a_rmsc", (128, 32), F32)
    identf = ctx.sb("c2a_idf", (128, 128), F32)
    onesc = ctx.sb("c2a_onesc", (128, 1), F32)
    P = [ctx.ps(f"c2a_ps{i}", (128, 512)) for i in range(6)]

    p.add("sync", lambda e: e.dma_start(out=identf[:], in_=dr["c_identf"][:, :]), inc=ld, inck=16)
    p.add("sync", lambda e: e.dma_start(out=onesc[:], in_=dr["c_onesf"][:, 0:1],
                                        allow_slow_non_contiguous=True), inc=ld, inck=16)
    for dc in range(16):
        for step in range(8):
            b, st = divmod(step, NST)
            src = dr["h3bf"][b, st * 512:(st + 1) * 512, dc * 128:(dc + 1) * 128]
            dst = h3T[:, dc * 4096 + step * 512: dc * 4096 + (step + 1) * 512]
            p.add("sync", lambda e, dst=dst, src=src: e.dma_start_transpose(dst, src),
                  inc=ld, inck=16)
    h3_ld = ld.n
    p.add("vector", lambda e: e.memset(acc[:], 0.0))
    acc_z = v.n

    hr_g = {}
    hr_st = {}
    for hb in range(16):
        wsl = w1T[:, (hb % 2) * 8192:(hb % 2 + 1) * 8192]
        w_l = [(t, t.n)] if hb >= 2 else []
        for dc in range(16):
            src = dr["w1bf"][hb * 512:(hb + 1) * 512, dc * 128:(dc + 1) * 128]
            p.add("sync", lambda e, wsl=wsl, dc=dc, src=src: e.dma_start_transpose(
                wsl[:, dc * 512:(dc + 1) * 512], src),
                waits=w_l if dc == 0 else (), inc=ld, inck=16)
        w_ld = ld.n
        for hc4 in range(4):
            hcg = hb * 4 + hc4
            for step in range(8):
                bank = P[step % 2]
                w_mm = [(ld, max(w_ld, h3_ld))]
                if (step % 2) in hr_g:
                    w_mm.append((s, hr_g[step % 2]))
                for dc in range(16):
                    lhsT = wsl[:, dc * 512 + hc4 * 128: dc * 512 + hc4 * 128 + 128]
                    rhs = h3T[:, dc * 4096 + step * 512: dc * 4096 + (step + 1) * 512]
                    p.add("tensor", lambda e, bank=bank, lhsT=lhsT, rhs=rhs, dc=dc: e.matmul(
                        bank[:], lhsT, rhs, start=(dc == 0), stop=(dc == 15)),
                        waits=w_mm if dc == 0 else (), inc=t if dc == 15 else None)
                mm_done = t.n
                bi4 = (hcg * 8 + step) % 4
                p.add("scalar", lambda e, bank=bank: e.activation(
                    hsq[:].bitcast(F32) if False else relu_t[:], bank[:], AF.Relu),
                    waits=[(t, mm_done)])
                hr_g[step % 2] = s.n
                relu_done = s.n
                w_sq = [(s, relu_done), (v, acc_z)]
                p.add("vector", lambda e: e.tensor_tensor(
                    sqf[:], relu_t[:], relu_t[:], ALU.mult), waits=w_sq)
                w_hr = [(g, hr_st[bi4])] if bi4 in hr_st else []
                p.add("vector", lambda e, bi4=bi4: e.tensor_copy(
                    hrb[:, bi4 * 512:(bi4 + 1) * 512], sqf[:]), waits=w_hr)
                sq_done = v.n
                dst = dr["hrs"][hcg, :, step * 512:(step + 1) * 512]
                p.add("gpsimd", lambda e, dst=dst, bi4=bi4: e.dma_start(
                    out=dst, in_=hrb[:, bi4 * 512:(bi4 + 1) * 512]),
                    waits=[(v, sq_done)], inc=g, inck=16)
                hr_st[bi4] = g.n
                p.add("vector", lambda e: e.tensor_tensor(
                    hsq[:], sqf[:], sqf[:], ALU.mult))
                p.add("vector", lambda e, step=step: e.tensor_tensor(
                    acc[:, step * 512:(step + 1) * 512],
                    acc[:, step * 512:(step + 1) * 512], hsq[:], ALU.add))
    acc_done = v.n
    for step in range(8):
        p.add("tensor", lambda e, step=step: e.matmul(
            P[2][0:1, :], onesc[:],
            acc[:, step * 512:(step + 1) * 512], start=True, stop=True),
            waits=[(v, max(acc_done, v.n))], inc=t)
        cs_done = t.n
        p.add("vector", lambda e: e.tensor_copy(ssr[:], P[2][0:1, :]),
              waits=[(t, cs_done)])
        ssr_done = v.n
        for q4 in range(4):
            p.add("tensor", lambda e, q4=q4: e.transpose(
                P[3][:, 0:1], ssr[0:1, q4 * 128:(q4 + 1) * 128], identf[0:1, 0:1]),
                waits=[(v, v.n)], inc=t)
            p.add("vector", lambda e, step=step, q4=q4: e.tensor_copy(
                rmsc[:, step * 4 + q4: step * 4 + q4 + 1], P[3][:, 0:1]),
                waits=[(t, t.n)])
    p.add("vector", lambda e: e.tensor_scalar(
        rmsc[:], rmsc[:], 1.0 / 8192.0, EPS_RMS, ALU.mult, ALU.add))
    p.add("scalar", lambda e: e.activation(rmsc[:], rmsc[:], AF.Ln), waits=[(v, v.n)])
    p.add("vector", lambda e: e.tensor_scalar_mul(rmsc[:], rmsc[:], -0.5), waits=[(s, s.n)])
    p.add("scalar", lambda e: e.activation(rmsc[:], rmsc[:], AF.Exp), waits=[(v, v.n)])
    p.add("gpsimd", lambda e: e.dma_start(out=dr["rms_s"][:, :], in_=rmsc[:]),
          waits=[(s, s.n)], inc=g, inck=16)
    p.add("gpsimd", lambda e, gn=g.n: e.wait_ge(g.h, gn))



def ph_c2b(ctx):
    nc, p, dr = ctx.nc, ctx.p, ctx.dram
    ld, t, v, s, g = (ctx.sem(n) for n in ("ld", "t", "v", "s", "g"))
    w2T = ctx.sb("c2b_w2T", (128, 128 * 512), BF16)
    hrt = ctx.sb("c2b_hrt", (128, 2 * 8192), BF16)
    rmsc = ctx.sb("c2b_rmsc", (128, 32), F32)
    yb = ctx.sb("c2b_y", (128, 4 * 512), BF16)
    P = [ctx.ps(f"c2b_ps{i}", (128, 512)) for i in range(4)]

    p.add("sync", lambda e: e.dma_start(out=rmsc[:], in_=dr["rms_s"][:, :]), inc=ld, inck=16)
    y_st = {}
    for half in range(2):
        d0 = half * 1024
        w_l = [(t, t.n)] if half > 0 else []
        for hc in range(64):
            for dc2 in range(2):
                src = dr["w2bf"][d0 + dc2 * 512: d0 + (dc2 + 1) * 512,
                                 hc * 128:(hc + 1) * 128]
                dst = w2T[:, (hc * 2 + dc2) * 512:(hc * 2 + dc2 + 1) * 512]
                p.add("sync", lambda e, dst=dst, src=src: e.dma_start_transpose(dst, src),
                      waits=w_l if (hc == 0 and dc2 == 0) else (), inc=ld, inck=16)
        w_ld = ld.n
        for tt in range(32):
            hb = hrt[:, (tt % 2) * 8192:(tt % 2 + 1) * 8192]
            w_h = [(ld, w_ld)]
            if tt >= 2:
                w_h.append((t, t.n))
            for hq in range(4):
                src = dr["hrs"][hq * 16:(hq + 1) * 16, :, tt * 128:(tt + 1) * 128]
                dst = hb[:, hq * 2048:(hq + 1) * 2048]
                p.add("sync", lambda e, dst=dst, src=src: e.dma_start(
                    out=dst.rearrange("p (hc i) -> p hc i", i=128),
                    in_=src.transpose([1, 0, 2])),
                    waits=w_h if hq == 0 else (), inc=ld, inck=16)
            hr_ld = ld.n
            for dc2 in range(2):
                bank = P[dc2 % 2]
                w_mm = [(ld, hr_ld)]
                if (dc2 % 2) in y_st:
                    w_mm.append((v, y_st[dc2 % 2][0]))
                for hc in range(64):
                    lhsT = hb[:, hc * 128:(hc + 1) * 128]
                    rhs = w2T[:, (hc * 2 + dc2) * 512:(hc * 2 + dc2 + 1) * 512]
                    p.add("tensor", lambda e, bank=bank, lhsT=lhsT, rhs=rhs, hc=hc: e.matmul(
                        bank[:], lhsT, rhs, start=(hc == 0), stop=(hc == 63)),
                        waits=w_mm if hc == 0 else (), inc=t if hc == 63 else None)
                mm_done = t.n
                w_e = [(t, mm_done)]
                if (dc2 % 2) in y_st:
                    w_e.append((g, y_st[dc2 % 2][1]))
                p.add("vector", lambda e, bank=bank, dc2=dc2, tt=tt: e.tensor_scalar(
                    yb[:, dc2 * 512:(dc2 + 1) * 512], bank[:],
                    rmsc[:, tt:tt + 1], None, ALU.mult), waits=w_e)
                ev_done = v.n
                dst = dr["ysb"][tt * 128:(tt + 1) * 128, d0 + dc2 * 512: d0 + (dc2 + 1) * 512]
                p.add("gpsimd", lambda e, dst=dst, dc2=dc2: e.dma_start(
                    out=dst, in_=yb[:, dc2 * 512:(dc2 + 1) * 512]),
                    waits=[(v, ev_done)], inc=g, inck=16)
                y_st[dc2 % 2] = (ev_done, g.n)
    p.add("gpsimd", lambda e, gn=g.n: e.wait_ge(g.h, gn))



def ph_c3(ctx):
    nc, p, dr = ctx.nc, ctx.p, ctx.dram
    ld, t, v, s, g = (ctx.sem(n) for n in ("ld", "t", "v", "s", "g"))
    yf = ctx.sb("c3_y", (128, 2 * 2048), BF16)
    h3t = ctx.sb("c3_h3", (128, 2 * 2048), F32)
    pre = ctx.sb("c3_pre", (128, 2048), F32)
    dump = ctx.sb("c3_dump", (128, 2048), F32)
    outt = ctx.sb("c3_out", (128, 2048), F32)
    sm = {k: ctx.sb("c3s_" + k, (128, 1), F32) for k in
          ("sumb", "negm", "ssq", "vpe", "rstd", "mean")}
    sm["dump"] = dump

    out_st = 0
    for tt in range(32):
        b, row0 = divmod(tt * 128, S)
        bi = tt % 2
        w_l = [(v, v.n)] if tt >= 2 else []
        p.add("sync", lambda e, tt=tt, bi=bi: e.dma_start(
            out=yf[:, bi * 2048:(bi + 1) * 2048],
            in_=dr["ysb"][tt * 128:(tt + 1) * 128, :]), waits=w_l, inc=ld, inck=16)
        p.add("sync", lambda e, b=b, row0=row0, bi=bi: e.dma_start(
            out=h3t[:, bi * 2048:(bi + 1) * 2048],
            in_=dr["h3f"][b, row0:row0 + 128, :]), inc=ld, inck=16)
        t_ld = ld.n
        p.add("vector", lambda e, bi=bi: e.tensor_copy(
            pre[:], yf[:, bi * 2048:(bi + 1) * 2048]), waits=[(ld, t_ld)])
        p.add("vector", lambda e, bi=bi: e.tensor_tensor(
            pre[:], pre[:], h3t[:, bi * 2048:(bi + 1) * 2048], ALU.add))
        _ln_stats(ctx, pre[:], 2048, sm, EPS_LN)
        p.add("vector", lambda e: e.tensor_scalar(
            outt[:], pre[:], sm["mean"][:, 0:1], sm["rstd"][:, 0:1],
            ALU.subtract, ALU.mult))
        o_done = v.n
        p.add("gpsimd", lambda e, b=b, row0=row0: e.dma_start(
            out=dr["out"][b, row0:row0 + 128, :], in_=outt[:]),
            waits=[(v, o_done)], inc=g, inck=16)
        p.add("vector", lambda e: e.nop(), waits=[(g, g.n)])
    p.add("gpsimd", lambda e, gn=g.n: e.wait_ge(g.h, gn))


def build(group_phases, for_run=True):
    nc = bass.Bass("TRN2", target_bir_lowering=False)
    es = ExitStack()
    ctx = Ctx(nc, es, group_phases)
    prog = ctx.p
    with es:
        in_names, out_names = declare_dram(ctx, group_phases)
        block = es.enter_context(nc.Block())
        first = True
        for ph in group_phases:
            if not first:
                prog.barrier(ctx.sem("bar"))
            first = False
            globals()["ph_" + ph](ctx)
        prog.emit(block)
    return nc, in_names, out_names



N_CORES = 8
GROUPS = [["prep"], ["a1"], ["a2"], ["b1"], ["b2"], ["c2a"], ["c2b"], ["c3"]]


def kernel(**inputs):
    consts = host_consts()
    xfull = np.ascontiguousarray(inputs["L0_attn_x"])
    per_core = []
    for c in range(N_CORES):
        d = {"x": np.ascontiguousarray(xfull[2 * c:2 * c + 2])}
        for k, v in inputs.items():
            if k != "L0_attn_x":
                d[k] = np.ascontiguousarray(v)
        d.update(consts)
        per_core.append(d)
    outs = None
    for grp in GROUPS:
        nc, in_names, out_names = build(grp)
        in_maps = [{n: per_core[c][n] for n in in_names} for c in range(N_CORES)]
        res = run_bass_kernel_spmd(nc, in_maps, core_ids=list(range(N_CORES)))
        for c in range(N_CORES):
            for n in out_names:
                per_core[c][n] = res.results[c][n]
        if "out" in out_names:
            outs = [res.results[c]["out"] for c in range(N_CORES)]
    return np.concatenate(outs, axis=0)


# version 02
# speedup vs baseline: 1.0102x; 1.0102x over previous
import numpy as np
import ml_dtypes
import concourse.bass as bass
import concourse.mybir as mybir
from concourse.bass_utils import run_bass_kernel_spmd
from contextlib import ExitStack

F32 = mybir.dt.float32
BF16 = mybir.dt.bfloat16
AF = mybir.ActivationFunctionType
ALU = mybir.AluOpType
AX = mybir.AxisListType

B_LOC, S, D, H, HD = 2, 2048, 2048, 16, 128
TOK = B_LOC * S
ST = 512
NST = S // ST
NDC = D // 128
EPS_LN = 1e-5
EPS_RMS = float(np.finfo(np.float32).eps)
MAGIC = 12582912.0

ENGINES = ("sync", "tensor", "vector", "scalar", "gpsimd")

ALL_PHASES = ("prep", "a1", "a2", "b1", "b2", "c2a", "c2b", "c3")

bfnp = ml_dtypes.bfloat16


def host_consts():
    i = np.arange(128)
    ang = 2.0 * np.pi * np.outer(i, i) / 128.0
    onescol = np.zeros((128, 16, 16), np.float32)
    for ec in range(16):
        onescol[:, ec, ec] = 1.0
    hsel = np.zeros((16, 16, 128), np.float32)
    for h in range(16):
        hsel[h, h, :] = 1.0
    c = {
        "c_fre": np.cos(ang),
        "c_fim": -np.sin(ang),
        "c_ifre": np.cos(ang) / 128.0,
        "c_ifim": -np.sin(ang) / 128.0,
        "c_ident": np.eye(128, dtype=np.float32),
        "c_onescol": onescol.reshape(128, 256),
        "c_hsel": hsel.reshape(16, 2048),
        "c_ones": np.ones((128, 512), np.float32),
        "c_onesr": np.ones((1, 128), np.float32),
    }
    out = {}
    for k, v in c.items():
        out[k] = v.astype(bfnp)
    out["c_identf"] = np.eye(128, dtype=np.float32)
    out["c_onesf"] = np.ones((128, 512), np.float32)
    out["c_onesrf"] = np.ones((1, 128), np.float32)
    return out


CONST_SPECS = {k: (v.shape, BF16 if v.dtype == bfnp else F32)
               for k, v in host_consts().items()}

SCRATCH = {
    "xbf":    ((B_LOC, S, D), BF16, "prep", ("a1",)),
    "wqbf":   ((D, D), BF16, "prep", ("a1",)),
    "wkvbf":  ((D, D), BF16, "prep", ("a1",)),
    "wobf":   ((D, D), BF16, "prep", ("a2",)),
    "wqkvbf": ((3 * D, D), BF16, "prep", ("b1",)),
    "wo1bf":  ((D, D), BF16, "prep", ("b2",)),
    "wfcbf":  ((8192, D), BF16, "prep", ("prep",)),
    "wprojbf": ((D, 8192), BF16, "prep", ("prep",)),
    "w1bf":   ((8192, D), BF16, "prep", ("c2a",)),
    "w2bf":   ((D, 8192), BF16, "prep", ("c2b",)),
    "ffnw_s": ((128, 16 * 208), BF16, "prep", ("a2",)),
    "dgn_s":  ((72, D), BF16, "prep", ("a2",)),
    "k1_s":   ((1, 136), BF16, "prep", ("a2",)),
    "gate_s": ((2, 8), F32, "prep", ("a2",)),
    "sn_s":   ((8, 8), F32, "prep", ("prep",)),
    "cc_s":   ((8, 8), F32, "prep", ("prep",)),
    "mn2_s":  ((8, 1), F32, "prep", ("prep",)),
    "vhT":    ((B_LOC, NST, D, ST), BF16, "a1", ("a2",)),
    "h2f":    ((B_LOC, S, D), F32, "a2", ("b2",)),
    "h2bf":   ((B_LOC, S, D), BF16, "a2", ("b1",)),
    "qkvbf":  ((B_LOC, S, 3 * D), BF16, "b1", ("b2",)),
    "h3f":    ((B_LOC, S, D), F32, "b2", ("c3",)),
    "h3bf":   ((B_LOC, S, D), BF16, "b2", ("c2a",)),
    "hrs":    ((64, 128, TOK), BF16, "c2a", ("c2b",)),
    "rms_s":  ((128, 32), F32, "c2a", ("c2b",)),
    "ysb":    ((TOK, D), BF16, "c2b", ("c3",)),
}

INPUT_SPECS = {
    "x": ((B_LOC, S, D), F32),
    "L0_attn_W_q": ((D, D), F32),
    "L0_attn_W_kv": ((D, D), F32),
    "L0_attn_W_o": ((D, D), F32),
    "L0_ffn_select_weight": ((8, D, 8), F32),
    "L0_ffn_select_mean_diff": ((8, D), F32),
    "L0_ffn_imp_center": ((8,), F32),
    "L0_ffn_imp_slope": ((8,), F32),
    "L0_ffn_lora_update": ((8, D, 8), F32),
    "L0_ffn_lora_degen": ((8, D, 8), F32),
    "L0_ffn_bias_w": ((8, D), F32),
    "L0_ffn_debias_w": ((8, D), F32),
    "L1_attn_W_qkv": ((3 * D, D), F32),
    "L1_attn_W_o": ((D, D), F32),
    "L1_ffn_W_fc": ((8192, D), F32),
    "L1_ffn_W_proj": ((D, 8192), F32),
}

PHASE_INPUTS = {
    "prep": ("x", "L0_attn_W_q", "L0_attn_W_kv", "L0_attn_W_o",
             "L0_ffn_select_weight", "L0_ffn_select_mean_diff",
             "L0_ffn_imp_center", "L0_ffn_imp_slope", "L0_ffn_lora_update",
             "L0_ffn_lora_degen", "L0_ffn_bias_w", "L0_ffn_debias_w",
             "L1_attn_W_qkv", "L1_attn_W_o", "L1_ffn_W_fc", "L1_ffn_W_proj"),
    "a1": (), "a2": ("x",), "b1": (), "b2": (), "c2a": (), "c2b": (), "c3": (),
}
PHASE_CONSTS = {
    "prep": ("c_ident", "c_identf"),
    "a1": ("c_fre", "c_fim", "c_ifre", "c_ifim", "c_onescol", "c_hsel", "c_onesf"),
    "a2": ("c_onesr", "c_onesrf", "c_identf", "c_ident", "c_onesf"),
    "b1": (),
    "b2": ("c_ones", "c_onesr"),

    "c2a": ("c_onesf", "c_identf"),
    "c2b": (),
    "c3": (),
}


class Sem:
    def __init__(self, nc, es, name):
        self.h = es.enter_context(nc.semaphore(name))
        self.n = 0


class Prog:

    def __init__(self, ctx=None):
        self.ops = []
        self.ctx = ctx
        self.last = {}

    def add(self, eng, fn, waits=(), inc=None, inck=1, chain=None):
        if chain is None:
            chain = eng in ("vector", "scalar")
        waits = list(waits)
        if eng in ("vector", "scalar"):
            dflt = self.ctx.sem("v" if eng == "vector" else "s")
            if chain and eng in self.last:
                waits.append(self.last[eng])
            if inc is None:
                inc = dflt
                inck = 1
        self.ops.append((eng, fn, tuple(waits), inc, inck))
        if inc is not None:
            inc.n += inck
            if eng in ("vector", "scalar"):
                self.last[eng] = (inc, inc.n)

    def barrier(self, bar):
        target = bar.n + 5
        cur = [(self.ctx.sem(n), self.ctx.sem(n).n)
               for n in ("ld", "t", "v", "s", "g")]
        for g in ENGINES:
            self.ops.append((g, lambda e: e.nop(), tuple(cur), bar, 1))
            bar.n += 1
            self.ops.append((g, None, ((bar, target),), None, 1))
        assert bar.n == target
        self.last = {}

    def check(self):
        import collections
        qs = {g: collections.deque() for g in ENGINES}
        for op in self.ops:
            qs[op[0]].append(op)
        counts = collections.defaultdict(int)
        progress = True
        while progress:
            progress = False
            for g in ENGINES:
                while qs[g]:
                    (eng, fn, waits, inc, inck) = qs[g][0]
                    if any(counts[id(sem)] < val for (sem, val) in waits):
                        break
                    if inc is not None:
                        counts[id(inc)] += inck
                    qs[g].popleft()
                    progress = True
        stuck = {g: len(qs[g]) for g in ENGINES if qs[g]}
        if stuck:
            msgs = []
            for g in ENGINES:
                if not qs[g]:
                    continue
                (eng, fn, waits, inc, inck) = qs[g][0]
                bad = [(self._semname(sem), val, counts[id(sem)])
                       for (sem, val) in waits if counts[id(sem)] < val]
                msgs.append(f"{g}: {len(qs[g])} ops stuck; head waits {bad}")
            raise RuntimeError("DEADLOCK in schedule:\n" + "\n".join(msgs))

    def _semname(self, sem):
        for k, v in self.ctx.sems.items():
            if v is sem:
                return k
        return "?"

    def emit(self, block):
        self.check()
        for engname in ENGINES:
            items = [(f, w, i, k) for (g, f, w, i, k) in self.ops if g == engname]
            if not items:
                continue

            def mk(items):
                def run(e):
                    for (f, w, inc, k) in items:
                        for (sem, val) in w:
                            e.wait_ge(sem.h, val)
                        if f is None:
                            continue
                        ins = f(e)
                        if inc is not None:
                            assert ins is not None
                            ins.then_inc(inc.h, k)
                return run
            getattr(block, engname)(mk(items))


class Ctx:
    def __init__(self, nc, es, phases):
        self.nc = nc
        self.es = es
        self.p = Prog(self)
        self.phases = phases
        self.dram = {}
        self.sems = {}
        for name in ("ld", "ld2", "t", "v", "s", "g", "bar"):
            self.sems[name] = Sem(nc, es, "sem_" + name)

    def sem(self, n):
        return self.sems[n]

    def sb(self, name, shape, dtype, stack=None):
        st = stack or getattr(self, "phase_stack", None) or self.es
        return st.enter_context(self.nc.sbuf_tensor(name, list(shape), dtype))

    def ps(self, name, shape, dtype=F32, stack=None):
        st = stack or getattr(self, "phase_stack", None) or self.es
        return st.enter_context(self.nc.psum_tensor(name, list(shape), dtype))


def declare_dram(ctx, group_phases):
    nc = ctx.nc
    phs = set(group_phases)
    needed_inputs = set()
    needed_consts = set()
    for ph in phs:
        needed_inputs.update(PHASE_INPUTS[ph])
        needed_consts.update(PHASE_CONSTS[ph])
    for name in sorted(needed_inputs):
        shape, dt = INPUT_SPECS[name]
        ctx.dram[name] = nc.dram_tensor(name, list(shape), dt, kind="ExternalInput").ap()
    for name in sorted(needed_consts):
        shape, dt = CONST_SPECS[name]
        ctx.dram[name] = nc.dram_tensor(name, list(shape), dt, kind="ExternalInput").ap()
    ext_in, ext_out = [], []
    for name, (shape, dt, producer, consumers) in SCRATCH.items():
        prod_here = producer in phs
        cons_here = any(c in phs for c in consumers)
        cons_later = any(c not in phs for c in consumers)
        if not prod_here and not cons_here:
            continue
        if prod_here and not cons_later:
            kind = "Internal"
        elif prod_here:
            kind = "ExternalOutput"
            ext_out.append(name)
        else:
            kind = "ExternalInput"
            ext_in.append(name)
        ctx.dram[name] = nc.dram_tensor(name, list(shape), dt, kind=kind).ap()
    if "c3" in phs:
        ctx.dram["out"] = nc.dram_tensor("out", [B_LOC, S, D], F32, kind="ExternalOutput").ap()
        ext_out.append("out")
    return sorted(needed_inputs | needed_consts) + ext_in, ext_out



def ph_prep(ctx):
    nc, p, dr = ctx.nc, ctx.p, ctx.dram
    g = ctx.sem("g")
    ld = ctx.sem("ld")
    v = ctx.sem("v")
    t = ctx.sem("t")
    s = ctx.sem("s")

    casts = [
        (dr["xbf"][0], dr["x"][0]), (dr["xbf"][1], dr["x"][1]),
        (dr["wqbf"], dr["L0_attn_W_q"]), (dr["wkvbf"], dr["L0_attn_W_kv"]),
        (dr["wobf"], dr["L0_attn_W_o"]), (dr["wo1bf"], dr["L1_attn_W_o"]),
        (dr["wqkvbf"][0:3072], dr["L1_attn_W_qkv"][0:3072]),
        (dr["wqkvbf"][3072:6144], dr["L1_attn_W_qkv"][3072:6144]),
        (dr["wfcbf"][0:4096], dr["L1_ffn_W_fc"][0:4096]),
        (dr["wfcbf"][4096:8192], dr["L1_ffn_W_fc"][4096:8192]),
        (dr["wprojbf"], dr["L1_ffn_W_proj"]),
        (dr["dgn_s"][64:72], dr["L0_ffn_bias_w"]),
    ]
    for (o, i) in casts:
        p.add("gpsimd", lambda e, o=o, i=i: e.dma_start(out=o, in_=i), inc=g, inck=16)
    cast_done = g.n

    tern = ctx.sb("tern_in", (128, 2 * 8192), BF16)
    tred = ctx.sb("tred", (128, 128), F32)
    tsc = ctx.sb("tsc", (128, 128), BF16)
    trec = ctx.sb("trec", (128, 128), F32)
    trat = ctx.sb("trat", (128, 8192), BF16)
    ternf = ctx.sb("ternf", (128, 8192), F32)
    tscf = ctx.sb("tscf", (128, 128), F32)
    tq = ctx.sb("tq", (128, 8192), BF16)
    tw1 = ctx.sb("tw1", (128, 2 * 8192), BF16)

    jobs = []
    wfc3 = dr["wfcbf"].rearrange("(n p) d -> n p d", p=128)
    w1b3 = dr["w1bf"].rearrange("(n p) d -> n p d", p=128)
    for i in range(64):
        jobs.append((wfc3[i], w1b3[i], 2048))
    wpr3 = dr["wprojbf"].rearrange("(n p) d -> n p d", p=128)
    w2b3 = dr["w2bf"].rearrange("(n p) d -> n p d", p=128)
    for i in range(16):
        jobs.append((wpr3[i], w2b3[i], 8192))

    v_per_tile = 11
    tern_v0 = v.n
    for i, (src, dst, F) in enumerate(jobs):
        bi = i % 2
        ti = tern[:, bi * 8192: bi * 8192 + F]
        to = tw1[:, bi * 8192: bi * 8192 + F]
        G = F // 64
        w_in = [(g, cast_done)]
        if i >= 2:
            w_in.append((v, tern_v0 + (i - 1) * v_per_tile))
        p.add("sync", lambda e, ti=ti, src=src: e.dma_start(out=ti, in_=src),
              waits=w_in, inc=ld, inck=16)
        w_v = [(ld, ld.n)]
        if i >= 2:
            w_v.append((g, cast_done + (i - 1) * 16))
        ti3 = ti.rearrange("p (gg o) -> p gg o", o=64)
        tf3 = ternf[:, 0:F].rearrange("p (gg o) -> p gg o", o=64)
        p.add("vector", lambda e, ti3=ti3, G=G: e.tensor_reduce(
            tred[:, 0:G].unsqueeze(2), ti3, AX.X, ALU.add, apply_absolute_value=True),
            waits=w_v)
        p.add("vector", lambda e, G=G: e.tensor_scalar(
            tsc[:, 0:G], tred[:, 0:G], 1.0 / 64.0, 1e-8, ALU.mult, ALU.max))
        p.add("vector", lambda e, G=G: e.reciprocal(trec[:, 0:G], tsc[:, 0:G]))
        p.add("vector", lambda e, ti=ti, F=F: e.tensor_copy(ternf[:, 0:F], ti))
        p.add("vector", lambda e, tf3=tf3, G=G: e.tensor_tensor(
            tf3, tf3, trec[:, 0:G].unsqueeze(2).broadcast_to([128, G, 64]), ALU.mult))
        p.add("vector", lambda e, F=F: e.tensor_copy(trat[:, 0:F], ternf[:, 0:F]))
        p.add("vector", lambda e, F=F: e.tensor_scalar(
            ternf[:, 0:F], trat[:, 0:F], MAGIC, MAGIC, ALU.add, ALU.subtract))
        p.add("vector", lambda e, F=F: e.tensor_scalar(
            ternf[:, 0:F], ternf[:, 0:F], 1.0, -1.0, ALU.min, ALU.max))
        p.add("vector", lambda e, G=G: e.tensor_copy(tscf[:, 0:G], tsc[:, 0:G]))
        p.add("vector", lambda e, tf3=tf3, G=G: e.tensor_tensor(
            tf3, tf3, tscf[:, 0:G].unsqueeze(2).broadcast_to([128, G, 64]), ALU.mult))
        p.add("vector", lambda e, to=to, F=F: e.tensor_copy(to, ternf[:, 0:F]))
        p.add("gpsimd", lambda e, to=to, dst=dst: e.dma_start(out=dst, in_=to),
              waits=[(v, v.n)], inc=g, inck=16)
    tern_v_end = v.n

    mu_f = ctx.sb("mu_f", (8, D), F32)
    deb_f = ctx.sb("deb_f", (8, D), F32)
    musq = ctx.sb("musq", (8, D), F32)
    mn2 = ctx.sb("mn2", (8, 1), F32)
    slp0 = ctx.sb("slp0", (1, 8), F32)
    slpc = ctx.sb("slpc", (1, 8), F32)
    p.add("sync", lambda e: e.dma_start(out=mu_f[:], in_=dr["L0_ffn_select_mean_diff"][:, :]), inc=ld, inck=16)
    p.add("sync", lambda e: e.dma_start(out=deb_f[:], in_=dr["L0_ffn_debias_w"][:, :]), inc=ld, inck=16)
    p.add("sync", lambda e: e.dma_start(out=slp0[:], in_=dr["L0_ffn_imp_slope"].unsqueeze(0)), inc=ld, inck=16)
    p.add("sync", lambda e: e.dma_start(out=slpc[:], in_=dr["L0_ffn_imp_center"].unsqueeze(0)), inc=ld, inck=16)
    small_ld = ld.n
    p.add("vector", lambda e: e.tensor_tensor(musq[:], mu_f[:], mu_f[:], ALU.mult),
          waits=[(ld, small_ld)], inc=v)
    p.add("vector", lambda e: e.tensor_reduce(mn2[:], musq[:], AX.X, ALU.add), inc=v)
    p.add("vector", lambda e: e.tensor_tensor(slpc[:], slp0[:], slpc[:], ALU.mult), inc=v)
    p.add("gpsimd", lambda e: e.dma_start(out=dr["gate_s"][0:1, :], in_=slp0[:]),
          waits=[(v, v.n)], inc=g, inck=16)
    p.add("gpsimd", lambda e: e.dma_start(out=dr["gate_s"][1:2, :], in_=slpc[:]),
          waits=[(v, v.n)], inc=g, inck=16)
    p.add("gpsimd", lambda e: e.dma_start(out=dr["mn2_s"][:, :], in_=mn2[:]),
          waits=[(v, v.n)], inc=g, inck=16)
    small_v = v.n

    ffnw = ctx.sb("ffnw", (128, 16 * 208), BF16)
    muhi = ctx.sb("muhi", (128, 16 * 8), BF16)
    debhi = ctx.sb("debhi", (128, 16 * 8), BF16)
    ident_f = ctx.sb("ident_f", (128, 128), F32)
    ident_b = ctx.sb("ident_b", (128, 128), BF16)
    p.add("sync", lambda e: e.dma_start(out=ident_f[:], in_=dr["c_identf"][:, :]), inc=ld, inck=16)
    p.add("sync", lambda e: e.dma_start(out=ident_b[:], in_=dr["c_ident"][:, :]), inc=ld, inck=16)
    ident_ld = ld.n
    ps_a = ctx.ps("pp_a", (128, 512))
    ps_b = ctx.ps("pp_b", (128, 512))
    mulo_t = ctx.sb("mulo_t", (128, 8), BF16)
    muf32 = ctx.sb("muf32", (128, 8), F32)
    hif32 = ctx.sb("hif32", (128, 8), F32)
    for which, srcf, hidst in (("mu", mu_f, muhi), ("deb", deb_f, debhi)):
        for dc in range(16):
            pst = ps_a[:, 0:8] if dc % 2 == 0 else ps_b[:, 0:8]
            p.add("tensor", lambda e, srcf=srcf, dc=dc, pst=pst: e.transpose(
                pst, srcf[:, dc * 128:(dc + 1) * 128], ident_f[0:8, 0:8]),
                waits=[(ld, ident_ld), (v, v.n)], inc=t)
            p.add("vector", lambda e, pst=pst: e.tensor_copy(muf32[:], pst),
                  waits=[(t, t.n)], inc=v)
            p.add("vector", lambda e, hidst=hidst, dc=dc: e.tensor_copy(
                hidst[:, dc * 8:(dc + 1) * 8], muf32[:]), inc=v)
            if which == "mu":
                p.add("vector", lambda e, dc=dc: e.tensor_copy(
                    hif32[:], muhi[:, dc * 8:(dc + 1) * 8]), inc=v)
                p.add("vector", lambda e: e.tensor_tensor(
                    mulo_t[:], muf32[:], hif32[:], ALU.subtract), inc=v)
                p.add("vector", lambda e, dc=dc: e.tensor_scalar_mul(
                    ffnw[:, dc * 208 + 64: dc * 208 + 72], muhi[:, dc * 8:(dc + 1) * 8], -2.0), inc=v)
                p.add("vector", lambda e, dc=dc: e.tensor_scalar_mul(
                    ffnw[:, dc * 208 + 200: dc * 208 + 208], mulo_t[:], -2.0), inc=v)
    mu_v = v.n

    wsl_t = ctx.sb("wsl_t", (128, 8), F32)
    for n in range(8):
        for dc in range(16):
            src = dr["L0_ffn_select_weight"][n, dc * 128:(dc + 1) * 128, :]
            p.add("sync", lambda e, src=src: e.dma_start(out=wsl_t[:], in_=src),
                  waits=[(v, v.n)], inc=ld, inck=16)
            p.add("vector", lambda e, dc=dc, n=n: e.tensor_copy(
                ffnw[:, dc * 208 + n * 8: dc * 208 + n * 8 + 8], wsl_t[:]),
                waits=[(ld, ld.n)], inc=v)
            p.add("vector", lambda e, dc=dc, n=n: e.tensor_copy(
                hif32[:], ffnw[:, dc * 208 + n * 8: dc * 208 + n * 8 + 8]), inc=v)
            p.add("vector", lambda e, dc=dc, n=n: e.tensor_tensor(
                ffnw[:, dc * 208 + 136 + n * 8: dc * 208 + 136 + n * 8 + 8], wsl_t[:],
                hif32[:], ALU.subtract), inc=v)
    for n in range(8):
        for dc in range(16):
            src = dr["L0_ffn_lora_update"][n, dc * 128:(dc + 1) * 128, :]
            p.add("sync", lambda e, src=src: e.dma_start(out=wsl_t[:], in_=src),
                  waits=[(v, v.n)], inc=ld, inck=16)
            p.add("vector", lambda e, dc=dc, n=n: e.tensor_copy(
                ffnw[:, dc * 208 + 72 + n * 8: dc * 208 + 72 + n * 8 + 8], wsl_t[:]),
                waits=[(ld, ld.n)], inc=v)
    ffnw_v = v.n
    p.add("gpsimd", lambda e: e.dma_start(out=dr["ffnw_s"][:, :], in_=ffnw[:]),
          waits=[(v, ffnw_v)], inc=g, inck=16)

    snneg = ctx.sb("snneg", (8, 8), F32)
    ccneg = ctx.sb("ccneg", (8, 8), F32)
    for dstbuf, colbase, vecbuf in ((snneg, 0, muhi), (ccneg, 72, debhi)):
        for n in range(8):
            for dc in range(16):
                lhsT = ffnw[:, dc * 208 + colbase + n * 8: dc * 208 + colbase + n * 8 + 8]
                rhs = vecbuf[:, dc * 8 + n: dc * 8 + n + 1]
                p.add("tensor", lambda e, lhsT=lhsT, rhs=rhs, dc=dc: e.matmul(
                    ps_a[0:8, 0:1], lhsT, rhs, start=(dc == 0), stop=(dc == 15)),
                    waits=[(v, ffnw_v)] if dc == 0 else (), inc=t if dc == 15 else None)
            p.add("vector", lambda e, dstbuf=dstbuf, n=n: e.tensor_scalar_mul(
                dstbuf[:, n:n + 1], ps_a[0:8, 0:1], -1.0), waits=[(t, t.n)], inc=v)
            p.add("tensor", lambda e: e.nop(), waits=[(v, v.n)])
    p.add("gpsimd", lambda e: e.dma_start(out=dr["sn_s"][:, :], in_=snneg[:]),
          waits=[(v, v.n)], inc=g, inck=16)
    p.add("gpsimd", lambda e: e.dma_start(out=dr["cc_s"][:, :], in_=ccneg[:]),
          waits=[(v, v.n)], inc=g, inck=16)
    sncc_g = g.n
    p.add("gpsimd", lambda e: e.dma_start(
        out=dr["k1_s"][0:1, 0:64].rearrange("o (n s) -> o n s", s=8),
        in_=dr["sn_s"].transpose([1, 0]).unsqueeze(0),
        allow_slow_non_contiguous=True),
        waits=[(g, sncc_g)], inc=g, inck=16)
    p.add("gpsimd", lambda e: e.dma_start(
        out=dr["k1_s"][0:1, 64:72], in_=dr["mn2_s"].rearrange("n o -> (n o)").unsqueeze(0)),
        inc=g, inck=16)
    p.add("gpsimd", lambda e: e.dma_start(
        out=dr["k1_s"][0:1, 72:136].rearrange("o (n h) -> o n h", h=8),
        in_=dr["cc_s"].transpose([1, 0]).unsqueeze(0),
        allow_slow_non_contiguous=True),
        inc=g, inck=16)

    dgrow = ctx.sb("dgrow", (8, 16 * 128), BF16)
    ldg_f = ctx.sb("ldg_f", (128, 8), F32)
    ldg_b = ctx.sb("ldg_b", (128, 8), BF16)
    ps_tb = ctx.ps("pp_tb", (128, 1024), BF16)
    for n in range(8):
        for dc in range(16):
            src = dr["L0_ffn_lora_degen"][n, dc * 128:(dc + 1) * 128, :]
            p.add("sync", lambda e, src=src: e.dma_start(out=ldg_f[:], in_=src),
                  waits=[(v, v.n)], inc=ld, inck=16)
            p.add("vector", lambda e: e.tensor_copy(ldg_b[:], ldg_f[:]),
                  waits=[(ld, ld.n)], inc=v)
            pst_b = ps_tb[0:8, (dc % 2) * 512:(dc % 2) * 512 + 128]
            p.add("tensor", lambda e, pst_b=pst_b: e.transpose(pst_b, ldg_b[:], ident_b[:]),
                  waits=[(v, v.n)], inc=t)
            p.add("vector", lambda e, dc=dc, pst_b=pst_b: e.tensor_copy(
                dgrow[:, dc * 128:(dc + 1) * 128], pst_b), waits=[(t, t.n)], inc=v)
        p.add("gpsimd", lambda e, n=n: e.dma_start(
            out=dr["dgn_s"][n * 8:(n + 1) * 8, :], in_=dgrow[:]),
            waits=[(v, v.n)], inc=g, inck=16)
    p.add("gpsimd", lambda e, gn=g.n: e.wait_ge(g.h, gn))



def ph_a1(ctx):
    nc, p, dr = ctx.nc, ctx.p, ctx.dram
    ld, t, v, s, g = (ctx.sem(n) for n in ("ld", "t", "v", "s", "g"))

    wq = ctx.sb("a1_wq", (128, 16 * 2048), BF16)
    wkv = ctx.sb("a1_wkv", (128, 16 * 2048), BF16)
    xT = ctx.sb("a1_xT", (128, 16 * 512), BF16)
    qsb = ctx.sb("a1_q", (128, 16 * 512), BF16)
    kvsb = ctx.sb("a1_kv", (128, 16 * 512), BF16)
    sqb = ctx.sb("a1_sq", (128, 2 * 512), BF16)
    rstdq = ctx.sb("a1_rstdq", (16, 512), BF16)
    rstdkv = ctx.sb("a1_rstdkv", (16, 512), BF16)
    rtmp = ctx.sb("a1_rtmp", (16, 512), F32)
    qn = ctx.sb("a1_qn", (128, 512), BF16)
    kvn = ctx.sb("a1_kvn", (128, 512), F32)
    kvcs = ctx.sb("a1_kvcs", (128, 512), F32)
    kvcsb = ctx.sb("a1_kvcsb", (128, 512), BF16)
    carry = ctx.sb("a1_carry", (128, 16), F32)
    ffre = ctx.sb("a1_ffre", (128, 128), BF16)
    ffim = ctx.sb("a1_ffim", (128, 128), BF16)
    fire = ctx.sb("a1_fire", (128, 128), BF16)
    fiim = ctx.sb("a1_fiim", (128, 128), BF16)
    onescol = ctx.sb("a1_ones", (128, 256), BF16)
    hsel = ctx.sb("a1_hsel", (16, 2048), BF16)
    onesd = ctx.sb("a1_onesd", (128, 512), F32)
    bcb = ctx.sb("a1_bcb", (128, 512), BF16)
    qfre = ctx.sb("a1_qfre", (128, 512), BF16)
    qfim = ctx.sb("a1_qfim", (128, 512), BF16)
    kfre = ctx.sb("a1_kfre", (128, 512), BF16)
    kfim = ctx.sb("a1_kfim", (128, 512), BF16)
    qvre = ctx.sb("a1_qvre", (128, 512), BF16)
    qvim = ctx.sb("a1_qvim", (128, 512), BF16)
    tmpc = ctx.sb("a1_tmpc", (128, 512), F32)
    vhsb = ctx.sb("a1_vh", (128, 2 * 512), BF16)

    P = [ctx.ps(f"a1_ps{i}", (128, 512)) for i in range(8)]

    for dst, nmc in ((ffre, "c_fre"), (ffim, "c_fim"), (fire, "c_ifre"), (fiim, "c_ifim"),
                     (onescol, "c_onescol"), (hsel, "c_hsel"), (onesd, "c_onesf")):
        p.add("sync", lambda e, dst=dst, nmc=nmc: e.dma_start(out=dst[:], in_=dr[nmc][:, :]),
              inc=ld, inck=16)
    for wsb, wd in ((wq, "wqbf"), (wkv, "wkvbf")):
        for dc in range(16):
            for eb in range(4):
                src = dr[wd][eb * 512:(eb + 1) * 512, dc * 128:(dc + 1) * 128]
                dst = wsb[:, dc * 2048 + eb * 512: dc * 2048 + (eb + 1) * 512]
                p.add("sync", lambda e, dst=dst, src=src: e.dma_start_transpose(dst, src),
                      inc=ld, inck=16)
    const_ld = ld.n

    def xT_ap(buf_i, dc):
        return xT[:, dc * 512: (dc + 1) * 512]

    n_steps = B_LOC * NST
    evq_s = {}
    step_t_end = {}

    for step in range(n_steps):
        b, st = divmod(step, NST)
        bi = step % 2
        w_l = [(ld, const_ld)]
        if step >= 1:
            w_l.append((t, step_t_end[step - 1]))
        for dc in range(16):
            src = dr["xbf"][b, st * 512:(st + 1) * 512, dc * 128:(dc + 1) * 128]
            p.add("sync", lambda e, d=xT_ap(bi, dc), src=src: e.dma_start_transpose(d, src),
                  waits=w_l if dc == 0 else (), inc=ld, inck=16)
        ld_step = ld.n

        if st == 0:
            p.add("vector", lambda e: e.memset(carry[:], 0.0), inc=v)

        if step == 0:
            bank_evac = {}
            nsq_t = {}
            vh_store = {}
        for wi, (wsb, dstq, nsqbank) in enumerate(((wq, qsb, 4), (wkv, kvsb, 5))):
            for ec in range(16):
                bidx = wi * 2 + (ec % 2)
                bank = P[bidx]
                w_mm = [(ld, ld_step)]
                if bidx in bank_evac:
                    w_mm.append((s, bank_evac[bidx]))
                for dc in range(16):
                    lhsT = wsb[:, dc * 2048 + ec * 128: dc * 2048 + (ec + 1) * 128]
                    p.add("tensor", lambda e, bank=bank, lhsT=lhsT, dc=dc, bi=bi: e.matmul(
                        bank[:], lhsT, xT_ap(bi, dc), start=(dc == 0), stop=(dc == 15)),
                        waits=w_mm if dc == 0 else (), inc=t if dc == 15 else None)
                mm_done = t.n
                p.add("scalar", lambda e, bank=bank, dstq=dstq, ec=ec: e.copy(
                    dstq[:, ec * 512:(ec + 1) * 512], bank[:]), waits=[(t, mm_done)], inc=s)
                w_sq = [(t, nsq_t[ec % 2])] if (ec % 2) in nsq_t else []
                p.add("scalar", lambda e, bank=bank, ec=ec: e.activation(
                    sqb[:, (ec % 2) * 512:(ec % 2 + 1) * 512], bank[:], AF.Square),
                    waits=w_sq, inc=s)
                sq_done = s.n
                bank_evac[bidx] = sq_done
                p.add("tensor", lambda e, nb=P[nsqbank], ec=ec: e.matmul(
                    nb[0:16, :], onescol[:, ec * 16:(ec + 1) * 16],
                    sqb[:, (ec % 2) * 512:(ec % 2 + 1) * 512],
                    start=(ec == 0), stop=(ec == 15)),
                    waits=[(s, sq_done)], inc=t)
                nsq_t[ec % 2] = t.n
        nsq_done = t.n

        for nsqbank, rdst in ((4, rstdq), (5, rstdkv)):
            p.add("vector", lambda e, nb=P[nsqbank]: e.tensor_scalar_mul(
                rtmp[:], nb[0:16, :], 128.0), waits=[(t, nsq_done)], inc=v)
            p.add("scalar", lambda e: e.activation(rtmp[:], rtmp[:], AF.Ln),
                  waits=[(v, v.n)], inc=s)
            p.add("vector", lambda e: e.tensor_scalar_mul(rtmp[:], rtmp[:], -0.5),
                  waits=[(s, s.n)], inc=v)
            p.add("scalar", lambda e: e.activation(rtmp[:], rtmp[:], AF.Exp),
                  waits=[(v, v.n)], inc=s)
            p.add("vector", lambda e, rdst=rdst: e.tensor_copy(rdst[:], rtmp[:]),
                  waits=[(s, s.n)], inc=v)
        rstd_done = v.n

        for ec in range(16):
            bcp = P[0]
            p.add("tensor", lambda e, ec=ec: e.matmul(
                bcp[:], hsel[:, ec * 128:(ec + 1) * 128], rstdq[:], start=True, stop=True),
                waits=[(v, v.n)], inc=t)
            p.add("vector", lambda e: e.tensor_copy(bcb[:], bcp[:]), waits=[(t, t.n)])
            p.add("vector", lambda e, ec=ec: e.tensor_tensor(
                qn[:], qsb[:, ec * 512:(ec + 1) * 512], bcb[:], ALU.mult))
            qn_done = v.n
            p.add("tensor", lambda e: e.matmul(P[1][:], ffre[:], qn[:], start=True, stop=True),
                  waits=[(v, qn_done)], inc=t)
            p.add("tensor", lambda e: e.matmul(P[2][:], ffim[:], qn[:], start=True, stop=True),
                  inc=t)
            qf_mm = t.n
            p.add("scalar", lambda e: e.copy(qfre[:], P[1][:]), waits=[(t, qf_mm)], inc=s)
            p.add("scalar", lambda e: e.copy(qfim[:], P[2][:]), inc=s)
            qf_ev = s.n
            p.add("tensor", lambda e, ec=ec: e.matmul(
                bcp[:], hsel[:, ec * 128:(ec + 1) * 128], rstdkv[:], start=True, stop=True),
                waits=[(v, qn_done)], inc=t)
            p.add("vector", lambda e: e.tensor_copy(bcb[:], bcp[:]), waits=[(t, t.n)])
            p.add("vector", lambda e, ec=ec: e.tensor_tensor(
                kvn[:], kvsb[:, ec * 512:(ec + 1) * 512], bcb[:], ALU.mult))
            p.add("vector", lambda e, ec=ec: e.tensor_tensor_scan(
                kvcs[:], onesd[:], kvn[:], carry[:, ec:ec + 1], ALU.mult, ALU.add), inc=v)
            p.add("vector", lambda e, ec=ec: e.tensor_copy(carry[:, ec:ec + 1], kvcs[:, 511:512]), inc=v)
            p.add("vector", lambda e: e.tensor_copy(kvcsb[:], kvcs[:]), inc=v)
            kvcs_done = v.n
            p.add("tensor", lambda e: e.matmul(P[3][:], ffre[:], kvcsb[:], start=True, stop=True),
                  waits=[(v, kvcs_done)], inc=t)
            p.add("tensor", lambda e: e.matmul(P[4][:], ffim[:], kvcsb[:], start=True, stop=True),
                  inc=t)
            kf_mm = t.n
            p.add("scalar", lambda e: e.copy(kfre[:], P[3][:]), waits=[(t, kf_mm)], inc=s)
            p.add("scalar", lambda e: e.copy(kfim[:], P[4][:]), inc=s)
            kf_ev = s.n
            p.add("vector", lambda e: e.tensor_tensor(tmpc[:], qfre[:], kfre[:], ALU.mult),
                  waits=[(s, kf_ev)], inc=v)
            p.add("vector", lambda e: e.tensor_tensor(qvre[:], qfim[:], kfim[:], ALU.mult), inc=v)
            p.add("vector", lambda e: e.tensor_tensor(qvre[:], tmpc[:], qvre[:], ALU.subtract), inc=v)
            p.add("vector", lambda e: e.tensor_tensor(tmpc[:], qfre[:], kfim[:], ALU.mult), inc=v)
            p.add("vector", lambda e: e.tensor_tensor(qvim[:], qfim[:], kfre[:], ALU.mult), inc=v)
            p.add("vector", lambda e: e.tensor_tensor(qvim[:], tmpc[:], qvim[:], ALU.add), inc=v)
            qv_done = v.n
            p.add("tensor", lambda e: e.matmul(P[5][:], fire[:], qvre[:], start=True, stop=False),
                  waits=[(v, qv_done)], inc=None)
            p.add("tensor", lambda e: e.matmul(P[5][:], fiim[:], qvim[:], start=False, stop=True),
                  inc=t)
            vh_mm = t.n
            w_vh = [(t, vh_mm)]
            if (ec % 2) in vh_store:
                w_vh.append((g, vh_store[ec % 2]))
            p.add("scalar", lambda e, ec=ec: e.copy(
                vhsb[:, (ec % 2) * 512:(ec % 2 + 1) * 512], P[5][:]),
                waits=w_vh, inc=s)
            vh_ev = s.n
            dst = dr["vhT"][b, st, ec * 128:(ec + 1) * 128, :]
            p.add("gpsimd", lambda e, dst=dst, ec=ec: e.dma_start(
                out=dst, in_=vhsb[:, (ec % 2) * 512:(ec % 2 + 1) * 512]),
                waits=[(s, vh_ev)], inc=g, inck=16)
            vh_store[ec % 2] = g.n
        step_t_end[step] = t.n
    p.add("gpsimd", lambda e, gn=g.n: e.wait_ge(g.h, gn))



def _ln_stats(ctx, pre_ap, nfree, tl, eps):
    p, v, s = ctx.p, ctx.sem("v"), ctx.sem("s")
    p.add("scalar", lambda e: e.activation(
        tl["dump"][:, 0:nfree], pre_ap, AF.Identity, accum_out=tl["sumb"][:, 0:1]),
        waits=[(v, v.n)])
    p.add("vector", lambda e: e.tensor_scalar_mul(
        tl["negm"][:, 0:1], tl["sumb"][:, 0:1], -1.0 / nfree), waits=[(s, s.n)])
    p.add("scalar", lambda e: e.activation(
        tl["dump"][:, 0:nfree], pre_ap, AF.Square, bias=tl["negm"][:, 0:1],
        accum_out=tl["ssq"][:, 0:1]), waits=[(v, v.n)])
    p.add("vector", lambda e, eps=eps: e.tensor_scalar(
        tl["vpe"][:, 0:1], tl["ssq"][:, 0:1], 1.0 / nfree, eps, ALU.mult, ALU.add),
        waits=[(s, s.n)])
    p.add("scalar", lambda e: e.activation(
        tl["rstd"][:, 0:1], tl["vpe"][:, 0:1], AF.Ln), waits=[(v, v.n)])
    p.add("vector", lambda e: e.tensor_scalar_mul(
        tl["rstd"][:, 0:1], tl["rstd"][:, 0:1], -0.5), waits=[(s, s.n)])
    p.add("scalar", lambda e: e.activation(
        tl["rstd"][:, 0:1], tl["rstd"][:, 0:1], AF.Exp), waits=[(v, v.n)])
    p.add("vector", lambda e: e.tensor_scalar_mul(
        tl["mean"][:, 0:1], tl["sumb"][:, 0:1], 1.0 / nfree), waits=[(s, s.n)])


def ph_a2(ctx):
    nc, p, dr = ctx.nc, ctx.p, ctx.dram
    ld, t, v, s, g = (ctx.sem(n) for n in ("ld", "t", "v", "s", "g"))

    wo = ctx.sb("a2_wo", (128, 16 * 2048), BF16)
    vh = ctx.sb("a2_vh", (128, 16 * 512), BF16)
    ffnw = ctx.sb("a2_ffnw", (128, 16 * 208), BF16)
    dgn = ctx.sb("a2_dgn", (72, 2048), BF16)
    k1 = ctx.sb("a2_k1", (1, 136), BF16)
    g0 = ctx.sb("a2_g0", (1, 8), F32)
    g1 = ctx.sb("a2_g1", (1, 8), F32)
    slope_b = ctx.sb("a2_slb", (128, 8), F32)
    slc_b = ctx.sb("a2_slc", (128, 8), F32)
    onesr = ctx.sb("a2_onesr", (1, 128), BF16)
    onesrf = ctx.sb("a2_onesrf", (1, 128), F32)
    identf = ctx.sb("a2_idf", (128, 128), F32)
    identb = ctx.sb("a2_idb", (128, 128), BF16)
    onesf = ctx.sb("a2_onesf", (128, 8), F32)
    xf = ctx.sb("a2_xf", (128, 2 * 2048), F32)
    pre = ctx.sb("a2_pre", (128, 2048), F32)
    h1f = ctx.sb("a2_h1", (128, 2048), F32)
    dump = ctx.sb("a2_dump", (128, 2048), F32)
    xhi = ctx.sb("a2_xhi", (128, 2048), BF16)
    xlo = ctx.sb("a2_xlo", (128, 2048), BF16)
    tpf = ctx.sb("a2_tpf", (128, 128), F32)
    tpf2 = ctx.sb("a2_tpf2", (128, 128), F32)
    sel = ctx.sb("a2_sel", (128, 136), F32)
    h2 = ctx.sb("a2_h2", (128, 2048), F32)
    h2b = ctx.sb("a2_h2b", (128, 2048), BF16)
    hext = ctx.sb("a2_hext", (128, 72), BF16)
    hexT = ctx.sb("a2_hexT", (72, 128), BF16)
    sm = {k: ctx.sb("a2_" + k, (128, 8), F32) for k in
          ("sumb", "negm", "ssq", "vpe", "rstd", "mean", "sumx2", "r2", "rr",
           "s1", "score", "z", "expz", "gate", "gmax", "ge", "pfx", "eq1",
           "oh", "s2m", "macc")}
    sm["dump"] = dump
    d2 = ctx.sb("a2_d2", (128, 64), F32)
    tmp512 = ctx.sb("a2_tmp512", (128, 512), F32)

    P = [ctx.ps(f"a2_ps{i}", (128, 512)) for i in range(7)]
    ps_tb = ctx.ps("a2_pstb", (128, 1024), BF16)

    for dst, nmc in ((onesr, "c_onesr"), (onesrf, "c_onesrf"), (identf, "c_identf"),
                     (identb, "c_ident")):
        p.add("sync", lambda e, dst=dst, nmc=nmc: e.dma_start(out=dst[:], in_=dr[nmc][:, :]),
              inc=ld, inck=16)
    p.add("sync", lambda e: e.dma_start(out=onesf[:], in_=dr["c_onesf"][:, 0:8],
                                        allow_slow_non_contiguous=True), inc=ld, inck=16)
    p.add("sync", lambda e: e.dma_start(out=ffnw[:], in_=dr["ffnw_s"][:, :]), inc=ld, inck=16)
    p.add("sync", lambda e: e.dma_start(out=dgn[:], in_=dr["dgn_s"][:, :]), inc=ld, inck=16)
    p.add("sync", lambda e: e.dma_start(out=k1[:], in_=dr["k1_s"][:, :]), inc=ld, inck=16)
    p.add("sync", lambda e: e.dma_start(out=g0[:], in_=dr["gate_s"][0:1, :]), inc=ld, inck=16)
    p.add("sync", lambda e: e.dma_start(out=g1[:], in_=dr["gate_s"][1:2, :]), inc=ld, inck=16)
    for dc in range(16):
        for eb in range(4):
            src = dr["wobf"][eb * 512:(eb + 1) * 512, dc * 128:(dc + 1) * 128]
            dst = wo[:, dc * 2048 + eb * 512: dc * 2048 + (eb + 1) * 512]
            p.add("sync", lambda e, dst=dst, src=src: e.dma_start_transpose(dst, src),
                  inc=ld, inck=16)
    const_ld = ld.n
    p.add("tensor", lambda e: e.matmul(P[4][:, 0:8], onesrf[:], g0[:], start=True, stop=True),
          waits=[(ld, const_ld)], inc=t)
    p.add("tensor", lambda e: e.matmul(P[5][:, 0:8], onesrf[:], g1[:], start=True, stop=True),
          inc=t)
    p.add("vector", lambda e: e.tensor_copy(slope_b[:], P[4][:, 0:8]), waits=[(t, t.n)])
    p.add("vector", lambda e: e.tensor_copy(slc_b[:], P[5][:, 0:8]))
    gates_v = v.n

    vh_t_guard = 0
    prev_store = {}
    for step in range(B_LOC * NST):
        b, st = divmod(step, NST)
        w_l = [(ld, const_ld)]
        if step > 0:
            w_l.append((t, vh_t_guard))
        for ec in range(16):
            src = dr["vhT"][b, st, ec * 128:(ec + 1) * 128, :]
            p.add("sync", lambda e, ec=ec, src=src: e.dma_start(
                out=vh[:, ec * 512:(ec + 1) * 512], in_=src),
                waits=w_l if ec == 0 else (), inc=ld, inck=16)
        vh_ld = ld.n
        for tt in range(4):
            row0 = st * 512 + tt * 128
            w_x = [(v, v.n)] if step + tt > 0 else []
            p.add("sync", lambda e, b=b, row0=row0, tt=tt: e.dma_start(
                out=xf[:, (tt % 2) * 2048:(tt % 2 + 1) * 2048],
                in_=dr["x"][b, row0:row0 + 128, :]), waits=w_x, inc=ld, inck=16)
            x_ld = ld.n
            for dc in range(4):
                for ec in range(16):
                    lhsT = vh[:, ec * 512 + tt * 128: ec * 512 + tt * 128 + 128]
                    rhs = wo[:, ec * 2048 + dc * 512: ec * 2048 + (dc + 1) * 512]
                    p.add("tensor", lambda e, dc=dc, lhsT=lhsT, rhs=rhs, ec=ec: e.matmul(
                        P[dc][:], lhsT, rhs, start=(ec == 0), stop=(ec == 15)),
                        waits=[(ld, vh_ld), (v, v.n)] if ec == 0 else (),
                        inc=t if ec == 15 else None)
            wo_done = t.n
            vh_t_guard = t.n
            for dc in range(4):
                p.add("vector", lambda e, dc=dc, tt=tt: e.tensor_tensor(
                    pre[:, dc * 512:(dc + 1) * 512],
                    xf[:, (tt % 2) * 2048 + dc * 512:(tt % 2) * 2048 + (dc + 1) * 512],
                    P[dc][:], ALU.add), waits=[(t, wo_done), (ld, x_ld)])
            _ln_stats(ctx, pre[:], 2048, sm, EPS_LN)
            p.add("vector", lambda e: e.reciprocal(sm["sumx2"][:, 0:1], sm["vpe"][:, 0:1]))
            p.add("vector", lambda e: e.tensor_tensor(
                sm["sumx2"][:, 0:1], sm["ssq"][:, 0:1], sm["sumx2"][:, 0:1], ALU.mult))
            p.add("vector", lambda e: e.tensor_scalar(
                h1f[:], pre[:], sm["mean"][:, 0:1], sm["rstd"][:, 0:1],
                ALU.subtract, ALU.mult))
            h1_done = v.n
            for dc in range(16):
                bank = P[4 + dc % 2]
                p.add("tensor", lambda e, dc=dc, bank=bank: e.transpose(
                    bank[:, 0:128], h1f[:, dc * 128:(dc + 1) * 128], identf[:]),
                    waits=[(v, v.n)], inc=t)
                p.add("vector", lambda e, bank=bank: e.tensor_copy(tpf[:], bank[:, 0:128]),
                      waits=[(t, t.n)])
                p.add("vector", lambda e, dc=dc: e.tensor_copy(
                    xhi[:, dc * 128:(dc + 1) * 128], tpf[:]))
                p.add("vector", lambda e, dc=dc: e.tensor_copy(
                    tpf2[:], xhi[:, dc * 128:(dc + 1) * 128]))
                p.add("vector", lambda e, dc=dc: e.tensor_tensor(
                    xlo[:, dc * 128:(dc + 1) * 128], tpf[:], tpf2[:], ALU.subtract))
            xsplit_done = v.n
            nsel = 0
            for pas, (xsrc, c0, c1) in enumerate(
                    ((xhi, 0, 136), (xhi, 136, 208), (xlo, 0, 72))):
                for dc in range(16):
                    lhsT = xsrc[:, dc * 128:(dc + 1) * 128]
                    rhs = ffnw[:, dc * 208 + c0: dc * 208 + c1]
                    odst = P[6][:, 0:c1 - c0]
                    p.add("tensor", lambda e, lhsT=lhsT, rhs=rhs, odst=odst,
                          first=(nsel == 0): e.matmul(
                        odst, lhsT, rhs, start=first, stop=False),
                        waits=[(v, xsplit_done)] if nsel == 0 else ())
                    nsel += 1
            p.add("tensor", lambda e: e.matmul(
                P[6][:, 0:136], onesr[:], k1[:], start=False, stop=True), inc=t)
            sel_done = t.n
            p.add("vector", lambda e: e.tensor_copy(sel[:], P[6][:, 0:136]),
                  waits=[(t, sel_done)])
            p.add("vector", lambda e: e.tensor_scalar(
                sm["r2"][:], sel[:, 64:72], sm["sumx2"][:, 0:1], None, ALU.add))
            p.add("vector", lambda e: e.reciprocal(sm["rr"][:], sm["r2"][:]))
            p.add("vector", lambda e: e.tensor_tensor(d2[:], sel[:, 0:64], sel[:, 0:64], ALU.mult))
            p.add("vector", lambda e: e.tensor_reduce(
                sm["s1"][:].unsqueeze(2), d2[:].rearrange("p (n s_) -> p n s_", s_=8),
                AX.X, ALU.add))
            p.add("vector", lambda e: e.tensor_tensor(sm["score"][:], sm["s1"][:], sm["rr"][:], ALU.mult))
            p.add("vector", lambda e: e.tensor_tensor(sm["z"][:], sm["score"][:], slope_b[:], ALU.mult),
                  waits=[(v, gates_v)])
            p.add("vector", lambda e: e.tensor_tensor(sm["z"][:], sm["z"][:], slc_b[:], ALU.subtract))
            p.add("scalar", lambda e: e.activation(sm["expz"][:], sm["z"][:], AF.Exp,
                                                   bias=0.0, scale=-1.0), waits=[(v, v.n)])
            p.add("vector", lambda e: e.tensor_scalar_add(sm["gate"][:], sm["expz"][:], 1.0),
                  waits=[(s, s.n)])
            p.add("vector", lambda e: e.reciprocal(sm["gate"][:], sm["gate"][:]))
            p.add("vector", lambda e: e.reduce_max(sm["gmax"][:, 0:1], sm["gate"][:], AX.X))
            p.add("vector", lambda e: e.tensor_single_scalar(
                sm["ge"][:], sm["gate"][:], sm["gmax"][:, 0:1], ALU.is_ge))
            p.add("vector", lambda e: e.tensor_tensor_scan(
                sm["pfx"][:], onesf[:], sm["ge"][:], 0.0, ALU.mult, ALU.add))
            p.add("vector", lambda e: e.tensor_single_scalar(
                sm["eq1"][:], sm["pfx"][:], 1.0, ALU.is_equal))
            p.add("vector", lambda e: e.tensor_tensor(sm["oh"][:], sm["ge"][:], sm["eq1"][:], ALU.mult))
            p.add("vector", lambda e: e.tensor_scalar(
                sm["s2m"][:, 0:1], sm["gmax"][:, 0:1], -1.0, 2.0, ALU.mult, ALU.add))
            p.add("vector", lambda e: e.tensor_scalar(
                sm["macc"][:], sel[:, 72:80], sm["oh"][:, 0:1], None, ALU.mult))
            for n in range(1, 8):
                p.add("vector", lambda e, n=n: e.scalar_tensor_tensor(
                    sm["macc"][:], sel[:, 72 + 8 * n: 80 + 8 * n], sm["oh"][:, n:n + 1],
                    sm["macc"][:], ALU.mult, ALU.add))
            for n in range(8):
                p.add("vector", lambda e, n=n: e.tensor_scalar(
                    hext[:, 8 * n:8 * n + 8], sm["macc"][:], sm["oh"][:, n:n + 1],
                    None, ALU.mult))
            p.add("vector", lambda e: e.tensor_copy(hext[:, 64:72], sm["oh"][:]))
            hext_done = v.n
            p.add("tensor", lambda e: e.transpose(ps_tb[0:72, 0:128], hext[:], identb[:]),
                  waits=[(v, hext_done)], inc=t)
            p.add("vector", lambda e: e.tensor_copy(hexT[:], ps_tb[0:72, 0:128]),
                  waits=[(t, t.n)])
            hexT_done = v.n
            for dc in range(4):
                p.add("tensor", lambda e, dc=dc: e.matmul(
                    P[dc][:], hexT[:], dgn[:, dc * 512:(dc + 1) * 512],
                    start=True, stop=True),
                    waits=[(v, hexT_done)], inc=t)
            mod_done = t.n
            for dc in range(4):
                p.add("vector", lambda e, dc=dc: e.tensor_scalar(
                    tmp512[:], h1f[:, dc * 512:(dc + 1) * 512], sm["s2m"][:, 0:1],
                    None, ALU.mult), waits=[(t, mod_done)] if dc == 0 else ())
                p.add("vector", lambda e, dc=dc: e.scalar_tensor_tensor(
                    pre[:, dc * 512:(dc + 1) * 512], P[dc][:], sm["gmax"][:, 0:1],
                    tmp512[:], ALU.mult, ALU.add))
            _ln_stats(ctx, pre[:], 2048, sm, EPS_LN)
            p.add("vector", lambda e: e.tensor_scalar(
                h2[:], pre[:], sm["mean"][:, 0:1], sm["rstd"][:, 0:1],
                ALU.subtract, ALU.mult))
            p.add("vector", lambda e: e.tensor_copy(h2b[:], h2[:]))
            h2_done = v.n
            w_st = [(v, h2_done)]
            p.add("gpsimd", lambda e, b=b, row0=row0: e.dma_start(
                out=dr["h2f"][b, row0:row0 + 128, :], in_=h2[:]), waits=w_st, inc=g, inck=16)
            p.add("gpsimd", lambda e, b=b, row0=row0: e.dma_start(
                out=dr["h2bf"][b, row0:row0 + 128, :], in_=h2b[:]), waits=[(v, h2_done)],
                inc=g, inck=16)
            p.add("vector", lambda e, gn=g.n: e.nop(), waits=[(g, g.n)])
    p.add("gpsimd", lambda e, gn=g.n: e.wait_ge(g.h, gn))



def ph_b1(ctx):
    nc, p, dr = ctx.nc, ctx.p, ctx.dram
    ld, t, v, s, g = (ctx.sem(n) for n in ("ld", "t", "v", "s", "g"))
    wk = ctx.sb("b1_wk", (128, 16 * 3072), BF16)
    h2T = ctx.sb("b1_h2T", (128, 16 * 512), BF16)
    ev = ctx.sb("b1_ev", (128, 4 * 512), BF16)
    P = [ctx.ps(f"b1_ps{i}", (128, 512)) for i in range(4)]

    ev_guard = {}
    ev_g = {}
    for half in range(2):
        e0 = half * 3072
        w_l = [(t, t.n)]
        for dc in range(16):
            for eb in range(6):
                src = dr["wqkvbf"][e0 + eb * 512: e0 + (eb + 1) * 512,
                                   dc * 128:(dc + 1) * 128]
                dst = wk[:, dc * 3072 + eb * 512: dc * 3072 + (eb + 1) * 512]
                p.add("sync", lambda e, dst=dst, src=src: e.dma_start_transpose(dst, src),
                      waits=w_l if (dc == 0 and eb == 0) else (), inc=ld, inck=16)
        wk_ld = ld.n
        for step in range(B_LOC * NST):
            b, st = divmod(step, NST)
            w_h = [(ld, wk_ld)]
            if (half, step) != (0, 0):
                w_h.append((t, t.n))
            for dc in range(16):
                src = dr["h2bf"][b, st * 512:(st + 1) * 512, dc * 128:(dc + 1) * 128]
                p.add("sync", lambda e, dc=dc, src=src: e.dma_start_transpose(
                    h2T[:, dc * 512:(dc + 1) * 512], src),
                    waits=w_h if dc == 0 else (), inc=ld, inck=16)
            h2_ld = ld.n
            for tt in range(4):
                row0 = st * 512 + tt * 128
                for eb in range(6):
                    bank = P[eb % 4]
                    w_mm = [(ld, h2_ld)]
                    if (eb % 4) in ev_guard:
                        w_mm.append((s, ev_guard[eb % 4]))
                    for dc in range(16):
                        lhsT = h2T[:, dc * 512 + tt * 128: dc * 512 + tt * 128 + 128]
                        rhs = wk[:, dc * 3072 + eb * 512: dc * 3072 + (eb + 1) * 512]
                        p.add("tensor", lambda e, bank=bank, lhsT=lhsT, rhs=rhs, dc=dc: e.matmul(
                            bank[:], lhsT, rhs, start=(dc == 0), stop=(dc == 15)),
                            waits=w_mm if dc == 0 else (), inc=t if dc == 15 else None)
                    mm_done = t.n
                    w_ev = [(t, mm_done)]
                    p.add("scalar", lambda e, bank=bank, eb=eb: e.copy(
                        ev[:, (eb % 4) * 512:(eb % 4 + 1) * 512], bank[:]),
                        waits=w_ev + ([(g, ev_g[eb % 4])] if (eb % 4) in ev_g else []))
                    ev_guard[eb % 4] = s.n
                    dst = dr["qkvbf"][b, row0:row0 + 128,
                                      e0 + eb * 512: e0 + (eb + 1) * 512]
                    p.add("gpsimd", lambda e, dst=dst, eb=eb: e.dma_start(
                        out=dst, in_=ev[:, (eb % 4) * 512:(eb % 4 + 1) * 512]),
                        waits=[(s, s.n)], inc=g, inck=16)
                    ev_g[eb % 4] = g.n
    p.add("gpsimd", lambda e, gn=g.n: e.wait_ge(g.h, gn))



def ph_b2(ctx):
    nc, p, dr = ctx.nc, ctx.p, ctx.dram
    ld, t, v, s, g = (ctx.sem(n) for n in ("ld", "t", "v", "s", "g"))
    SC = 1.0 / np.sqrt(128.0)

    wo = ctx.sb("b2_wo", (128, 16 * 2048), BF16)
    atn = ctx.sb("b2_atn", (128, 16 * 2048), BF16)
    qT = ctx.sb("b2_qT", (128, 2 * 2048), BF16)
    kT = ctx.sb("b2_kT", (128, 2 * 2048), BF16)
    vv = ctx.sb("b2_vv", (128, 2 * 2048), BF16)
    pt = ctx.sb("b2_pt", (128, 1024), BF16)
    rsum = ctx.sb("b2_rs", (1, 512), F32)
    rsumb = ctx.sb("b2_rsb", (1, 512), BF16)
    rbc = ctx.sb("b2_rbc", (128, 512), F32)
    onesr = ctx.sb("b2_onesr", (1, 128), BF16)
    onesc = ctx.sb("b2_onesc", (128, 1), BF16)
    h2t = ctx.sb("b2_h2t", (128, 2048), F32)
    pre = ctx.sb("b2_pre", (128, 2048), F32)
    dump = ctx.sb("b2_dump", (128, 2048), F32)
    h3 = ctx.sb("b2_h3", (128, 2048), F32)
    h3b = ctx.sb("b2_h3b", (128, 2048), BF16)
    sm = {k: ctx.sb("b2s_" + k, (128, 1), F32) for k in
          ("sumb", "negm", "ssq", "vpe", "rstd", "mean")}
    sm["dump"] = dump

    P = [ctx.ps(f"b2_ps{i}", (128, 512)) for i in range(8)]

    p.add("sync", lambda e: e.dma_start(out=onesr[:], in_=dr["c_onesr"][:, :]), inc=ld, inck=16)
    p.add("sync", lambda e: e.dma_start(out=onesc[:], in_=dr["c_ones"][:, 0:1],
                                        allow_slow_non_contiguous=True), inc=ld, inck=16)
    for dc in range(16):
        for eb in range(4):
            src = dr["wo1bf"][eb * 512:(eb + 1) * 512, dc * 128:(dc + 1) * 128]
            dst = wo[:, dc * 2048 + eb * 512: dc * 2048 + (eb + 1) * 512]
            p.add("sync", lambda e, dst=dst, src=src: e.dma_start_transpose(dst, src),
                  inc=ld, inck=16)
    const_ld = ld.n

    qkv_flat = dr["qkvbf"].rearrange("b s e -> b (s e)").rearrange(
        "b (th hh l i) -> b th hh l i", th=3, hh=16, l=2048)

    for b in range(B_LOC):
        atn_t_guard = t.n
        for h in range(16):
            bi = h % 2
            w_l = [(ld, const_ld)]
            if (b, h) != (0, 0):
                w_l.append((t, t.n))
            p.add("sync", lambda e, b=b, h=h, bi=bi: e.dma_start_transpose(
                qT[:, bi * 2048:(bi + 1) * 2048], qkv_flat[b, 0, h]),
                waits=w_l, inc=ld, inck=16)
            p.add("sync", lambda e, b=b, h=h, bi=bi: e.dma_start_transpose(
                kT[:, bi * 2048:(bi + 1) * 2048], qkv_flat[b, 1, h]),
                inc=ld, inck=16)
            p.add("sync", lambda e, b=b, h=h, bi=bi: e.dma_start(
                out=vv[:, bi * 2048:(bi + 1) * 2048].rearrange("p (kc i) -> p kc i", i=128),
                in_=qkv_flat[b, 2, h].rearrange("(kc pp) i -> pp kc i", pp=128)),
                inc=ld, inck=16)
            qkv_ld = ld.n
            p.add("vector", lambda e, bi=bi: e.tensor_scalar_mul(
                qT[:, bi * 2048:(bi + 1) * 2048], qT[:, bi * 2048:(bi + 1) * 2048], SC),
                waits=[(ld, qkv_ld)])
            q_ready = v.n
            pt_guard = {}
            for qg in range(4):
                for kc in range(16):
                    sb_bank = P[kc % 2]
                    p.add("tensor", lambda e, kc=kc, qg=qg, bi=bi, sb_bank=sb_bank: e.matmul(
                        sb_bank[:], kT[:, bi * 2048 + kc * 128: bi * 2048 + kc * 128 + 128],
                        qT[:, bi * 2048 + qg * 512: bi * 2048 + (qg + 1) * 512],
                        start=True, stop=True),
                        waits=[(v, q_ready), (s, s.n)], inc=t)
                    sc_done = t.n
                    w_exp = [(t, sc_done)]
                    if (kc % 2) in pt_guard:
                        w_exp.append((t, pt_guard[kc % 2]))
                    p.add("scalar", lambda e, kc=kc, sb_bank=sb_bank: e.activation(
                        pt[:, (kc % 2) * 512:(kc % 2 + 1) * 512], sb_bank[:], AF.Exp),
                        waits=w_exp)
                    exp_done = s.n
                    w_pv = [(s, exp_done)]
                    if kc == 0:
                        w_pv.append((v, v.n))
                    p.add("tensor", lambda e, kc=kc, bi=bi: e.matmul(
                        P[2][:], vv[:, bi * 2048 + kc * 128: bi * 2048 + kc * 128 + 128],
                        pt[:, (kc % 2) * 512:(kc % 2 + 1) * 512],
                        start=(kc == 0), stop=(kc == 15)),
                        waits=w_pv, inc=t)
                    p.add("tensor", lambda e, kc=kc: e.matmul(
                        P[3][0:1, :], onesc[:], pt[:, (kc % 2) * 512:(kc % 2 + 1) * 512],
                        start=(kc == 0), stop=(kc == 15)),
                        inc=t)
                    pt_guard[kc % 2] = t.n
                pv_done = t.n
                p.add("vector", lambda e: e.reciprocal(rsum[:], P[3][0:1, :]),
                      waits=[(t, pv_done)])
                p.add("vector", lambda e: e.tensor_copy(rsumb[:], rsum[:]))
                rs_done = v.n
                p.add("tensor", lambda e: e.matmul(
                    P[4][:], onesr[:], rsumb[:], start=True, stop=True),
                    waits=[(v, rs_done)], inc=t)
                bc_done = t.n
                p.add("vector", lambda e: e.tensor_copy(rbc[:], P[4][:]),
                      waits=[(t, bc_done)])
                p.add("vector", lambda e, h=h, qg=qg: e.tensor_tensor(
                    atn[:, h * 2048 + qg * 512: h * 2048 + (qg + 1) * 512],
                    P[2][:], rbc[:], ALU.mult))
        for tt in range(16):
            row0 = tt * 128
            p.add("sync", lambda e, b=b, row0=row0: e.dma_start(
                out=h2t[:], in_=dr["h2f"][b, row0:row0 + 128, :]),
                waits=[(v, v.n)], inc=ld, inck=16)
            h2_ld = ld.n
            for dc in range(4):
                for ec in range(16):
                    lhsT = atn[:, ec * 2048 + tt * 128: ec * 2048 + tt * 128 + 128]
                    rhs = wo[:, ec * 2048 + dc * 512: ec * 2048 + (dc + 1) * 512]
                    p.add("tensor", lambda e, dc=dc, lhsT=lhsT, rhs=rhs, ec=ec: e.matmul(
                        P[4 + dc][:], lhsT, rhs, start=(ec == 0), stop=(ec == 15)),
                        waits=[(v, v.n), (s, s.n)] if ec == 0 else (),
                        inc=t if ec == 15 else None)
            wo_done = t.n
            for dc in range(4):
                p.add("vector", lambda e, dc=dc: e.tensor_tensor(
                    pre[:, dc * 512:(dc + 1) * 512], h2t[:, dc * 512:(dc + 1) * 512],
                    P[4 + dc][:], ALU.add), waits=[(t, wo_done), (ld, h2_ld)])
            _ln_stats(ctx, pre[:], 2048, sm, EPS_LN)
            p.add("vector", lambda e: e.tensor_scalar(
                h3[:], pre[:], sm["mean"][:, 0:1], sm["rstd"][:, 0:1],
                ALU.subtract, ALU.mult))
            p.add("vector", lambda e: e.tensor_copy(h3b[:], h3[:]))
            h3_done = v.n
            p.add("gpsimd", lambda e, b=b, row0=row0: e.dma_start(
                out=dr["h3f"][b, row0:row0 + 128, :], in_=h3[:]),
                waits=[(v, h3_done)], inc=g, inck=16)
            p.add("gpsimd", lambda e, b=b, row0=row0: e.dma_start(
                out=dr["h3bf"][b, row0:row0 + 128, :], in_=h3b[:]),
                waits=[(v, h3_done)], inc=g, inck=16)
            p.add("vector", lambda e: e.nop(), waits=[(g, g.n)])
    p.add("gpsimd", lambda e, gn=g.n: e.wait_ge(g.h, gn))



def ph_c2a(ctx):
    nc, p, dr = ctx.nc, ctx.p, ctx.dram
    ld, t, v, s, g = (ctx.sem(n) for n in ("ld", "t", "v", "s", "g"))
    h3T = ctx.sb("c2a_h3T", (128, 16 * 4096), BF16)
    w1T = ctx.sb("c2a_w1T", (128, 2 * 16 * 512), BF16)
    hrb = ctx.sb("c2a_hr", (128, 4 * 512), BF16)
    hsq = ctx.sb("c2a_hsq", (128, 512), F32)
    relu_t = ctx.sb("c2a_relu", (128, 512), BF16)
    sqf = ctx.sb("c2a_sqf", (128, 512), F32)
    acc = ctx.sb("c2a_acc", (128, 8 * 512), F32)
    ssr = ctx.sb("c2a_ssr", (1, 512), F32)
    rmsc = ctx.sb("c2a_rmsc", (128, 32), F32)
    identf = ctx.sb("c2a_idf", (128, 128), F32)
    onesc = ctx.sb("c2a_onesc", (128, 1), F32)
    P = [ctx.ps(f"c2a_ps{i}", (128, 512)) for i in range(6)]

    p.add("sync", lambda e: e.dma_start(out=identf[:], in_=dr["c_identf"][:, :]), inc=ld, inck=16)
    p.add("sync", lambda e: e.dma_start(out=onesc[:], in_=dr["c_onesf"][:, 0:1],
                                        allow_slow_non_contiguous=True), inc=ld, inck=16)
    for dc in range(16):
        for step in range(8):
            b, st = divmod(step, NST)
            src = dr["h3bf"][b, st * 512:(st + 1) * 512, dc * 128:(dc + 1) * 128]
            dst = h3T[:, dc * 4096 + step * 512: dc * 4096 + (step + 1) * 512]
            p.add("sync", lambda e, dst=dst, src=src: e.dma_start_transpose(dst, src),
                  inc=ld, inck=16)
    h3_ld = ld.n
    p.add("vector", lambda e: e.memset(acc[:], 0.0))
    acc_z = v.n

    hr_g = {}
    hr_st = {}
    for hb in range(16):
        wsl = w1T[:, (hb % 2) * 8192:(hb % 2 + 1) * 8192]
        w_l = [(t, t.n)] if hb >= 2 else []
        for dc in range(16):
            src = dr["w1bf"][hb * 512:(hb + 1) * 512, dc * 128:(dc + 1) * 128]
            p.add("sync", lambda e, wsl=wsl, dc=dc, src=src: e.dma_start_transpose(
                wsl[:, dc * 512:(dc + 1) * 512], src),
                waits=w_l if dc == 0 else (), inc=ld, inck=16)
        w_ld = ld.n
        for hc4 in range(4):
            hcg = hb * 4 + hc4
            for step in range(8):
                bank = P[step % 2]
                w_mm = [(ld, max(w_ld, h3_ld))]
                if (step % 2) in hr_g:
                    w_mm.append((s, hr_g[step % 2]))
                for dc in range(16):
                    lhsT = wsl[:, dc * 512 + hc4 * 128: dc * 512 + hc4 * 128 + 128]
                    rhs = h3T[:, dc * 4096 + step * 512: dc * 4096 + (step + 1) * 512]
                    p.add("tensor", lambda e, bank=bank, lhsT=lhsT, rhs=rhs, dc=dc: e.matmul(
                        bank[:], lhsT, rhs, start=(dc == 0), stop=(dc == 15)),
                        waits=w_mm if dc == 0 else (), inc=t if dc == 15 else None)
                mm_done = t.n
                bi4 = (hcg * 8 + step) % 4
                p.add("scalar", lambda e, bank=bank: e.activation(
                    hsq[:].bitcast(F32) if False else relu_t[:], bank[:], AF.Relu),
                    waits=[(t, mm_done)])
                hr_g[step % 2] = s.n
                relu_done = s.n
                w_sq = [(s, relu_done), (v, acc_z)]
                p.add("vector", lambda e: e.tensor_tensor(
                    sqf[:], relu_t[:], relu_t[:], ALU.mult), waits=w_sq)
                w_hr = [(g, hr_st[bi4])] if bi4 in hr_st else []
                p.add("vector", lambda e, bi4=bi4: e.tensor_copy(
                    hrb[:, bi4 * 512:(bi4 + 1) * 512], sqf[:]), waits=w_hr)
                sq_done = v.n
                dst = dr["hrs"][hcg, :, step * 512:(step + 1) * 512]
                p.add("gpsimd", lambda e, dst=dst, bi4=bi4: e.dma_start(
                    out=dst, in_=hrb[:, bi4 * 512:(bi4 + 1) * 512]),
                    waits=[(v, sq_done)], inc=g, inck=16)
                hr_st[bi4] = g.n
                p.add("vector", lambda e: e.tensor_tensor(
                    hsq[:], sqf[:], sqf[:], ALU.mult))
                p.add("vector", lambda e, step=step: e.tensor_tensor(
                    acc[:, step * 512:(step + 1) * 512],
                    acc[:, step * 512:(step + 1) * 512], hsq[:], ALU.add))
    acc_done = v.n
    for step in range(8):
        p.add("tensor", lambda e, step=step: e.matmul(
            P[2][0:1, :], onesc[:],
            acc[:, step * 512:(step + 1) * 512], start=True, stop=True),
            waits=[(v, max(acc_done, v.n))], inc=t)
        cs_done = t.n
        p.add("vector", lambda e: e.tensor_copy(ssr[:], P[2][0:1, :]),
              waits=[(t, cs_done)])
        ssr_done = v.n
        for q4 in range(4):
            p.add("tensor", lambda e, q4=q4: e.transpose(
                P[3][:, 0:1], ssr[0:1, q4 * 128:(q4 + 1) * 128], identf[0:1, 0:1]),
                waits=[(v, v.n)], inc=t)
            p.add("vector", lambda e, step=step, q4=q4: e.tensor_copy(
                rmsc[:, step * 4 + q4: step * 4 + q4 + 1], P[3][:, 0:1]),
                waits=[(t, t.n)])
    p.add("vector", lambda e: e.tensor_scalar(
        rmsc[:], rmsc[:], 1.0 / 8192.0, EPS_RMS, ALU.mult, ALU.add))
    p.add("scalar", lambda e: e.activation(rmsc[:], rmsc[:], AF.Ln), waits=[(v, v.n)])
    p.add("vector", lambda e: e.tensor_scalar_mul(rmsc[:], rmsc[:], -0.5), waits=[(s, s.n)])
    p.add("scalar", lambda e: e.activation(rmsc[:], rmsc[:], AF.Exp), waits=[(v, v.n)])
    p.add("gpsimd", lambda e: e.dma_start(out=dr["rms_s"][:, :], in_=rmsc[:]),
          waits=[(s, s.n)], inc=g, inck=16)
    p.add("gpsimd", lambda e, gn=g.n: e.wait_ge(g.h, gn))



def ph_c2b(ctx):
    nc, p, dr = ctx.nc, ctx.p, ctx.dram
    ld, t, v, s, g = (ctx.sem(n) for n in ("ld", "t", "v", "s", "g"))
    w2T = ctx.sb("c2b_w2T", (128, 128 * 512), BF16)
    hrt = ctx.sb("c2b_hrt", (128, 2 * 8192), BF16)
    rmsc = ctx.sb("c2b_rmsc", (128, 32), F32)
    yb = ctx.sb("c2b_y", (128, 4 * 512), BF16)
    P = [ctx.ps(f"c2b_ps{i}", (128, 512)) for i in range(4)]

    p.add("sync", lambda e: e.dma_start(out=rmsc[:], in_=dr["rms_s"][:, :]), inc=ld, inck=16)
    y_st = {}
    for half in range(2):
        d0 = half * 1024
        w_l = [(t, t.n)] if half > 0 else []
        for hc in range(64):
            for dc2 in range(2):
                src = dr["w2bf"][d0 + dc2 * 512: d0 + (dc2 + 1) * 512,
                                 hc * 128:(hc + 1) * 128]
                dst = w2T[:, (hc * 2 + dc2) * 512:(hc * 2 + dc2 + 1) * 512]
                p.add("sync", lambda e, dst=dst, src=src: e.dma_start_transpose(dst, src),
                      waits=w_l if (hc == 0 and dc2 == 0) else (), inc=ld, inck=16)
        w_ld = ld.n
        for tt in range(32):
            hb = hrt[:, (tt % 2) * 8192:(tt % 2 + 1) * 8192]
            w_h = [(ld, w_ld)]
            if tt >= 2:
                w_h.append((t, t.n))
            for hq in range(4):
                src = dr["hrs"][hq * 16:(hq + 1) * 16, :, tt * 128:(tt + 1) * 128]
                dst = hb[:, hq * 2048:(hq + 1) * 2048]
                p.add("sync", lambda e, dst=dst, src=src: e.dma_start(
                    out=dst.rearrange("p (hc i) -> p hc i", i=128),
                    in_=src.transpose([1, 0, 2])),
                    waits=w_h if hq == 0 else (), inc=ld, inck=16)
            hr_ld = ld.n
            for dc2 in range(2):
                bank = P[dc2 % 2]
                w_mm = [(ld, hr_ld)]
                if (dc2 % 2) in y_st:
                    w_mm.append((v, y_st[dc2 % 2][0]))
                for hc in range(64):
                    lhsT = hb[:, hc * 128:(hc + 1) * 128]
                    rhs = w2T[:, (hc * 2 + dc2) * 512:(hc * 2 + dc2 + 1) * 512]
                    p.add("tensor", lambda e, bank=bank, lhsT=lhsT, rhs=rhs, hc=hc: e.matmul(
                        bank[:], lhsT, rhs, start=(hc == 0), stop=(hc == 63)),
                        waits=w_mm if hc == 0 else (), inc=t if hc == 63 else None)
                mm_done = t.n
                w_e = [(t, mm_done)]
                if (dc2 % 2) in y_st:
                    w_e.append((g, y_st[dc2 % 2][1]))
                p.add("vector", lambda e, bank=bank, dc2=dc2, tt=tt: e.tensor_scalar(
                    yb[:, dc2 * 512:(dc2 + 1) * 512], bank[:],
                    rmsc[:, tt:tt + 1], None, ALU.mult), waits=w_e)
                ev_done = v.n
                dst = dr["ysb"][tt * 128:(tt + 1) * 128, d0 + dc2 * 512: d0 + (dc2 + 1) * 512]
                p.add("gpsimd", lambda e, dst=dst, dc2=dc2: e.dma_start(
                    out=dst, in_=yb[:, dc2 * 512:(dc2 + 1) * 512]),
                    waits=[(v, ev_done)], inc=g, inck=16)
                y_st[dc2 % 2] = (ev_done, g.n)
    p.add("gpsimd", lambda e, gn=g.n: e.wait_ge(g.h, gn))



def ph_c3(ctx):
    nc, p, dr = ctx.nc, ctx.p, ctx.dram
    ld, t, v, s, g = (ctx.sem(n) for n in ("ld", "t", "v", "s", "g"))
    yf = ctx.sb("c3_y", (128, 2 * 2048), BF16)
    h3t = ctx.sb("c3_h3", (128, 2 * 2048), F32)
    pre = ctx.sb("c3_pre", (128, 2048), F32)
    dump = ctx.sb("c3_dump", (128, 2048), F32)
    outt = ctx.sb("c3_out", (128, 2048), F32)
    sm = {k: ctx.sb("c3s_" + k, (128, 1), F32) for k in
          ("sumb", "negm", "ssq", "vpe", "rstd", "mean")}
    sm["dump"] = dump

    out_st = 0
    for tt in range(32):
        b, row0 = divmod(tt * 128, S)
        bi = tt % 2
        w_l = [(v, v.n)] if tt >= 2 else []
        p.add("sync", lambda e, tt=tt, bi=bi: e.dma_start(
            out=yf[:, bi * 2048:(bi + 1) * 2048],
            in_=dr["ysb"][tt * 128:(tt + 1) * 128, :]), waits=w_l, inc=ld, inck=16)
        p.add("sync", lambda e, b=b, row0=row0, bi=bi: e.dma_start(
            out=h3t[:, bi * 2048:(bi + 1) * 2048],
            in_=dr["h3f"][b, row0:row0 + 128, :]), inc=ld, inck=16)
        t_ld = ld.n
        p.add("vector", lambda e, bi=bi: e.tensor_copy(
            pre[:], yf[:, bi * 2048:(bi + 1) * 2048]), waits=[(ld, t_ld)])
        p.add("vector", lambda e, bi=bi: e.tensor_tensor(
            pre[:], pre[:], h3t[:, bi * 2048:(bi + 1) * 2048], ALU.add))
        _ln_stats(ctx, pre[:], 2048, sm, EPS_LN)
        p.add("vector", lambda e: e.tensor_scalar(
            outt[:], pre[:], sm["mean"][:, 0:1], sm["rstd"][:, 0:1],
            ALU.subtract, ALU.mult))
        o_done = v.n
        p.add("gpsimd", lambda e, b=b, row0=row0: e.dma_start(
            out=dr["out"][b, row0:row0 + 128, :], in_=outt[:]),
            waits=[(v, o_done)], inc=g, inck=16)
        p.add("vector", lambda e: e.nop(), waits=[(g, g.n)])
    p.add("gpsimd", lambda e, gn=g.n: e.wait_ge(g.h, gn))


def build(group_phases, for_run=True):
    nc = bass.Bass("TRN2", target_bir_lowering=False)
    es = ExitStack()
    ctx = Ctx(nc, es, group_phases)
    prog = ctx.p
    with es:
        in_names, out_names = declare_dram(ctx, group_phases)
        block = es.enter_context(nc.Block())
        first = True
        for ph in group_phases:
            if not first:
                prog.barrier(ctx.sem("bar"))
            first = False
            with ExitStack() as phstack:
                ctx.phase_stack = phstack
                globals()["ph_" + ph](ctx)
                ctx.phase_stack = None
        prog.emit(block)
    return nc, in_names, out_names



N_CORES = 8
GROUPS = [list(ALL_PHASES)]


def kernel(**inputs):
    consts = host_consts()
    xfull = np.ascontiguousarray(inputs["L0_attn_x"])
    per_core = []
    for c in range(N_CORES):
        d = {"x": np.ascontiguousarray(xfull[2 * c:2 * c + 2])}
        for k, v in inputs.items():
            if k != "L0_attn_x":
                d[k] = np.ascontiguousarray(v)
        d.update(consts)
        per_core.append(d)
    outs = None
    for grp in GROUPS:
        nc, in_names, out_names = build(grp)
        in_maps = [{n: per_core[c][n] for n in in_names} for c in range(N_CORES)]
        res = run_bass_kernel_spmd(nc, in_maps, core_ids=list(range(N_CORES)))
        for c in range(N_CORES):
            for n in out_names:
                per_core[c][n] = res.results[c][n]
        if "out" in out_names:
            outs = [res.results[c]["out"] for c in range(N_CORES)]
    return np.concatenate(outs, axis=0)


# version 09
# speedup vs baseline: 1.0740x; 1.0632x over previous
import numpy as np
import ml_dtypes
import concourse.bass as bass
import concourse.mybir as mybir
from concourse.bass_utils import run_bass_kernel_spmd
from contextlib import ExitStack

F32 = mybir.dt.float32
BF16 = mybir.dt.bfloat16
AF = mybir.ActivationFunctionType
ALU = mybir.AluOpType
AX = mybir.AxisListType

B_LOC, S, D, H, HD = 2, 2048, 2048, 16, 128
TOK = B_LOC * S
ST = 512
NST = S // ST
NDC = D // 128
EPS_LN = 1e-5
EPS_RMS = float(np.finfo(np.float32).eps)
MAGIC = 12582912.0

ENGINES = ("sync", "tensor", "vector", "scalar", "gpsimd")

ALL_PHASES = ("prep", "a1", "a2", "b1", "b2", "c2a", "c2b", "c3")

bfnp = ml_dtypes.bfloat16


def host_consts():
    i = np.arange(128)
    ang = 2.0 * np.pi * np.outer(i, i) / 128.0
    onescol = np.zeros((128, 16, 16), np.float32)
    for ec in range(16):
        onescol[:, ec, ec] = 1.0
    hsel = np.zeros((16, 16, 128), np.float32)
    for h in range(16):
        hsel[h, h, :] = 1.0
    c = {
        "c_fre": np.cos(ang),
        "c_fim": -np.sin(ang),
        "c_ifre": np.cos(ang) / 128.0,
        "c_ifim": -np.sin(ang) / 128.0,
        "c_ident": np.eye(128, dtype=np.float32),
        "c_onescol": onescol.reshape(128, 256),
        "c_hsel": hsel.reshape(16, 2048),
        "c_ones": np.ones((128, 512), np.float32),
        "c_onesr": np.ones((1, 128), np.float32),
    }
    out = {}
    for k, v in c.items():
        out[k] = v.astype(bfnp)
    out["c_identf"] = np.eye(128, dtype=np.float32)
    out["c_onesf"] = np.ones((128, 512), np.float32)
    out["c_onesrf"] = np.ones((1, 128), np.float32)
    return out


CONST_SPECS = {k: (v.shape, BF16 if v.dtype == bfnp else F32)
               for k, v in host_consts().items()}

SCRATCH = {
    "xbf":    ((B_LOC, S, D), BF16, "prep", ("a1",)),
    "wqbf":   ((D, D), BF16, "prep", ("a1",)),
    "wkvbf":  ((D, D), BF16, "prep", ("a1",)),
    "wobf":   ((D, D), BF16, "prep", ("a2",)),
    "wqkvbf": ((3 * D, D), BF16, "prep", ("b1",)),
    "wo1bf":  ((D, D), BF16, "prep", ("b2",)),
    "wfcbf":  ((8192, D), BF16, "prep", ("prep",)),
    "wprojbf": ((D, 8192), BF16, "prep", ("prep",)),
    "w1bf":   ((8192, D), BF16, "prep", ("c2a",)),
    "w2bf":   ((D, 8192), BF16, "prep", ("c2b",)),
    "ffnw_s": ((128, 16 * 208), BF16, "prep", ("a2",)),
    "dgn_s":  ((72, D), BF16, "prep", ("a2",)),
    "k1_s":   ((1, 136), BF16, "prep", ("a2",)),
    "gate_s": ((2, 8), F32, "prep", ("a2",)),
    "sn_s":   ((8, 8), F32, "prep", ("prep",)),
    "cc_s":   ((8, 8), F32, "prep", ("prep",)),
    "mn2_s":  ((8, 1), F32, "prep", ("prep",)),
    "vhT":    ((B_LOC, NST, D, ST), BF16, "a1", ("a2",)),
    "h2f":    ((B_LOC, S, D), F32, "a2", ("b2",)),
    "h2bf":   ((B_LOC, S, D), BF16, "a2", ("b1",)),
    "qkvbf":  ((B_LOC, S, 3 * D), BF16, "b1", ("b2",)),
    "h3f":    ((B_LOC, S, D), F32, "b2", ("c3",)),
    "h3bf":   ((B_LOC, S, D), BF16, "b2", ("c2a",)),
    "hrs":    ((64, 128, TOK), BF16, "c2a", ("c2b",)),
    "rms_s":  ((128, 32), F32, "c2a", ("c2b",)),
    "ysb":    ((TOK, D), BF16, "c2b", ("c3",)),
}

INPUT_SPECS = {
    "x": ((B_LOC, S, D), F32),
    "L0_attn_W_q": ((D, D), F32),
    "L0_attn_W_kv": ((D, D), F32),
    "L0_attn_W_o": ((D, D), F32),
    "L0_ffn_select_weight": ((8, D, 8), F32),
    "L0_ffn_select_mean_diff": ((8, D), F32),
    "L0_ffn_imp_center": ((8,), F32),
    "L0_ffn_imp_slope": ((8,), F32),
    "L0_ffn_lora_update": ((8, D, 8), F32),
    "L0_ffn_lora_degen": ((8, D, 8), F32),
    "L0_ffn_bias_w": ((8, D), F32),
    "L0_ffn_debias_w": ((8, D), F32),
    "L1_attn_W_qkv": ((3 * D, D), F32),
    "L1_attn_W_o": ((D, D), F32),
    "L1_ffn_W_fc": ((8192, D), F32),
    "L1_ffn_W_proj": ((D, 8192), F32),
}

PHASE_INPUTS = {
    "prep": ("x", "L0_attn_W_q", "L0_attn_W_kv", "L0_attn_W_o",
             "L0_ffn_select_weight", "L0_ffn_select_mean_diff",
             "L0_ffn_imp_center", "L0_ffn_imp_slope", "L0_ffn_lora_update",
             "L0_ffn_lora_degen", "L0_ffn_bias_w", "L0_ffn_debias_w",
             "L1_attn_W_qkv", "L1_attn_W_o", "L1_ffn_W_fc", "L1_ffn_W_proj"),
    "a1": (), "a2": ("x",), "b1": (), "b2": (), "c2a": (), "c2b": (), "c3": (),
}
PHASE_CONSTS = {
    "prep": ("c_ident", "c_identf"),
    "a1": ("c_fre", "c_fim", "c_ifre", "c_ifim", "c_onescol", "c_hsel", "c_onesf"),
    "a2": ("c_onesr", "c_onesrf", "c_identf", "c_ident", "c_onesf"),
    "b1": (),
    "b2": ("c_ones", "c_onesr"),

    "c2a": ("c_onesf", "c_identf"),
    "c2b": (),
    "c3": (),
}


class Sem:
    def __init__(self, nc, es, name):
        self.h = es.enter_context(nc.semaphore(name))
        self.n = 0


class Prog:

    def __init__(self, ctx=None):
        self.ops = []
        self.ctx = ctx
        self.last = {}

    def add(self, eng, fn, waits=(), inc=None, inck=1, chain=None):
        if chain is None:
            chain = eng in ("vector", "scalar")
        waits = list(waits)
        if eng in ("vector", "scalar"):
            dflt = self.ctx.sem("v" if eng == "vector" else "s")
            if chain and eng in self.last:
                waits.append(self.last[eng])
            if inc is None:
                inc = dflt
                inck = 1
        self.ops.append((eng, fn, tuple(waits), inc, inck))
        if inc is not None:
            inc.n += inck
            if eng in ("vector", "scalar"):
                self.last[eng] = (inc, inc.n)

    def barrier(self, bar):
        target = bar.n + 5
        cur = [(self.ctx.sem(n), self.ctx.sem(n).n)
               for n in ("ld", "t", "v", "s", "g")]
        for g in ENGINES:
            self.ops.append((g, lambda e: e.nop(), tuple(cur), bar, 1))
            bar.n += 1
            self.ops.append((g, None, ((bar, target),), None, 1))
        assert bar.n == target
        self.last = {}

    def check(self):
        import collections
        qs = {g: collections.deque() for g in ENGINES}
        for op in self.ops:
            qs[op[0]].append(op)
        counts = collections.defaultdict(int)
        progress = True
        while progress:
            progress = False
            for g in ENGINES:
                while qs[g]:
                    (eng, fn, waits, inc, inck) = qs[g][0]
                    if any(counts[id(sem)] < val for (sem, val) in waits):
                        break
                    if inc is not None:
                        counts[id(inc)] += inck
                    qs[g].popleft()
                    progress = True
        stuck = {g: len(qs[g]) for g in ENGINES if qs[g]}
        if stuck:
            msgs = []
            for g in ENGINES:
                if not qs[g]:
                    continue
                (eng, fn, waits, inc, inck) = qs[g][0]
                bad = [(self._semname(sem), val, counts[id(sem)])
                       for (sem, val) in waits if counts[id(sem)] < val]
                msgs.append(f"{g}: {len(qs[g])} ops stuck; head waits {bad}")
            raise RuntimeError("DEADLOCK in schedule:\n" + "\n".join(msgs))

    def _semname(self, sem):
        for k, v in self.ctx.sems.items():
            if v is sem:
                return k
        return "?"

    def emit(self, block):
        self.check()
        for engname in ENGINES:
            items = [(f, w, i, k) for (g, f, w, i, k) in self.ops if g == engname]
            if not items:
                continue

            def mk(items):
                def run(e):
                    for (f, w, inc, k) in items:
                        for (sem, val) in w:
                            e.wait_ge(sem.h, val)
                        if f is None:
                            continue
                        ins = f(e)
                        if inc is not None:
                            assert ins is not None
                            ins.then_inc(inc.h, k)
                return run
            getattr(block, engname)(mk(items))


class Ctx:
    def __init__(self, nc, es, phases):
        self.nc = nc
        self.es = es
        self.p = Prog(self)
        self.phases = phases
        self.dram = {}
        self.sems = {}
        for name in ("ld", "ld2", "ld3", "ld4", "ld5", "st4", "st5", "t", "v", "s", "g", "bar"):
            self.sems[name] = Sem(nc, es, "sem_" + name)

    def sem(self, n):
        return self.sems[n]

    def sb(self, name, shape, dtype, stack=None):
        st = stack or getattr(self, "phase_stack", None) or self.es
        return st.enter_context(self.nc.sbuf_tensor(name, list(shape), dtype))

    def ps(self, name, shape, dtype=F32, stack=None):
        st = stack or getattr(self, "phase_stack", None) or self.es
        return st.enter_context(self.nc.psum_tensor(name, list(shape), dtype))


def declare_dram(ctx, group_phases):
    nc = ctx.nc
    phs = set(group_phases)
    needed_inputs = set()
    needed_consts = set()
    for ph in phs:
        needed_inputs.update(PHASE_INPUTS[ph])
        needed_consts.update(PHASE_CONSTS[ph])
    for name in sorted(needed_inputs):
        shape, dt = INPUT_SPECS[name]
        ctx.dram[name] = nc.dram_tensor(name, list(shape), dt, kind="ExternalInput").ap()
    for name in sorted(needed_consts):
        shape, dt = CONST_SPECS[name]
        ctx.dram[name] = nc.dram_tensor(name, list(shape), dt, kind="ExternalInput").ap()
    ext_in, ext_out = [], []
    for name, (shape, dt, producer, consumers) in SCRATCH.items():
        prod_here = producer in phs
        cons_here = any(c in phs for c in consumers)
        cons_later = any(c not in phs for c in consumers)
        if not prod_here and not cons_here:
            continue
        if prod_here and not cons_later:
            kind = "Internal"
        elif prod_here:
            kind = "ExternalOutput"
            ext_out.append(name)
        else:
            kind = "ExternalInput"
            ext_in.append(name)
        ctx.dram[name] = nc.dram_tensor(name, list(shape), dt, kind=kind).ap()
    if "c3" in phs:
        ctx.dram["out"] = nc.dram_tensor("out", [B_LOC, S, D], F32, kind="ExternalOutput").ap()
        ext_out.append("out")
    import os as _os
    if _os.environ.get("B2_DBG") and "b2" in phs:
        ctx.dram["dbg"] = nc.dram_tensor("dbg", [128, 16 * 2048], BF16, kind="ExternalOutput").ap()
        ext_out.append("dbg")
    return sorted(needed_inputs | needed_consts) + ext_in, ext_out



def ph_prep(ctx):
    nc, p, dr = ctx.nc, ctx.p, ctx.dram
    g = ctx.sem("g")
    ld = ctx.sem("ld")
    v = ctx.sem("v")
    t = ctx.sem("t")
    s = ctx.sem("s")

    ld2 = ctx.sem("ld2")
    tern_casts = [
        (dr["wfcbf"][0:4096], dr["L1_ffn_W_fc"][0:4096]),
        (dr["wfcbf"][4096:8192], dr["L1_ffn_W_fc"][4096:8192]),
        (dr["wprojbf"], dr["L1_ffn_W_proj"]),
    ]
    for (o, i) in tern_casts:
        p.add("gpsimd", lambda e, o=o, i=i: e.dma_start(out=o, in_=i), inc=ld2, inck=16)
    tern_cast_done = ld2.n
    casts = [
        (dr["xbf"][0], dr["x"][0]), (dr["xbf"][1], dr["x"][1]),
        (dr["wqbf"], dr["L0_attn_W_q"]), (dr["wkvbf"], dr["L0_attn_W_kv"]),
        (dr["wobf"], dr["L0_attn_W_o"]), (dr["wo1bf"], dr["L1_attn_W_o"]),
        (dr["wqkvbf"][0:3072], dr["L1_attn_W_qkv"][0:3072]),
        (dr["wqkvbf"][3072:6144], dr["L1_attn_W_qkv"][3072:6144]),
        (dr["dgn_s"][64:72], dr["L0_ffn_bias_w"]),
    ]
    for (o, i) in casts:
        p.add("gpsimd", lambda e, o=o, i=i: e.dma_start(out=o, in_=i), inc=g, inck=16)
    cast_done = g.n

    tern = ctx.sb("tern_in", (128, 2 * 8192), BF16)
    tw1 = ctx.sb("tw1", (128, 2 * 8192), BF16)
    EB = {}
    for en in ("v", "g"):
        EB[en] = {
            "tred": ctx.sb("tred_" + en, (128, 128), F32),
            "tsc": ctx.sb("tsc_" + en, (128, 128), BF16),
            "trec": ctx.sb("trec_" + en, (128, 128), F32),
            "tscf": ctx.sb("tscf_" + en, (128, 128), F32),
            "trat": ctx.sb("trat_" + en, (128, 8192), BF16),
            "ternf": ctx.sb("ternf_" + en, (128, 8192), F32),
        }

    jobs = []
    wfc3 = dr["wfcbf"].rearrange("(n p) d -> n p d", p=128)
    w1b3 = dr["w1bf"].rearrange("(n p) d -> n p d", p=128)
    for i in range(64):
        jobs.append((wfc3[i], w1b3[i], 2048))
    wpr3 = dr["wprojbf"].rearrange("(n p) d -> n p d", p=128)
    w2b3 = dr["w2bf"].rearrange("(n p) d -> n p d", p=128)
    for i in range(16):
        jobs.append((wpr3[i], w2b3[i], 8192))

    gch = [g.n]

    def eng_op(en, fn, extra=(), consume=None):
        if en == "v":
            p.add("vector", fn, waits=list(extra))
            return v.n
        w = [(g, gch[0])] + list(extra)
        p.add("gpsimd", fn, waits=w, inc=g)
        gch[0] = g.n
        return g.n

    in_guard = {"v": 0, "g": 0}
    st_guard = {"v": 0, "g": 0}
    red_guard = {}
    scal_guard = {}
    for i, (src, dst, F) in enumerate(jobs):
        en = "v" if i % 2 == 0 else "g"
        esem = v if en == "v" else g
        lsem = ctx.sem("ld4" if en == "v" else "ld5")
        bi = i % 2
        B = EB[en]
        ti = tern[:, bi * 8192: bi * 8192 + F]
        to = tw1[:, bi * 8192: bi * 8192 + F]
        G = F // 64
        w_in = [(ld2, tern_cast_done), (esem, in_guard[en]),
                (v, red_guard.get(en, 0))]
        p.add("sync", lambda e, ti=ti, src=src: e.dma_start(out=ti, in_=src),
              waits=w_in, inc=lsem, inck=16)
        tile_ld = lsem.n
        ti3 = ti.rearrange("p (gg o) -> p gg o", o=64)
        tf3 = B["ternf"][:, 0:F].rearrange("p (gg o) -> p gg o", o=64)
        p.add("vector", lambda e, ti3=ti3, B=B, G=G: e.tensor_reduce(
            B["tred"][:, 0:G].unsqueeze(2), ti3, AX.X, ALU.add,
            apply_absolute_value=True), waits=[(lsem, tile_ld)])
        red_guard[en] = v.n
        p.add("vector", lambda e, B=B, G=G: e.tensor_scalar(
            B["tsc"][:, 0:G], B["tred"][:, 0:G], 1.0 / 64.0, 1e-8, ALU.mult, ALU.max),
            waits=[(esem, scal_guard.get(en, 0))])
        p.add("vector", lambda e, B=B, G=G: e.reciprocal(
            B["trec"][:, 0:G], B["tsc"][:, 0:G]))
        p.add("vector", lambda e, B=B, G=G: e.tensor_copy(
            B["tscf"][:, 0:G], B["tsc"][:, 0:G]))
        scal_done = v.n
        cp = eng_op(en, lambda e, B=B, ti=ti, F=F: e.tensor_copy(
            B["ternf"][:, 0:F], ti), extra=[(lsem, tile_ld)])
        in_guard[en] = cp
        eng_op(en, lambda e, tf3=tf3, B=B, G=G: e.tensor_tensor(
            tf3, tf3, B["trec"][:, 0:G].unsqueeze(2).broadcast_to([128, G, 64]),
            ALU.mult), extra=[(v, scal_done)])
        eng_op(en, lambda e, B=B, F=F: e.tensor_copy(B["trat"][:, 0:F], B["ternf"][:, 0:F]))
        eng_op(en, lambda e, B=B, F=F: e.tensor_scalar(
            B["ternf"][:, 0:F], B["trat"][:, 0:F], MAGIC, MAGIC, ALU.add, ALU.subtract))
        eng_op(en, lambda e, B=B, F=F: e.tensor_scalar(
            B["ternf"][:, 0:F], B["ternf"][:, 0:F], 1.0, -1.0, ALU.min, ALU.max))
        scal_guard[en] = eng_op(en, lambda e, tf3=tf3, B=B, G=G: e.tensor_tensor(
            tf3, tf3, B["tscf"][:, 0:G].unsqueeze(2).broadcast_to([128, G, 64]),
            ALU.mult))
        out_done = eng_op(en, lambda e, to=to, B=B, F=F: e.tensor_copy(
            to, B["ternf"][:, 0:F]),
            extra=[(ctx.sem("st4" if en == "v" else "st5"), st_guard[en])])
        st_eng = "scalar" if en == "v" else "gpsimd"
        ssem = ctx.sem("st4" if en == "v" else "st5")
        p.add(st_eng, lambda e, to=to, dst=dst: e.dma_start(out=dst, in_=to),
              waits=[(esem, out_done)], inc=ssem, inck=16)
        st_guard[en] = ssem.n
    tern_v_end = v.n

    mu_f = ctx.sb("mu_f", (8, D), F32)
    deb_f = ctx.sb("deb_f", (8, D), F32)
    musq = ctx.sb("musq", (8, D), F32)
    mn2 = ctx.sb("mn2", (8, 1), F32)
    slp0 = ctx.sb("slp0", (1, 8), F32)
    slpc = ctx.sb("slpc", (1, 8), F32)
    p.add("sync", lambda e: e.dma_start(out=mu_f[:], in_=dr["L0_ffn_select_mean_diff"][:, :]), inc=ld, inck=16)
    p.add("sync", lambda e: e.dma_start(out=deb_f[:], in_=dr["L0_ffn_debias_w"][:, :]), inc=ld, inck=16)
    p.add("sync", lambda e: e.dma_start(out=slp0[:], in_=dr["L0_ffn_imp_slope"].unsqueeze(0)), inc=ld, inck=16)
    p.add("sync", lambda e: e.dma_start(out=slpc[:], in_=dr["L0_ffn_imp_center"].unsqueeze(0)), inc=ld, inck=16)
    small_ld = ld.n
    p.add("vector", lambda e: e.tensor_tensor(musq[:], mu_f[:], mu_f[:], ALU.mult),
          waits=[(ld, small_ld)], inc=v)
    p.add("vector", lambda e: e.tensor_reduce(mn2[:], musq[:], AX.X, ALU.add), inc=v)
    p.add("vector", lambda e: e.tensor_tensor(slpc[:], slp0[:], slpc[:], ALU.mult), inc=v)
    p.add("gpsimd", lambda e: e.dma_start(out=dr["gate_s"][0:1, :], in_=slp0[:]),
          waits=[(v, v.n)], inc=g, inck=16)
    p.add("gpsimd", lambda e: e.dma_start(out=dr["gate_s"][1:2, :], in_=slpc[:]),
          waits=[(v, v.n)], inc=g, inck=16)
    p.add("gpsimd", lambda e: e.dma_start(out=dr["mn2_s"][:, :], in_=mn2[:]),
          waits=[(v, v.n)], inc=g, inck=16)
    small_v = v.n

    ffnw = ctx.sb("ffnw", (128, 16 * 208), BF16)
    muhi = ctx.sb("muhi", (128, 16 * 8), BF16)
    debhi = ctx.sb("debhi", (128, 16 * 8), BF16)
    ident_f = ctx.sb("ident_f", (128, 128), F32)
    ident_b = ctx.sb("ident_b", (128, 128), BF16)
    p.add("sync", lambda e: e.dma_start(out=ident_f[:], in_=dr["c_identf"][:, :]), inc=ld, inck=16)
    p.add("sync", lambda e: e.dma_start(out=ident_b[:], in_=dr["c_ident"][:, :]), inc=ld, inck=16)
    ident_ld = ld.n
    ps_a = ctx.ps("pp_a", (128, 512))
    ps_b = ctx.ps("pp_b", (128, 512))
    mulo_t = ctx.sb("mulo_t", (128, 8), BF16)
    muf32 = ctx.sb("muf32", (128, 8), F32)
    hif32 = ctx.sb("hif32", (128, 8), F32)
    for which, srcf, hidst in (("mu", mu_f, muhi), ("deb", deb_f, debhi)):
        for dc in range(16):
            pst = ps_a[:, 0:8] if dc % 2 == 0 else ps_b[:, 0:8]
            p.add("tensor", lambda e, srcf=srcf, dc=dc, pst=pst: e.transpose(
                pst, srcf[:, dc * 128:(dc + 1) * 128], ident_f[0:8, 0:8]),
                waits=[(ld, ident_ld), (v, v.n)], inc=t)
            p.add("vector", lambda e, pst=pst: e.tensor_copy(muf32[:], pst),
                  waits=[(t, t.n)], inc=v)
            p.add("vector", lambda e, hidst=hidst, dc=dc: e.tensor_copy(
                hidst[:, dc * 8:(dc + 1) * 8], muf32[:]), inc=v)
            if which == "mu":
                p.add("vector", lambda e, dc=dc: e.tensor_copy(
                    hif32[:], muhi[:, dc * 8:(dc + 1) * 8]), inc=v)
                p.add("vector", lambda e: e.tensor_tensor(
                    mulo_t[:], muf32[:], hif32[:], ALU.subtract), inc=v)
                p.add("vector", lambda e, dc=dc: e.tensor_scalar_mul(
                    ffnw[:, dc * 208 + 64: dc * 208 + 72], muhi[:, dc * 8:(dc + 1) * 8], -2.0), inc=v)
                p.add("vector", lambda e, dc=dc: e.tensor_scalar_mul(
                    ffnw[:, dc * 208 + 200: dc * 208 + 208], mulo_t[:], -2.0), inc=v)
    mu_v = v.n

    wsl_t = ctx.sb("wsl_t", (128, 8), F32)
    for n in range(8):
        for dc in range(16):
            src = dr["L0_ffn_select_weight"][n, dc * 128:(dc + 1) * 128, :]
            p.add("sync", lambda e, src=src: e.dma_start(out=wsl_t[:], in_=src),
                  waits=[(v, v.n)], inc=ld, inck=16)
            p.add("vector", lambda e, dc=dc, n=n: e.tensor_copy(
                ffnw[:, dc * 208 + n * 8: dc * 208 + n * 8 + 8], wsl_t[:]),
                waits=[(ld, ld.n)], inc=v)
            p.add("vector", lambda e, dc=dc, n=n: e.tensor_copy(
                hif32[:], ffnw[:, dc * 208 + n * 8: dc * 208 + n * 8 + 8]), inc=v)
            p.add("vector", lambda e, dc=dc, n=n: e.tensor_tensor(
                ffnw[:, dc * 208 + 136 + n * 8: dc * 208 + 136 + n * 8 + 8], wsl_t[:],
                hif32[:], ALU.subtract), inc=v)
    for n in range(8):
        for dc in range(16):
            src = dr["L0_ffn_lora_update"][n, dc * 128:(dc + 1) * 128, :]
            p.add("sync", lambda e, src=src: e.dma_start(out=wsl_t[:], in_=src),
                  waits=[(v, v.n)], inc=ld, inck=16)
            p.add("vector", lambda e, dc=dc, n=n: e.tensor_copy(
                ffnw[:, dc * 208 + 72 + n * 8: dc * 208 + 72 + n * 8 + 8], wsl_t[:]),
                waits=[(ld, ld.n)], inc=v)
    ffnw_v = v.n
    p.add("gpsimd", lambda e: e.dma_start(out=dr["ffnw_s"][:, :], in_=ffnw[:]),
          waits=[(v, ffnw_v)], inc=g, inck=16)

    snneg = ctx.sb("snneg", (8, 8), F32)
    ccneg = ctx.sb("ccneg", (8, 8), F32)
    for dstbuf, colbase, vecbuf in ((snneg, 0, muhi), (ccneg, 72, debhi)):
        for n in range(8):
            for dc in range(16):
                lhsT = ffnw[:, dc * 208 + colbase + n * 8: dc * 208 + colbase + n * 8 + 8]
                rhs = vecbuf[:, dc * 8 + n: dc * 8 + n + 1]
                p.add("tensor", lambda e, lhsT=lhsT, rhs=rhs, dc=dc: e.matmul(
                    ps_a[0:8, 0:1], lhsT, rhs, start=(dc == 0), stop=(dc == 15)),
                    waits=[(v, ffnw_v)] if dc == 0 else (), inc=t if dc == 15 else None)
            p.add("vector", lambda e, dstbuf=dstbuf, n=n: e.tensor_scalar_mul(
                dstbuf[:, n:n + 1], ps_a[0:8, 0:1], -1.0), waits=[(t, t.n)], inc=v)
            p.add("tensor", lambda e: e.nop(), waits=[(v, v.n)])
    p.add("gpsimd", lambda e: e.dma_start(out=dr["sn_s"][:, :], in_=snneg[:]),
          waits=[(v, v.n)], inc=g, inck=16)
    p.add("gpsimd", lambda e: e.dma_start(out=dr["cc_s"][:, :], in_=ccneg[:]),
          waits=[(v, v.n)], inc=g, inck=16)
    sncc_g = g.n
    p.add("gpsimd", lambda e: e.dma_start(
        out=dr["k1_s"][0:1, 0:64].rearrange("o (n s) -> o n s", s=8),
        in_=dr["sn_s"].transpose([1, 0]).unsqueeze(0),
        allow_slow_non_contiguous=True),
        waits=[(g, sncc_g)], inc=g, inck=16)
    p.add("gpsimd", lambda e: e.dma_start(
        out=dr["k1_s"][0:1, 64:72], in_=dr["mn2_s"].rearrange("n o -> (n o)").unsqueeze(0)),
        inc=g, inck=16)
    p.add("gpsimd", lambda e: e.dma_start(
        out=dr["k1_s"][0:1, 72:136].rearrange("o (n h) -> o n h", h=8),
        in_=dr["cc_s"].transpose([1, 0]).unsqueeze(0),
        allow_slow_non_contiguous=True),
        inc=g, inck=16)

    st4, st5 = ctx.sem("st4"), ctx.sem("st5")
    p.add("gpsimd", lambda e, n4=st4.n, n5=st5.n, h4=st4.h, h5=st5.h: (
        e.wait_ge(h4, n4), e.wait_ge(h5, n5))[0])

    dgrow = ctx.sb("dgrow", (8, 16 * 128), BF16)
    ldg_f = ctx.sb("ldg_f", (128, 8), F32)
    ldg_b = ctx.sb("ldg_b", (128, 8), BF16)
    ps_tb = ctx.ps("pp_tb", (128, 1024), BF16)
    for n in range(8):
        for dc in range(16):
            src = dr["L0_ffn_lora_degen"][n, dc * 128:(dc + 1) * 128, :]
            p.add("sync", lambda e, src=src: e.dma_start(out=ldg_f[:], in_=src),
                  waits=[(v, v.n)], inc=ld, inck=16)
            p.add("vector", lambda e: e.tensor_copy(ldg_b[:], ldg_f[:]),
                  waits=[(ld, ld.n)], inc=v)
            pst_b = ps_tb[0:8, (dc % 2) * 512:(dc % 2) * 512 + 128]
            p.add("tensor", lambda e, pst_b=pst_b: e.transpose(pst_b, ldg_b[:], ident_b[:]),
                  waits=[(v, v.n)], inc=t)
            p.add("vector", lambda e, dc=dc, pst_b=pst_b: e.tensor_copy(
                dgrow[:, dc * 128:(dc + 1) * 128], pst_b), waits=[(t, t.n)], inc=v)
        p.add("gpsimd", lambda e, n=n: e.dma_start(
            out=dr["dgn_s"][n * 8:(n + 1) * 8, :], in_=dgrow[:]),
            waits=[(v, v.n)], inc=g, inck=16)
    p.add("gpsimd", lambda e, gn=g.n: e.wait_ge(g.h, gn))



def ph_a1(ctx):
    nc, p, dr = ctx.nc, ctx.p, ctx.dram
    ld, t, v, s, g = (ctx.sem(n) for n in ("ld", "t", "v", "s", "g"))

    wq = ctx.sb("a1_wq", (128, 16 * 2048), BF16)
    wkv = ctx.sb("a1_wkv", (128, 16 * 2048), BF16)
    xT = ctx.sb("a1_xT", (128, 16 * 512), BF16)
    qsb = ctx.sb("a1_q", (128, 16 * 512), BF16)
    kvsb = ctx.sb("a1_kv", (128, 16 * 512), BF16)
    sqb = ctx.sb("a1_sq", (128, 2 * 512), BF16)
    rstdq = ctx.sb("a1_rstdq", (16, 512), BF16)
    rstdkv = ctx.sb("a1_rstdkv", (16, 512), BF16)
    rtmp = ctx.sb("a1_rtmp", (16, 512), F32)
    qn = ctx.sb("a1_qn", (128, 512), BF16)
    kvn = ctx.sb("a1_kvn", (128, 512), F32)
    kvcs = ctx.sb("a1_kvcs", (128, 512), F32)
    kvcsb = ctx.sb("a1_kvcsb", (128, 512), BF16)
    carry = ctx.sb("a1_carry", (128, 16), F32)
    ffre = ctx.sb("a1_ffre", (128, 128), BF16)
    ffim = ctx.sb("a1_ffim", (128, 128), BF16)
    fire = ctx.sb("a1_fire", (128, 128), BF16)
    fiim = ctx.sb("a1_fiim", (128, 128), BF16)
    onescol = ctx.sb("a1_ones", (128, 256), BF16)
    hsel = ctx.sb("a1_hsel", (16, 2048), BF16)
    onesd = ctx.sb("a1_onesd", (128, 512), F32)
    bcb = ctx.sb("a1_bcb", (128, 512), BF16)
    qfre = ctx.sb("a1_qfre", (128, 512), BF16)
    qfim = ctx.sb("a1_qfim", (128, 512), BF16)
    kfre = ctx.sb("a1_kfre", (128, 512), BF16)
    kfim = ctx.sb("a1_kfim", (128, 512), BF16)
    qvre = ctx.sb("a1_qvre", (128, 512), BF16)
    qvim = ctx.sb("a1_qvim", (128, 512), BF16)
    tmpc = ctx.sb("a1_tmpc", (128, 512), F32)
    vhsb = ctx.sb("a1_vh", (128, 2 * 512), BF16)

    P = [ctx.ps(f"a1_ps{i}", (128, 512)) for i in range(8)]

    for dst, nmc in ((ffre, "c_fre"), (ffim, "c_fim"), (fire, "c_ifre"), (fiim, "c_ifim"),
                     (onescol, "c_onescol"), (hsel, "c_hsel"), (onesd, "c_onesf")):
        p.add("sync", lambda e, dst=dst, nmc=nmc: e.dma_start(out=dst[:], in_=dr[nmc][:, :]),
              inc=ld, inck=16)
    for wsb, wd in ((wq, "wqbf"), (wkv, "wkvbf")):
        for dc in range(16):
            for eb in range(4):
                src = dr[wd][eb * 512:(eb + 1) * 512, dc * 128:(dc + 1) * 128]
                dst = wsb[:, dc * 2048 + eb * 512: dc * 2048 + (eb + 1) * 512]
                p.add("sync", lambda e, dst=dst, src=src: e.dma_start_transpose(dst, src),
                      inc=ld, inck=16)
    const_ld = ld.n

    def xT_ap(buf_i, dc):
        return xT[:, dc * 512: (dc + 1) * 512]

    n_steps = B_LOC * NST
    evq_s = {}
    step_t_end = {}

    for step in range(n_steps):
        b, st = divmod(step, NST)
        bi = step % 2
        w_l = [(ld, const_ld)]
        if step >= 1:
            w_l.append((t, step_t_end[step - 1]))
        for dc in range(16):
            src = dr["xbf"][b, st * 512:(st + 1) * 512, dc * 128:(dc + 1) * 128]
            p.add("sync", lambda e, d=xT_ap(bi, dc), src=src: e.dma_start_transpose(d, src),
                  waits=w_l if dc == 0 else (), inc=ld, inck=16)
        ld_step = ld.n

        if st == 0:
            p.add("vector", lambda e: e.memset(carry[:], 0.0), inc=v)

        if step == 0:
            bank_evac = {}
            nsq_t = {}
            vh_store = {}
        for wi, (wsb, dstq, nsqbank) in enumerate(((wq, qsb, 4), (wkv, kvsb, 5))):
            for ec in range(16):
                bidx = wi * 2 + (ec % 2)
                bank = P[bidx]
                w_mm = [(ld, ld_step)]
                if bidx in bank_evac:
                    w_mm.append((s, bank_evac[bidx]))
                for dc in range(16):
                    lhsT = wsb[:, dc * 2048 + ec * 128: dc * 2048 + (ec + 1) * 128]
                    p.add("tensor", lambda e, bank=bank, lhsT=lhsT, dc=dc, bi=bi: e.matmul(
                        bank[:], lhsT, xT_ap(bi, dc), start=(dc == 0), stop=(dc == 15)),
                        waits=w_mm if dc == 0 else (), inc=t if dc == 15 else None)
                mm_done = t.n
                p.add("scalar", lambda e, bank=bank, dstq=dstq, ec=ec: e.copy(
                    dstq[:, ec * 512:(ec + 1) * 512], bank[:]), waits=[(t, mm_done)], inc=s)
                w_sq = [(t, nsq_t[ec % 2])] if (ec % 2) in nsq_t else []
                p.add("scalar", lambda e, bank=bank, ec=ec: e.activation(
                    sqb[:, (ec % 2) * 512:(ec % 2 + 1) * 512], bank[:], AF.Square),
                    waits=w_sq, inc=s)
                sq_done = s.n
                bank_evac[bidx] = sq_done
                p.add("tensor", lambda e, nb=P[nsqbank], ec=ec: e.matmul(
                    nb[0:16, :], onescol[:, ec * 16:(ec + 1) * 16],
                    sqb[:, (ec % 2) * 512:(ec % 2 + 1) * 512],
                    start=(ec == 0), stop=(ec == 15)),
                    waits=[(s, sq_done)], inc=t)
                nsq_t[ec % 2] = t.n
        nsq_done = t.n

        for nsqbank, rdst in ((4, rstdq), (5, rstdkv)):
            p.add("vector", lambda e, nb=P[nsqbank]: e.tensor_scalar_mul(
                rtmp[:], nb[0:16, :], 128.0), waits=[(t, nsq_done)], inc=v)
            p.add("scalar", lambda e: e.activation(rtmp[:], rtmp[:], AF.Ln),
                  waits=[(v, v.n)], inc=s)
            p.add("vector", lambda e: e.tensor_scalar_mul(rtmp[:], rtmp[:], -0.5),
                  waits=[(s, s.n)], inc=v)
            p.add("scalar", lambda e: e.activation(rtmp[:], rtmp[:], AF.Exp),
                  waits=[(v, v.n)], inc=s)
            p.add("vector", lambda e, rdst=rdst: e.tensor_copy(rdst[:], rtmp[:]),
                  waits=[(s, s.n)], inc=v)
        rstd_done = v.n

        for ec in range(16):
            bcp = P[0]
            p.add("tensor", lambda e, ec=ec: e.matmul(
                bcp[:], hsel[:, ec * 128:(ec + 1) * 128], rstdq[:], start=True, stop=True),
                waits=[(v, v.n)], inc=t)
            p.add("vector", lambda e: e.tensor_copy(bcb[:], bcp[:]), waits=[(t, t.n)])
            p.add("vector", lambda e, ec=ec: e.tensor_tensor(
                qn[:], qsb[:, ec * 512:(ec + 1) * 512], bcb[:], ALU.mult))
            qn_done = v.n
            p.add("tensor", lambda e: e.matmul(P[1][:], ffre[:], qn[:], start=True, stop=True),
                  waits=[(v, qn_done)], inc=t)
            p.add("tensor", lambda e: e.matmul(P[2][:], ffim[:], qn[:], start=True, stop=True),
                  inc=t)
            qf_mm = t.n
            p.add("scalar", lambda e: e.copy(qfre[:], P[1][:]), waits=[(t, qf_mm)], inc=s)
            p.add("scalar", lambda e: e.copy(qfim[:], P[2][:]), inc=s)
            qf_ev = s.n
            p.add("tensor", lambda e, ec=ec: e.matmul(
                bcp[:], hsel[:, ec * 128:(ec + 1) * 128], rstdkv[:], start=True, stop=True),
                waits=[(v, qn_done)], inc=t)
            p.add("vector", lambda e: e.tensor_copy(bcb[:], bcp[:]), waits=[(t, t.n)])
            p.add("vector", lambda e, ec=ec: e.tensor_tensor(
                kvn[:], kvsb[:, ec * 512:(ec + 1) * 512], bcb[:], ALU.mult))
            p.add("vector", lambda e, ec=ec: e.tensor_tensor_scan(
                kvcs[:], onesd[:], kvn[:], carry[:, ec:ec + 1], ALU.mult, ALU.add), inc=v)
            p.add("vector", lambda e, ec=ec: e.tensor_copy(carry[:, ec:ec + 1], kvcs[:, 511:512]), inc=v)
            p.add("vector", lambda e: e.tensor_copy(kvcsb[:], kvcs[:]), inc=v)
            kvcs_done = v.n
            p.add("tensor", lambda e: e.matmul(P[3][:], ffre[:], kvcsb[:], start=True, stop=True),
                  waits=[(v, kvcs_done)], inc=t)
            p.add("tensor", lambda e: e.matmul(P[4][:], ffim[:], kvcsb[:], start=True, stop=True),
                  inc=t)
            kf_mm = t.n
            p.add("scalar", lambda e: e.copy(kfre[:], P[3][:]), waits=[(t, kf_mm)], inc=s)
            p.add("scalar", lambda e: e.copy(kfim[:], P[4][:]), inc=s)
            kf_ev = s.n
            p.add("vector", lambda e: e.tensor_tensor(tmpc[:], qfre[:], kfre[:], ALU.mult),
                  waits=[(s, kf_ev)], inc=v)
            p.add("vector", lambda e: e.tensor_tensor(qvre[:], qfim[:], kfim[:], ALU.mult), inc=v)
            p.add("vector", lambda e: e.tensor_tensor(qvre[:], tmpc[:], qvre[:], ALU.subtract), inc=v)
            p.add("vector", lambda e: e.tensor_tensor(tmpc[:], qfre[:], kfim[:], ALU.mult), inc=v)
            p.add("vector", lambda e: e.tensor_tensor(qvim[:], qfim[:], kfre[:], ALU.mult), inc=v)
            p.add("vector", lambda e: e.tensor_tensor(qvim[:], tmpc[:], qvim[:], ALU.add), inc=v)
            qv_done = v.n
            p.add("tensor", lambda e: e.matmul(P[5][:], fire[:], qvre[:], start=True, stop=False),
                  waits=[(v, qv_done)], inc=None)
            p.add("tensor", lambda e: e.matmul(P[5][:], fiim[:], qvim[:], start=False, stop=True),
                  inc=t)
            vh_mm = t.n
            w_vh = [(t, vh_mm)]
            if (ec % 2) in vh_store:
                w_vh.append((g, vh_store[ec % 2]))
            p.add("scalar", lambda e, ec=ec: e.copy(
                vhsb[:, (ec % 2) * 512:(ec % 2 + 1) * 512], P[5][:]),
                waits=w_vh, inc=s)
            vh_ev = s.n
            dst = dr["vhT"][b, st, ec * 128:(ec + 1) * 128, :]
            p.add("gpsimd", lambda e, dst=dst, ec=ec: e.dma_start(
                out=dst, in_=vhsb[:, (ec % 2) * 512:(ec % 2 + 1) * 512]),
                waits=[(s, vh_ev)], inc=g, inck=16)
            vh_store[ec % 2] = g.n
        step_t_end[step] = t.n
    p.add("gpsimd", lambda e, gn=g.n: e.wait_ge(g.h, gn))



def _ln_stats(ctx, pre_ap, nfree, tl, eps):
    p, v, s = ctx.p, ctx.sem("v"), ctx.sem("s")
    p.add("scalar", lambda e: e.activation(
        tl["dump"][:, 0:nfree], pre_ap, AF.Identity, accum_out=tl["sumb"][:, 0:1]),
        waits=[(v, v.n)])
    p.add("vector", lambda e: e.tensor_scalar_mul(
        tl["negm"][:, 0:1], tl["sumb"][:, 0:1], -1.0 / nfree), waits=[(s, s.n)])
    p.add("scalar", lambda e: e.activation(
        tl["dump"][:, 0:nfree], pre_ap, AF.Square, bias=tl["negm"][:, 0:1],
        accum_out=tl["ssq"][:, 0:1]), waits=[(v, v.n)])
    p.add("vector", lambda e, eps=eps: e.tensor_scalar(
        tl["vpe"][:, 0:1], tl["ssq"][:, 0:1], 1.0 / nfree, eps, ALU.mult, ALU.add),
        waits=[(s, s.n)])
    p.add("scalar", lambda e: e.activation(
        tl["rstd"][:, 0:1], tl["vpe"][:, 0:1], AF.Ln), waits=[(v, v.n)])
    p.add("vector", lambda e: e.tensor_scalar_mul(
        tl["rstd"][:, 0:1], tl["rstd"][:, 0:1], -0.5), waits=[(s, s.n)])
    p.add("scalar", lambda e: e.activation(
        tl["rstd"][:, 0:1], tl["rstd"][:, 0:1], AF.Exp), waits=[(v, v.n)])
    p.add("vector", lambda e: e.tensor_scalar_mul(
        tl["mean"][:, 0:1], tl["sumb"][:, 0:1], 1.0 / nfree), waits=[(s, s.n)])


def ph_a2(ctx):
    nc, p, dr = ctx.nc, ctx.p, ctx.dram
    ld, t, v, s, g = (ctx.sem(n) for n in ("ld", "t", "v", "s", "g"))

    wo = ctx.sb("a2_wo", (128, 16 * 2048), BF16)
    vh = ctx.sb("a2_vh", (128, 16 * 512), BF16)
    ffnw = ctx.sb("a2_ffnw", (128, 16 * 208), BF16)
    dgn = ctx.sb("a2_dgn", (72, 2048), BF16)
    k1 = ctx.sb("a2_k1", (1, 136), BF16)
    g0 = ctx.sb("a2_g0", (1, 8), F32)
    g1 = ctx.sb("a2_g1", (1, 8), F32)
    slope_b = ctx.sb("a2_slb", (128, 8), F32)
    slc_b = ctx.sb("a2_slc", (128, 8), F32)
    onesr = ctx.sb("a2_onesr", (1, 128), BF16)
    onesrf = ctx.sb("a2_onesrf", (1, 128), F32)
    identf = ctx.sb("a2_idf", (128, 128), F32)
    identb = ctx.sb("a2_idb", (128, 128), BF16)
    onesf = ctx.sb("a2_onesf", (128, 8), F32)
    xf = ctx.sb("a2_xf", (128, 2 * 2048), F32)
    pre = ctx.sb("a2_pre", (128, 2048), F32)
    h1f = ctx.sb("a2_h1", (128, 2048), F32)
    dump = ctx.sb("a2_dump", (128, 2048), F32)
    xhi = ctx.sb("a2_xhi", (128, 2048), BF16)
    xlo = ctx.sb("a2_xlo", (128, 2048), BF16)
    tpf = ctx.sb("a2_tpf", (128, 128), F32)
    tpf2 = ctx.sb("a2_tpf2", (128, 128), F32)
    sel = ctx.sb("a2_sel", (128, 136), F32)
    h2 = ctx.sb("a2_h2", (128, 2048), F32)
    h2b = ctx.sb("a2_h2b", (128, 2048), BF16)
    hext = ctx.sb("a2_hext", (128, 72), BF16)
    hexT = ctx.sb("a2_hexT", (72, 128), BF16)
    sm = {k: ctx.sb("a2_" + k, (128, 8), F32) for k in
          ("sumb", "negm", "ssq", "vpe", "rstd", "mean", "sumx2", "r2", "rr",
           "s1", "score", "z", "expz", "gate", "gmax", "ge", "pfx", "eq1",
           "oh", "s2m", "macc")}
    sm["dump"] = dump
    d2 = ctx.sb("a2_d2", (128, 64), F32)
    tmp512 = ctx.sb("a2_tmp512", (128, 512), F32)

    P = [ctx.ps(f"a2_ps{i}", (128, 512)) for i in range(7)]
    ps_tb = ctx.ps("a2_pstb", (128, 1024), BF16)

    for dst, nmc in ((onesr, "c_onesr"), (onesrf, "c_onesrf"), (identf, "c_identf"),
                     (identb, "c_ident")):
        p.add("sync", lambda e, dst=dst, nmc=nmc: e.dma_start(out=dst[:], in_=dr[nmc][:, :]),
              inc=ld, inck=16)
    p.add("sync", lambda e: e.dma_start(out=onesf[:], in_=dr["c_onesf"][:, 0:8],
                                        allow_slow_non_contiguous=True), inc=ld, inck=16)
    p.add("sync", lambda e: e.dma_start(out=ffnw[:], in_=dr["ffnw_s"][:, :]), inc=ld, inck=16)
    p.add("sync", lambda e: e.dma_start(out=dgn[:], in_=dr["dgn_s"][:, :]), inc=ld, inck=16)
    p.add("sync", lambda e: e.dma_start(out=k1[:], in_=dr["k1_s"][:, :]), inc=ld, inck=16)
    p.add("sync", lambda e: e.dma_start(out=g0[:], in_=dr["gate_s"][0:1, :]), inc=ld, inck=16)
    p.add("sync", lambda e: e.dma_start(out=g1[:], in_=dr["gate_s"][1:2, :]), inc=ld, inck=16)
    for dc in range(16):
        for eb in range(4):
            src = dr["wobf"][eb * 512:(eb + 1) * 512, dc * 128:(dc + 1) * 128]
            dst = wo[:, dc * 2048 + eb * 512: dc * 2048 + (eb + 1) * 512]
            p.add("sync", lambda e, dst=dst, src=src: e.dma_start_transpose(dst, src),
                  inc=ld, inck=16)
    const_ld = ld.n
    p.add("tensor", lambda e: e.matmul(P[4][:, 0:8], onesrf[:], g0[:], start=True, stop=True),
          waits=[(ld, const_ld)], inc=t)
    p.add("tensor", lambda e: e.matmul(P[5][:, 0:8], onesrf[:], g1[:], start=True, stop=True),
          inc=t)
    p.add("vector", lambda e: e.tensor_copy(slope_b[:], P[4][:, 0:8]), waits=[(t, t.n)])
    p.add("vector", lambda e: e.tensor_copy(slc_b[:], P[5][:, 0:8]))
    gates_v = v.n

    vh_t_guard = 0
    prev_store = {}
    for step in range(B_LOC * NST):
        b, st = divmod(step, NST)
        w_l = [(ld, const_ld)]
        if step > 0:
            w_l.append((t, vh_t_guard))
        for ec in range(16):
            src = dr["vhT"][b, st, ec * 128:(ec + 1) * 128, :]
            p.add("sync", lambda e, ec=ec, src=src: e.dma_start(
                out=vh[:, ec * 512:(ec + 1) * 512], in_=src),
                waits=w_l if ec == 0 else (), inc=ld, inck=16)
        vh_ld = ld.n
        for tt in range(4):
            row0 = st * 512 + tt * 128
            w_x = [(v, v.n)] if step + tt > 0 else []
            p.add("sync", lambda e, b=b, row0=row0, tt=tt: e.dma_start(
                out=xf[:, (tt % 2) * 2048:(tt % 2 + 1) * 2048],
                in_=dr["x"][b, row0:row0 + 128, :]), waits=w_x, inc=ld, inck=16)
            x_ld = ld.n
            for dc in range(4):
                for ec in range(16):
                    lhsT = vh[:, ec * 512 + tt * 128: ec * 512 + tt * 128 + 128]
                    rhs = wo[:, ec * 2048 + dc * 512: ec * 2048 + (dc + 1) * 512]
                    p.add("tensor", lambda e, dc=dc, lhsT=lhsT, rhs=rhs, ec=ec: e.matmul(
                        P[dc][:], lhsT, rhs, start=(ec == 0), stop=(ec == 15)),
                        waits=[(ld, vh_ld), (v, v.n)] if ec == 0 else (),
                        inc=t if ec == 15 else None)
            wo_done = t.n
            vh_t_guard = t.n
            for dc in range(4):
                p.add("vector", lambda e, dc=dc, tt=tt: e.tensor_tensor(
                    pre[:, dc * 512:(dc + 1) * 512],
                    xf[:, (tt % 2) * 2048 + dc * 512:(tt % 2) * 2048 + (dc + 1) * 512],
                    P[dc][:], ALU.add), waits=[(t, wo_done), (ld, x_ld)])
            _ln_stats(ctx, pre[:], 2048, sm, EPS_LN)
            p.add("vector", lambda e: e.reciprocal(sm["sumx2"][:, 0:1], sm["vpe"][:, 0:1]))
            p.add("vector", lambda e: e.tensor_tensor(
                sm["sumx2"][:, 0:1], sm["ssq"][:, 0:1], sm["sumx2"][:, 0:1], ALU.mult))
            p.add("vector", lambda e: e.tensor_scalar(
                h1f[:], pre[:], sm["mean"][:, 0:1], sm["rstd"][:, 0:1],
                ALU.subtract, ALU.mult))
            h1_done = v.n
            for dc in range(16):
                bank = P[4 + dc % 2]
                p.add("tensor", lambda e, dc=dc, bank=bank: e.transpose(
                    bank[:, 0:128], h1f[:, dc * 128:(dc + 1) * 128], identf[:]),
                    waits=[(v, v.n)], inc=t)
                p.add("vector", lambda e, bank=bank: e.tensor_copy(tpf[:], bank[:, 0:128]),
                      waits=[(t, t.n)])
                p.add("vector", lambda e, dc=dc: e.tensor_copy(
                    xhi[:, dc * 128:(dc + 1) * 128], tpf[:]))
                p.add("vector", lambda e, dc=dc: e.tensor_copy(
                    tpf2[:], xhi[:, dc * 128:(dc + 1) * 128]))
                p.add("vector", lambda e, dc=dc: e.tensor_tensor(
                    xlo[:, dc * 128:(dc + 1) * 128], tpf[:], tpf2[:], ALU.subtract))
            xsplit_done = v.n
            nsel = 0
            for pas, (xsrc, c0, c1) in enumerate(
                    ((xhi, 0, 136), (xhi, 136, 208), (xlo, 0, 72))):
                for dc in range(16):
                    lhsT = xsrc[:, dc * 128:(dc + 1) * 128]
                    rhs = ffnw[:, dc * 208 + c0: dc * 208 + c1]
                    odst = P[6][:, 0:c1 - c0]
                    p.add("tensor", lambda e, lhsT=lhsT, rhs=rhs, odst=odst,
                          first=(nsel == 0): e.matmul(
                        odst, lhsT, rhs, start=first, stop=False),
                        waits=[(v, xsplit_done)] if nsel == 0 else ())
                    nsel += 1
            p.add("tensor", lambda e: e.matmul(
                P[6][:, 0:136], onesr[:], k1[:], start=False, stop=True), inc=t)
            sel_done = t.n
            p.add("vector", lambda e: e.tensor_copy(sel[:], P[6][:, 0:136]),
                  waits=[(t, sel_done)])
            p.add("vector", lambda e: e.tensor_scalar(
                sm["r2"][:], sel[:, 64:72], sm["sumx2"][:, 0:1], None, ALU.add))
            p.add("vector", lambda e: e.reciprocal(sm["rr"][:], sm["r2"][:]))
            p.add("vector", lambda e: e.tensor_tensor(d2[:], sel[:, 0:64], sel[:, 0:64], ALU.mult))
            p.add("vector", lambda e: e.tensor_reduce(
                sm["s1"][:].unsqueeze(2), d2[:].rearrange("p (n s_) -> p n s_", s_=8),
                AX.X, ALU.add))
            p.add("vector", lambda e: e.tensor_tensor(sm["score"][:], sm["s1"][:], sm["rr"][:], ALU.mult))
            p.add("vector", lambda e: e.tensor_tensor(sm["z"][:], sm["score"][:], slope_b[:], ALU.mult),
                  waits=[(v, gates_v)])
            p.add("vector", lambda e: e.tensor_tensor(sm["z"][:], sm["z"][:], slc_b[:], ALU.subtract))
            p.add("scalar", lambda e: e.activation(sm["expz"][:], sm["z"][:], AF.Exp,
                                                   bias=0.0, scale=-1.0), waits=[(v, v.n)])
            p.add("vector", lambda e: e.tensor_scalar_add(sm["gate"][:], sm["expz"][:], 1.0),
                  waits=[(s, s.n)])
            p.add("vector", lambda e: e.reciprocal(sm["gate"][:], sm["gate"][:]))
            p.add("vector", lambda e: e.reduce_max(sm["gmax"][:, 0:1], sm["gate"][:], AX.X))
            p.add("vector", lambda e: e.tensor_single_scalar(
                sm["ge"][:], sm["gate"][:], sm["gmax"][:, 0:1], ALU.is_ge))
            p.add("vector", lambda e: e.tensor_tensor_scan(
                sm["pfx"][:], onesf[:], sm["ge"][:], 0.0, ALU.mult, ALU.add))
            p.add("vector", lambda e: e.tensor_single_scalar(
                sm["eq1"][:], sm["pfx"][:], 1.0, ALU.is_equal))
            p.add("vector", lambda e: e.tensor_tensor(sm["oh"][:], sm["ge"][:], sm["eq1"][:], ALU.mult))
            p.add("vector", lambda e: e.tensor_scalar(
                sm["s2m"][:, 0:1], sm["gmax"][:, 0:1], -1.0, 2.0, ALU.mult, ALU.add))
            p.add("vector", lambda e: e.tensor_scalar(
                sm["macc"][:], sel[:, 72:80], sm["oh"][:, 0:1], None, ALU.mult))
            for n in range(1, 8):
                p.add("vector", lambda e, n=n: e.scalar_tensor_tensor(
                    sm["macc"][:], sel[:, 72 + 8 * n: 80 + 8 * n], sm["oh"][:, n:n + 1],
                    sm["macc"][:], ALU.mult, ALU.add))
            for n in range(8):
                p.add("vector", lambda e, n=n: e.tensor_scalar(
                    hext[:, 8 * n:8 * n + 8], sm["macc"][:], sm["oh"][:, n:n + 1],
                    None, ALU.mult))
            p.add("vector", lambda e: e.tensor_copy(hext[:, 64:72], sm["oh"][:]))
            hext_done = v.n
            p.add("tensor", lambda e: e.transpose(ps_tb[0:72, 0:128], hext[:], identb[:]),
                  waits=[(v, hext_done)], inc=t)
            p.add("vector", lambda e: e.tensor_copy(hexT[:], ps_tb[0:72, 0:128]),
                  waits=[(t, t.n)])
            hexT_done = v.n
            for dc in range(4):
                p.add("tensor", lambda e, dc=dc: e.matmul(
                    P[dc][:], hexT[:], dgn[:, dc * 512:(dc + 1) * 512],
                    start=True, stop=True),
                    waits=[(v, hexT_done)], inc=t)
            mod_done = t.n
            for dc in range(4):
                p.add("vector", lambda e, dc=dc: e.tensor_scalar(
                    tmp512[:], h1f[:, dc * 512:(dc + 1) * 512], sm["s2m"][:, 0:1],
                    None, ALU.mult), waits=[(t, mod_done)] if dc == 0 else ())
                p.add("vector", lambda e, dc=dc: e.scalar_tensor_tensor(
                    pre[:, dc * 512:(dc + 1) * 512], P[dc][:], sm["gmax"][:, 0:1],
                    tmp512[:], ALU.mult, ALU.add))
            _ln_stats(ctx, pre[:], 2048, sm, EPS_LN)
            p.add("vector", lambda e: e.tensor_scalar(
                h2[:], pre[:], sm["mean"][:, 0:1], sm["rstd"][:, 0:1],
                ALU.subtract, ALU.mult))
            p.add("vector", lambda e: e.tensor_copy(h2b[:], h2[:]))
            h2_done = v.n
            w_st = [(v, h2_done)]
            p.add("gpsimd", lambda e, b=b, row0=row0: e.dma_start(
                out=dr["h2f"][b, row0:row0 + 128, :], in_=h2[:]), waits=w_st, inc=g, inck=16)
            p.add("gpsimd", lambda e, b=b, row0=row0: e.dma_start(
                out=dr["h2bf"][b, row0:row0 + 128, :], in_=h2b[:]), waits=[(v, h2_done)],
                inc=g, inck=16)
            p.add("vector", lambda e, gn=g.n: e.nop(), waits=[(g, g.n)])
    p.add("gpsimd", lambda e, gn=g.n: e.wait_ge(g.h, gn))



def ph_b1(ctx):
    nc, p, dr = ctx.nc, ctx.p, ctx.dram
    ld, t, v, s, g = (ctx.sem(n) for n in ("ld", "t", "v", "s", "g"))
    wk = ctx.sb("b1_wk", (128, 16 * 3072), BF16)
    h2T = ctx.sb("b1_h2T", (128, 2 * 16 * 512), BF16)
    ev = ctx.sb("b1_ev", (128, 4 * 512), BF16)
    P = [ctx.ps(f"b1_ps{i}", (128, 512)) for i in range(4)]

    ev_guard = {}
    ev_g = {}
    slot_guard = {}
    for half in range(2):
        e0 = half * 3072
        w_l = [(t, t.n)]
        for dc in range(16):
            for eb in range(6):
                src = dr["wqkvbf"][e0 + eb * 512: e0 + (eb + 1) * 512,
                                   dc * 128:(dc + 1) * 128]
                dst = wk[:, dc * 3072 + eb * 512: dc * 3072 + (eb + 1) * 512]
                p.add("sync", lambda e, dst=dst, src=src: e.dma_start_transpose(dst, src),
                      waits=w_l if (dc == 0 and eb == 0) else (), inc=ld, inck=16)
        wk_ld = ld.n
        for step in range(B_LOC * NST):
            b, st = divmod(step, NST)
            idx = half * 8 + step
            sl = idx % 2
            lh = ctx.sem("ld2" if sl == 0 else "ld3")
            hsl = h2T[:, sl * 8192:(sl + 1) * 8192]
            w_h = [(t, slot_guard.get(sl, 0))]
            for dc in range(16):
                src = dr["h2bf"][b, st * 512:(st + 1) * 512, dc * 128:(dc + 1) * 128]
                p.add("sync", lambda e, dc=dc, src=src, hsl=hsl: e.dma_start_transpose(
                    hsl[:, dc * 512:(dc + 1) * 512], src),
                    waits=w_h if dc == 0 else (), inc=lh, inck=16)
            h2_ld = lh.n
            for tt in range(4):
                row0 = st * 512 + tt * 128
                for eb in range(6):
                    bank = P[eb % 4]
                    w_mm = [(lh, h2_ld), (ld, wk_ld)]
                    if (eb % 4) in ev_guard:
                        w_mm.append((s, ev_guard[eb % 4]))
                    for dc in range(16):
                        lhsT = hsl[:, dc * 512 + tt * 128: dc * 512 + tt * 128 + 128]
                        rhs = wk[:, dc * 3072 + eb * 512: dc * 3072 + (eb + 1) * 512]
                        p.add("tensor", lambda e, bank=bank, lhsT=lhsT, rhs=rhs, dc=dc: e.matmul(
                            bank[:], lhsT, rhs, start=(dc == 0), stop=(dc == 15)),
                            waits=w_mm if dc == 0 else (), inc=t if dc == 15 else None)
                    mm_done = t.n
                    w_ev = [(t, mm_done)]
                    p.add("scalar", lambda e, bank=bank, eb=eb: e.copy(
                        ev[:, (eb % 4) * 512:(eb % 4 + 1) * 512], bank[:]),
                        waits=w_ev + ([(g, ev_g[eb % 4])] if (eb % 4) in ev_g else []))
                    ev_guard[eb % 4] = s.n
                    dst = dr["qkvbf"][b, row0:row0 + 128,
                                      e0 + eb * 512: e0 + (eb + 1) * 512]
                    p.add("gpsimd", lambda e, dst=dst, eb=eb: e.dma_start(
                        out=dst, in_=ev[:, (eb % 4) * 512:(eb % 4 + 1) * 512]),
                        waits=[(s, s.n)], inc=g, inck=16)
                    ev_g[eb % 4] = g.n
            slot_guard[sl] = t.n
    p.add("gpsimd", lambda e, gn=g.n: e.wait_ge(g.h, gn))



def ph_b2(ctx):
    nc, p, dr = ctx.nc, ctx.p, ctx.dram
    ld, t, v, s, g = (ctx.sem(n) for n in ("ld", "t", "v", "s", "g"))
    SC = 1.0 / np.sqrt(128.0)

    wo = ctx.sb("b2_wo", (128, 16 * 2048), BF16)
    atn = ctx.sb("b2_atn", (128, 16 * 2048), BF16)
    qT = ctx.sb("b2_qT", (128, 2 * 2048), BF16)
    kT = ctx.sb("b2_kT", (128, 2 * 2048), BF16)
    vv = ctx.sb("b2_vv", (128, 2 * 2048), BF16)
    pt = ctx.sb("b2_pt", (128, 2 * 1024), BF16)
    rsum = ctx.sb("b2_rs", (1, 512), F32)
    rsumb = ctx.sb("b2_rsb", (1, 512), BF16)
    rbc = ctx.sb("b2_rbc", (128, 512), F32)
    onesr = ctx.sb("b2_onesr", (1, 128), BF16)
    onesc = ctx.sb("b2_onesc", (128, 1), BF16)
    h2t = ctx.sb("b2_h2t", (128, 2048), F32)
    pre = ctx.sb("b2_pre", (128, 2048), F32)
    dump = ctx.sb("b2_dump", (128, 2048), F32)
    h3 = ctx.sb("b2_h3", (128, 2048), F32)
    h3b = ctx.sb("b2_h3b", (128, 2048), BF16)
    sm = {k: ctx.sb("b2s_" + k, (128, 1), F32) for k in
          ("sumb", "negm", "ssq", "vpe", "rstd", "mean")}
    sm["dump"] = dump

    SC4 = [ctx.ps(f"b2_sc{i}", (128, 512)) for i in range(4)]
    SCB = [None, None]
    PVB = [ctx.ps(f"b2_pv{i}", (128, 512)) for i in range(2)]
    RS = ctx.ps("b2_rsp", (128, 512))
    BC = ctx.ps("b2_bcp", (128, 512))

    p.add("sync", lambda e: e.dma_start(out=onesr[:], in_=dr["c_onesr"][:, :]), inc=ld, inck=16)
    p.add("sync", lambda e: e.dma_start(out=onesc[:], in_=dr["c_ones"][:, 0:1],
                                        allow_slow_non_contiguous=True), inc=ld, inck=16)
    for dc in range(16):
        for eb in range(4):
            src = dr["wo1bf"][eb * 512:(eb + 1) * 512, dc * 128:(dc + 1) * 128]
            dst = wo[:, dc * 2048 + eb * 512: dc * 2048 + (eb + 1) * 512]
            p.add("sync", lambda e, dst=dst, src=src: e.dma_start_transpose(dst, src),
                  inc=ld, inck=16)
    const_ld = ld.n

    qkv_flat = dr["qkvbf"].rearrange("b s e -> b (s e)").rearrange(
        "b (th hh l i) -> b th hh l i", th=3, hh=16, l=2048)

    slot_guard = {}
    sc_guard = {}
    sc_guard_v = {}
    pt_guard = {}
    pv_guard = {}
    bc_guard = 0
    rs_guard = 0
    atn_guard = 0
    preadd_guard = 0
    for b in range(B_LOC):
        for h in range(16):
            bi = h % 2
            lq = ctx.sem("ld2" if bi == 0 else "ld3")
            w_l = [(ld, const_ld), (t, slot_guard.get(bi, 0))]
            p.add("sync", lambda e, b=b, h=h, bi=bi: e.dma_start_transpose(
                qT[:, bi * 2048:(bi + 1) * 2048], qkv_flat[b, 0, h]),
                waits=w_l, inc=lq, inck=16)
            p.add("sync", lambda e, b=b, h=h, bi=bi: e.dma_start_transpose(
                kT[:, bi * 2048:(bi + 1) * 2048], qkv_flat[b, 1, h]),
                inc=lq, inck=16)
            p.add("sync", lambda e, b=b, h=h, bi=bi: e.dma_start(
                out=vv[:, bi * 2048:(bi + 1) * 2048].rearrange("p (kc i) -> p kc i", i=128),
                in_=qkv_flat[b, 2, h].rearrange("(kc pp) i -> pp kc i", pp=128)),
                inc=lq, inck=16)
            qkv_ld = lq.n
            p.add("vector", lambda e, bi=bi: e.tensor_scalar_mul(
                qT[:, bi * 2048:(bi + 1) * 2048], qT[:, bi * 2048:(bi + 1) * 2048], SC),
                waits=[(lq, qkv_ld)], chain=False)
            q_ready = v.n

            for qg in range(4):
                pvb = PVB[qg % 2]
                qsl = qT[:, bi * 2048 + qg * 512: bi * 2048 + (qg + 1) * 512]

                def scores(kp):
                    scb = SCB[kp % 2]
                    w0 = [(s, sc_guard.get(kp % 2, 0)),
                          (v, sc_guard_v.get(kp % 2, 0))]
                    if qg == 0 and kp <= 1:
                        w0.append((v, q_ready))
                    w0.append((lq, qkv_ld))
                    for half in range(2):
                        kc = 2 * kp + half
                        sc4 = SC4[2 * (kp % 2) + half]
                        p.add("tensor", lambda e, kc=kc, sc4=sc4, bi=bi, qsl=qsl: e.matmul(
                            sc4[:],
                            kT[:, bi * 2048 + kc * 128: bi * 2048 + kc * 128 + 128],
                            qsl, start=True, stop=True),
                            waits=w0 if half == 0 else (), inc=t if half == 1 else None)
                    return t.n

                sc_done = {0: scores(0)}
                for kp in range(8):
                    if kp + 1 < 8:
                        sc_done[kp + 1] = scores(kp + 1)
                    w_exp = [(t, sc_done[kp]), (t, pt_guard.get(kp % 2, 0))]
                    p.add("scalar", lambda e, kp=kp: e.activation(
                        pt[:, (kp % 2) * 1024:(kp % 2) * 1024 + 512],
                        SC4[2 * (kp % 2)][:], AF.Exp), waits=w_exp)
                    p.add("scalar", lambda e, kp=kp: e.activation(
                        pt[:, (kp % 2) * 1024 + 512:(kp % 2 + 1) * 1024],
                        SC4[2 * (kp % 2) + 1][:], AF.Exp))
                    exp_done = s.n
                    sc_guard[kp % 2] = exp_done
                    for half in range(2):
                        kc = 2 * kp + half
                        w_pv = [(s, exp_done)]
                        if kp == 0 and half == 0:
                            w_pv.append((v, pv_guard.get(qg % 2, 0)))
                        p.add("tensor", lambda e, kc=kc, kp=kp, half=half, pvb=pvb, bi=bi: e.matmul(
                            pvb[:], vv[:, bi * 2048 + kc * 128: bi * 2048 + kc * 128 + 128],
                            pt[:, (kp % 2) * 1024 + half * 512:(kp % 2) * 1024 + (half + 1) * 512],
                            start=(kp == 0 and half == 0), stop=(kp == 7 and half == 1)),
                            waits=w_pv if half == 0 else (), inc=t if half == 1 else None)
                    for half in range(2):
                        w_rs = [(v, rs_guard)] if (kp == 0 and half == 0) else ()
                        p.add("tensor", lambda e, kp=kp, half=half: e.matmul(
                            RS[0:1, :], onesc[:],
                            pt[:, (kp % 2) * 1024 + half * 512:(kp % 2) * 1024 + (half + 1) * 512],
                            start=(kp == 0 and half == 0), stop=(kp == 7 and half == 1)),
                            waits=w_rs, inc=t if half == 1 else None)
                    pt_guard[kp % 2] = t.n
                pv_done = t.n
                p.add("vector", lambda e: e.reciprocal(rsum[:], RS[0:1, :]),
                      waits=[(t, pv_done)])
                rs_guard = v.n
                p.add("vector", lambda e: e.tensor_copy(rsumb[:], rsum[:]))
                rs_done = v.n
                p.add("tensor", lambda e: e.matmul(
                    BC[:], onesr[:], rsumb[:], start=True, stop=True),
                    waits=[(v, rs_done), (v, bc_guard)], inc=t)
                bc_done = t.n
                p.add("vector", lambda e: e.tensor_copy(rbc[:], BC[:]),
                      waits=[(t, bc_done)])
                bc_guard = v.n
                p.add("vector", lambda e, h=h, qg=qg, pvb=pvb: e.tensor_tensor(
                    atn[:, h * 2048 + qg * 512: h * 2048 + (qg + 1) * 512],
                    pvb[:], rbc[:], ALU.mult), waits=[(t, atn_guard)])
                pv_guard[qg % 2] = v.n
            slot_guard[bi] = t.n
        norm_all = v.n
        exp_all = s.n
        import os as _os
        if _os.environ.get("B2_DBG") and b == 0:
            p.add("gpsimd", lambda e: e.dma_start(out=dr["dbg"][:, :], in_=atn[:]),
                  waits=[(v, norm_all)], inc=g, inck=16)
        WOB = [SC4[0][:], SC4[1][:], SC4[2][:], SC4[3][:]]
        for tt in range(16):
            row0 = tt * 128
            p.add("sync", lambda e, b=b, row0=row0: e.dma_start(
                out=h2t[:], in_=dr["h2f"][b, row0:row0 + 128, :]),
                waits=[(v, preadd_guard), (ld, const_ld)],
                inc=ld, inck=16)
            h2_ld = ld.n
            import os as _os
            if _os.environ.get("B2_STRONG"):
                w_wo = [(v, v.n), (s, s.n)]
            elif tt == 0:
                w_wo = [(v, norm_all), (s, exp_all), (v, pv_guard.get(0, 0)),
                        (v, pv_guard.get(1, 0)), (v, rs_guard), (v, bc_guard)]
            else:
                w_wo = [(v, preadd_guard)]
            for dc in range(4):
                for ec in range(16):
                    lhsT = atn[:, ec * 2048 + tt * 128: ec * 2048 + tt * 128 + 128]
                    rhs = wo[:, ec * 2048 + dc * 512: ec * 2048 + (dc + 1) * 512]
                    p.add("tensor", lambda e, dc=dc, lhsT=lhsT, rhs=rhs, ec=ec: e.matmul(
                        WOB[dc], lhsT, rhs, start=(ec == 0), stop=(ec == 15)),
                        waits=w_wo if (ec == 0 and dc == 0) else (),
                        inc=t if ec == 15 else None)
            wo_done = t.n
            for dc in range(4):
                p.add("vector", lambda e, dc=dc: e.tensor_tensor(
                    pre[:, dc * 512:(dc + 1) * 512], h2t[:, dc * 512:(dc + 1) * 512],
                    WOB[dc], ALU.add), waits=[(t, wo_done), (ld, h2_ld)])
            preadd_guard = v.n
            _ln_stats(ctx, pre[:], 2048, sm, EPS_LN)
            p.add("vector", lambda e: e.tensor_scalar(
                h3[:], pre[:], sm["mean"][:, 0:1], sm["rstd"][:, 0:1],
                ALU.subtract, ALU.mult))
            p.add("vector", lambda e: e.tensor_copy(h3b[:], h3[:]))
            h3_done = v.n
            p.add("gpsimd", lambda e, b=b, row0=row0: e.dma_start(
                out=dr["h3f"][b, row0:row0 + 128, :], in_=h3[:]),
                waits=[(v, h3_done)], inc=g, inck=16)
            p.add("gpsimd", lambda e, b=b, row0=row0: e.dma_start(
                out=dr["h3bf"][b, row0:row0 + 128, :], in_=h3b[:]),
                waits=[(v, h3_done)], inc=g, inck=16)
            p.add("vector", lambda e: e.nop(), waits=[(g, g.n)])
        atn_guard = t.n
        for kk in (0, 1):
            pt_guard[kk] = t.n
            sc_guard_v[kk] = preadd_guard
    p.add("gpsimd", lambda e, gn=g.n: e.wait_ge(g.h, gn))



def ph_c2a(ctx):
    nc, p, dr = ctx.nc, ctx.p, ctx.dram
    ld, t, v, s, g = (ctx.sem(n) for n in ("ld", "t", "v", "s", "g"))
    h3T = ctx.sb("c2a_h3T", (128, 16 * 4096), BF16)
    w1T = ctx.sb("c2a_w1T", (128, 2 * 16 * 512), BF16)
    hrb = ctx.sb("c2a_hr", (128, 4 * 512), BF16)
    hsq = ctx.sb("c2a_hsq", (128, 512), F32)
    relu_t = ctx.sb("c2a_relu", (128, 512), BF16)
    sqf = ctx.sb("c2a_sqf", (128, 512), F32)
    acc = ctx.sb("c2a_acc", (128, 8 * 512), F32)
    ssr = ctx.sb("c2a_ssr", (1, 512), F32)
    rmsc = ctx.sb("c2a_rmsc", (128, 32), F32)
    identf = ctx.sb("c2a_idf", (128, 128), F32)
    onesc = ctx.sb("c2a_onesc", (128, 1), F32)
    P = [ctx.ps(f"c2a_ps{i}", (128, 512)) for i in range(6)]

    p.add("sync", lambda e: e.dma_start(out=identf[:], in_=dr["c_identf"][:, :]), inc=ld, inck=16)
    p.add("sync", lambda e: e.dma_start(out=onesc[:], in_=dr["c_onesf"][:, 0:1],
                                        allow_slow_non_contiguous=True), inc=ld, inck=16)
    for dc in range(16):
        for step in range(8):
            b, st = divmod(step, NST)
            src = dr["h3bf"][b, st * 512:(st + 1) * 512, dc * 128:(dc + 1) * 128]
            dst = h3T[:, dc * 4096 + step * 512: dc * 4096 + (step + 1) * 512]
            p.add("sync", lambda e, dst=dst, src=src: e.dma_start_transpose(dst, src),
                  inc=ld, inck=16)
    h3_ld = ld.n
    p.add("vector", lambda e: e.memset(acc[:], 0.0))
    acc_z = v.n

    hr_g = {}
    hr_st = {}
    for hb in range(16):
        wsl = w1T[:, (hb % 2) * 8192:(hb % 2 + 1) * 8192]
        w_l = [(t, t.n)] if hb >= 2 else []
        for dc in range(16):
            src = dr["w1bf"][hb * 512:(hb + 1) * 512, dc * 128:(dc + 1) * 128]
            p.add("sync", lambda e, wsl=wsl, dc=dc, src=src: e.dma_start_transpose(
                wsl[:, dc * 512:(dc + 1) * 512], src),
                waits=w_l if dc == 0 else (), inc=ld, inck=16)
        w_ld = ld.n
        for hc4 in range(4):
            hcg = hb * 4 + hc4
            for step in range(8):
                bank = P[step % 2]
                w_mm = [(ld, max(w_ld, h3_ld))]
                if (step % 2) in hr_g:
                    w_mm.append((s, hr_g[step % 2]))
                for dc in range(16):
                    lhsT = wsl[:, dc * 512 + hc4 * 128: dc * 512 + hc4 * 128 + 128]
                    rhs = h3T[:, dc * 4096 + step * 512: dc * 4096 + (step + 1) * 512]
                    p.add("tensor", lambda e, bank=bank, lhsT=lhsT, rhs=rhs, dc=dc: e.matmul(
                        bank[:], lhsT, rhs, start=(dc == 0), stop=(dc == 15)),
                        waits=w_mm if dc == 0 else (), inc=t if dc == 15 else None)
                mm_done = t.n
                bi4 = (hcg * 8 + step) % 4
                p.add("scalar", lambda e, bank=bank: e.activation(
                    hsq[:].bitcast(F32) if False else relu_t[:], bank[:], AF.Relu),
                    waits=[(t, mm_done)])
                hr_g[step % 2] = s.n
                relu_done = s.n
                w_sq = [(s, relu_done), (v, acc_z)]
                p.add("vector", lambda e: e.tensor_tensor(
                    sqf[:], relu_t[:], relu_t[:], ALU.mult), waits=w_sq)
                w_hr = [(g, hr_st[bi4])] if bi4 in hr_st else []
                p.add("vector", lambda e, bi4=bi4: e.tensor_copy(
                    hrb[:, bi4 * 512:(bi4 + 1) * 512], sqf[:]), waits=w_hr)
                sq_done = v.n
                dst = dr["hrs"][hcg, :, step * 512:(step + 1) * 512]
                p.add("gpsimd", lambda e, dst=dst, bi4=bi4: e.dma_start(
                    out=dst, in_=hrb[:, bi4 * 512:(bi4 + 1) * 512]),
                    waits=[(v, sq_done)], inc=g, inck=16)
                hr_st[bi4] = g.n
                p.add("vector", lambda e: e.tensor_tensor(
                    hsq[:], sqf[:], sqf[:], ALU.mult))
                p.add("vector", lambda e, step=step: e.tensor_tensor(
                    acc[:, step * 512:(step + 1) * 512],
                    acc[:, step * 512:(step + 1) * 512], hsq[:], ALU.add))
    acc_done = v.n
    for step in range(8):
        p.add("tensor", lambda e, step=step: e.matmul(
            P[2][0:1, :], onesc[:],
            acc[:, step * 512:(step + 1) * 512], start=True, stop=True),
            waits=[(v, max(acc_done, v.n))], inc=t)
        cs_done = t.n
        p.add("vector", lambda e: e.tensor_copy(ssr[:], P[2][0:1, :]),
              waits=[(t, cs_done)])
        ssr_done = v.n
        for q4 in range(4):
            p.add("tensor", lambda e, q4=q4: e.transpose(
                P[3][:, 0:1], ssr[0:1, q4 * 128:(q4 + 1) * 128], identf[0:1, 0:1]),
                waits=[(v, v.n)], inc=t)
            p.add("vector", lambda e, step=step, q4=q4: e.tensor_copy(
                rmsc[:, step * 4 + q4: step * 4 + q4 + 1], P[3][:, 0:1]),
                waits=[(t, t.n)])
    p.add("vector", lambda e: e.tensor_scalar(
        rmsc[:], rmsc[:], 1.0 / 8192.0, EPS_RMS, ALU.mult, ALU.add))
    p.add("scalar", lambda e: e.activation(rmsc[:], rmsc[:], AF.Ln), waits=[(v, v.n)])
    p.add("vector", lambda e: e.tensor_scalar_mul(rmsc[:], rmsc[:], -0.5), waits=[(s, s.n)])
    p.add("scalar", lambda e: e.activation(rmsc[:], rmsc[:], AF.Exp), waits=[(v, v.n)])
    p.add("gpsimd", lambda e: e.dma_start(out=dr["rms_s"][:, :], in_=rmsc[:]),
          waits=[(s, s.n)], inc=g, inck=16)
    p.add("gpsimd", lambda e, gn=g.n: e.wait_ge(g.h, gn))



def ph_c2b(ctx):
    nc, p, dr = ctx.nc, ctx.p, ctx.dram
    ld, t, v, s, g = (ctx.sem(n) for n in ("ld", "t", "v", "s", "g"))
    w2T = ctx.sb("c2b_w2T", (128, 128 * 512), BF16)
    hrt = ctx.sb("c2b_hrt", (128, 2 * 8192), BF16)
    rmsc = ctx.sb("c2b_rmsc", (128, 32), F32)
    yb = ctx.sb("c2b_y", (128, 4 * 512), BF16)
    P = [ctx.ps(f"c2b_ps{i}", (128, 512)) for i in range(4)]

    p.add("sync", lambda e: e.dma_start(out=rmsc[:], in_=dr["rms_s"][:, :]), inc=ld, inck=16)
    y_st = {}
    for half in range(2):
        d0 = half * 1024
        w_l = [(t, t.n)] if half > 0 else []
        for hc in range(64):
            for dc2 in range(2):
                src = dr["w2bf"][d0 + dc2 * 512: d0 + (dc2 + 1) * 512,
                                 hc * 128:(hc + 1) * 128]
                dst = w2T[:, (hc * 2 + dc2) * 512:(hc * 2 + dc2 + 1) * 512]
                p.add("sync", lambda e, dst=dst, src=src: e.dma_start_transpose(dst, src),
                      waits=w_l if (hc == 0 and dc2 == 0) else (), inc=ld, inck=16)
        w_ld = ld.n
        for tt in range(32):
            hb = hrt[:, (tt % 2) * 8192:(tt % 2 + 1) * 8192]
            w_h = [(ld, w_ld)]
            if tt >= 2:
                w_h.append((t, t.n))
            for hq in range(4):
                src = dr["hrs"][hq * 16:(hq + 1) * 16, :, tt * 128:(tt + 1) * 128]
                dst = hb[:, hq * 2048:(hq + 1) * 2048]
                p.add("sync", lambda e, dst=dst, src=src: e.dma_start(
                    out=dst.rearrange("p (hc i) -> p hc i", i=128),
                    in_=src.transpose([1, 0, 2])),
                    waits=w_h if hq == 0 else (), inc=ld, inck=16)
            hr_ld = ld.n
            for dc2 in range(2):
                bank = P[dc2 % 2]
                w_mm = [(ld, hr_ld)]
                if (dc2 % 2) in y_st:
                    w_mm.append((v, y_st[dc2 % 2][0]))
                for hc in range(64):
                    lhsT = hb[:, hc * 128:(hc + 1) * 128]
                    rhs = w2T[:, (hc * 2 + dc2) * 512:(hc * 2 + dc2 + 1) * 512]
                    p.add("tensor", lambda e, bank=bank, lhsT=lhsT, rhs=rhs, hc=hc: e.matmul(
                        bank[:], lhsT, rhs, start=(hc == 0), stop=(hc == 63)),
                        waits=w_mm if hc == 0 else (), inc=t if hc == 63 else None)
                mm_done = t.n
                w_e = [(t, mm_done)]
                if (dc2 % 2) in y_st:
                    w_e.append((g, y_st[dc2 % 2][1]))
                p.add("vector", lambda e, bank=bank, dc2=dc2, tt=tt: e.tensor_scalar(
                    yb[:, dc2 * 512:(dc2 + 1) * 512], bank[:],
                    rmsc[:, tt:tt + 1], None, ALU.mult), waits=w_e)
                ev_done = v.n
                dst = dr["ysb"][tt * 128:(tt + 1) * 128, d0 + dc2 * 512: d0 + (dc2 + 1) * 512]
                p.add("gpsimd", lambda e, dst=dst, dc2=dc2: e.dma_start(
                    out=dst, in_=yb[:, dc2 * 512:(dc2 + 1) * 512]),
                    waits=[(v, ev_done)], inc=g, inck=16)
                y_st[dc2 % 2] = (ev_done, g.n)
    p.add("gpsimd", lambda e, gn=g.n: e.wait_ge(g.h, gn))



def ph_c3(ctx):
    nc, p, dr = ctx.nc, ctx.p, ctx.dram
    ld, t, v, s, g = (ctx.sem(n) for n in ("ld", "t", "v", "s", "g"))
    yf = ctx.sb("c3_y", (128, 2 * 2048), BF16)
    h3t = ctx.sb("c3_h3", (128, 2 * 2048), F32)
    pre = ctx.sb("c3_pre", (128, 2048), F32)
    dump = ctx.sb("c3_dump", (128, 2048), F32)
    outt = ctx.sb("c3_out", (128, 2048), F32)
    sm = {k: ctx.sb("c3s_" + k, (128, 1), F32) for k in
          ("sumb", "negm", "ssq", "vpe", "rstd", "mean")}
    sm["dump"] = dump

    out_st = 0
    for tt in range(32):
        b, row0 = divmod(tt * 128, S)
        bi = tt % 2
        w_l = [(v, v.n)] if tt >= 2 else []
        p.add("sync", lambda e, tt=tt, bi=bi: e.dma_start(
            out=yf[:, bi * 2048:(bi + 1) * 2048],
            in_=dr["ysb"][tt * 128:(tt + 1) * 128, :]), waits=w_l, inc=ld, inck=16)
        p.add("sync", lambda e, b=b, row0=row0, bi=bi: e.dma_start(
            out=h3t[:, bi * 2048:(bi + 1) * 2048],
            in_=dr["h3f"][b, row0:row0 + 128, :]), inc=ld, inck=16)
        t_ld = ld.n
        p.add("vector", lambda e, bi=bi: e.tensor_copy(
            pre[:], yf[:, bi * 2048:(bi + 1) * 2048]), waits=[(ld, t_ld)])
        p.add("vector", lambda e, bi=bi: e.tensor_tensor(
            pre[:], pre[:], h3t[:, bi * 2048:(bi + 1) * 2048], ALU.add))
        _ln_stats(ctx, pre[:], 2048, sm, EPS_LN)
        p.add("vector", lambda e: e.tensor_scalar(
            outt[:], pre[:], sm["mean"][:, 0:1], sm["rstd"][:, 0:1],
            ALU.subtract, ALU.mult))
        o_done = v.n
        p.add("gpsimd", lambda e, b=b, row0=row0: e.dma_start(
            out=dr["out"][b, row0:row0 + 128, :], in_=outt[:]),
            waits=[(v, o_done)], inc=g, inck=16)
        p.add("vector", lambda e: e.nop(), waits=[(g, g.n)])
    p.add("gpsimd", lambda e, gn=g.n: e.wait_ge(g.h, gn))


def build(group_phases, for_run=True):
    nc = bass.Bass("TRN2", target_bir_lowering=False)
    es = ExitStack()
    ctx = Ctx(nc, es, group_phases)
    prog = ctx.p
    with es:
        in_names, out_names = declare_dram(ctx, group_phases)
        block = es.enter_context(nc.Block())
        first = True
        for ph in group_phases:
            if not first:
                prog.barrier(ctx.sem("bar"))
            first = False
            with ExitStack() as phstack:
                ctx.phase_stack = phstack
                globals()["ph_" + ph](ctx)
                ctx.phase_stack = None
        prog.emit(block)
    return nc, in_names, out_names



N_CORES = 8
GROUPS = [list(ALL_PHASES)]


def kernel(**inputs):
    consts = host_consts()
    xfull = np.ascontiguousarray(inputs["L0_attn_x"])
    per_core = []
    for c in range(N_CORES):
        d = {"x": np.ascontiguousarray(xfull[2 * c:2 * c + 2])}
        for k, v in inputs.items():
            if k != "L0_attn_x":
                d[k] = np.ascontiguousarray(v)
        d.update(consts)
        per_core.append(d)
    outs = None
    for grp in GROUPS:
        nc, in_names, out_names = build(grp)
        in_maps = [{n: per_core[c][n] for n in in_names} for c in range(N_CORES)]
        res = run_bass_kernel_spmd(nc, in_maps, core_ids=list(range(N_CORES)))
        for c in range(N_CORES):
            for n in out_names:
                per_core[c][n] = res.results[c][n]
        if "out" in out_names:
            outs = [res.results[c]["out"] for c in range(N_CORES)]
    return np.concatenate(outs, axis=0)


# version 10
# speedup vs baseline: 1.2267x; 1.1422x over previous
import numpy as np
import ml_dtypes
import concourse.bass as bass
import concourse.mybir as mybir
from concourse.bass_utils import run_bass_kernel_spmd
from contextlib import ExitStack

F32 = mybir.dt.float32
BF16 = mybir.dt.bfloat16
AF = mybir.ActivationFunctionType
ALU = mybir.AluOpType
AX = mybir.AxisListType

B_LOC, S, D, H, HD = 2, 2048, 2048, 16, 128
TOK = B_LOC * S
ST = 512
NST = S // ST
NDC = D // 128
EPS_LN = 1e-5
EPS_RMS = float(np.finfo(np.float32).eps)
MAGIC = 12582912.0

ENGINES = ("sync", "tensor", "vector", "scalar", "gpsimd")

ALL_PHASES = ("prep", "a1", "a2", "b1", "b2", "c2a", "c2b", "c3")

bfnp = ml_dtypes.bfloat16


def host_consts():
    i = np.arange(128)
    ang = 2.0 * np.pi * np.outer(i, i) / 128.0
    onescol = np.zeros((128, 16, 16), np.float32)
    for ec in range(16):
        onescol[:, ec, ec] = 1.0
    hsel = np.zeros((16, 16, 128), np.float32)
    for h in range(16):
        hsel[h, h, :] = 1.0
    c = {
        "c_fre": np.cos(ang),
        "c_fim": -np.sin(ang),
        "c_ifre": np.cos(ang) / 128.0,
        "c_ifim": -np.sin(ang) / 128.0,
        "c_ident": np.eye(128, dtype=np.float32),
        "c_onescol": onescol.reshape(128, 256),
        "c_hsel": hsel.reshape(16, 2048),
        "c_ones": np.ones((128, 512), np.float32),
        "c_onesr": np.ones((1, 128), np.float32),
    }
    out = {}
    for k, v in c.items():
        out[k] = v.astype(bfnp)
    out["c_identf"] = np.eye(128, dtype=np.float32)
    out["c_onesf"] = np.ones((128, 512), np.float32)
    out["c_onesrf"] = np.ones((1, 128), np.float32)
    return out


CONST_SPECS = {k: (v.shape, BF16 if v.dtype == bfnp else F32)
               for k, v in host_consts().items()}

SCRATCH = {
    "xbf":    ((B_LOC, S, D), BF16, "prep", ("a1",)),
    "wqbf":   ((D, D), BF16, "prep", ("a1",)),
    "wkvbf":  ((D, D), BF16, "prep", ("a1",)),
    "wobf":   ((D, D), BF16, "prep", ("a2",)),
    "wqkvbf": ((3 * D, D), BF16, "prep", ("b1",)),
    "wo1bf":  ((D, D), BF16, "prep", ("b2",)),
    "wfcbf":  ((8192, D), BF16, "prep", ("prep",)),
    "wprojbf": ((D, 8192), BF16, "prep", ("prep",)),
    "w1bf":   ((8192, D), BF16, "prep", ("c2a",)),
    "w2bf":   ((D, 8192), BF16, "prep", ("c2b",)),
    "ffnw_s": ((128, 16 * 208), BF16, "prep", ("a2",)),
    "dgn_s":  ((72, D), BF16, "prep", ("a2",)),
    "k1_s":   ((1, 136), BF16, "prep", ("a2",)),
    "gate_s": ((2, 8), F32, "prep", ("a2",)),
    "sn_s":   ((8, 8), F32, "prep", ("prep",)),
    "cc_s":   ((8, 8), F32, "prep", ("prep",)),
    "mn2_s":  ((8, 1), F32, "prep", ("prep",)),
    "vhT":    ((B_LOC, NST, D, ST), BF16, "a1", ("a2",)),
    "h2f":    ((B_LOC, S, D), F32, "a2", ("b2",)),
    "h2bf":   ((B_LOC, S, D), BF16, "a2", ("b1",)),
    "qkvbf":  ((B_LOC, S, 3 * D), BF16, "b1", ("b2",)),
    "h3f":    ((B_LOC, S, D), F32, "b2", ("c3",)),
    "h3bf":   ((B_LOC, S, D), BF16, "b2", ("c2a",)),
    "hrs":    ((64, 128, TOK), BF16, "c2a", ("c2b",)),
    "rms_s":  ((128, 32), F32, "c2a", ("c2b",)),
    "ysb":    ((TOK, D), BF16, "c2b", ("c3",)),
}

INPUT_SPECS = {
    "x": ((B_LOC, S, D), F32),
    "L0_attn_W_q": ((D, D), F32),
    "L0_attn_W_kv": ((D, D), F32),
    "L0_attn_W_o": ((D, D), F32),
    "L0_ffn_select_weight": ((8, D, 8), F32),
    "L0_ffn_select_mean_diff": ((8, D), F32),
    "L0_ffn_imp_center": ((8,), F32),
    "L0_ffn_imp_slope": ((8,), F32),
    "L0_ffn_lora_update": ((8, D, 8), F32),
    "L0_ffn_lora_degen": ((8, D, 8), F32),
    "L0_ffn_bias_w": ((8, D), F32),
    "L0_ffn_debias_w": ((8, D), F32),
    "L1_attn_W_qkv": ((3 * D, D), F32),
    "L1_attn_W_o": ((D, D), F32),
    "L1_ffn_W_fc": ((8192, D), F32),
    "L1_ffn_W_proj": ((D, 8192), F32),
}

PHASE_INPUTS = {
    "prep": ("x", "L0_attn_W_q", "L0_attn_W_kv", "L0_attn_W_o",
             "L0_ffn_select_weight", "L0_ffn_select_mean_diff",
             "L0_ffn_imp_center", "L0_ffn_imp_slope", "L0_ffn_lora_update",
             "L0_ffn_lora_degen", "L0_ffn_bias_w", "L0_ffn_debias_w",
             "L1_attn_W_qkv", "L1_attn_W_o", "L1_ffn_W_fc", "L1_ffn_W_proj"),
    "a1": (), "a2": ("x",), "b1": (), "b2": (), "c2a": (), "c2b": (), "c3": (),
}
PHASE_CONSTS = {
    "prep": ("c_ident", "c_identf"),
    "a1": ("c_fre", "c_fim", "c_ifre", "c_ifim", "c_onescol", "c_hsel", "c_onesf"),
    "a2": ("c_onesr", "c_onesrf", "c_identf", "c_ident", "c_onesf"),
    "b1": (),
    "b2": ("c_ones", "c_onesr"),

    "c2a": ("c_onesf", "c_identf"),
    "c2b": (),
    "c3": (),
}


class Sem:
    def __init__(self, nc, es, name):
        self.h = es.enter_context(nc.semaphore(name))
        self.n = 0


class Prog:

    def __init__(self, ctx=None):
        self.ops = []
        self.ctx = ctx
        self.last = {}

    def add(self, eng, fn, waits=(), inc=None, inck=1, chain=None):
        if chain is None:
            chain = eng in ("vector", "scalar")
        waits = list(waits)
        if eng in ("vector", "scalar"):
            dflt = self.ctx.sem("v" if eng == "vector" else "s")
            if chain and eng in self.last:
                waits.append(self.last[eng])
            if inc is None:
                inc = dflt
                inck = 1
        self.ops.append((eng, fn, tuple(waits), inc, inck))
        if inc is not None:
            inc.n += inck
            if eng in ("vector", "scalar"):
                self.last[eng] = (inc, inc.n)

    def barrier(self, bar):
        target = bar.n + 5
        cur = [(self.ctx.sem(n), self.ctx.sem(n).n)
               for n in ("ld", "t", "v", "s", "g")]
        for g in ENGINES:
            self.ops.append((g, lambda e: e.nop(), tuple(cur), bar, 1))
            bar.n += 1
            self.ops.append((g, None, ((bar, target),), None, 1))
        assert bar.n == target
        self.last = {}

    def check(self):
        import collections
        qs = {g: collections.deque() for g in ENGINES}
        for op in self.ops:
            qs[op[0]].append(op)
        counts = collections.defaultdict(int)
        progress = True
        while progress:
            progress = False
            for g in ENGINES:
                while qs[g]:
                    (eng, fn, waits, inc, inck) = qs[g][0]
                    if any(counts[id(sem)] < val for (sem, val) in waits):
                        break
                    if inc is not None:
                        counts[id(inc)] += inck
                    qs[g].popleft()
                    progress = True
        stuck = {g: len(qs[g]) for g in ENGINES if qs[g]}
        if stuck:
            msgs = []
            for g in ENGINES:
                if not qs[g]:
                    continue
                (eng, fn, waits, inc, inck) = qs[g][0]
                bad = [(self._semname(sem), val, counts[id(sem)])
                       for (sem, val) in waits if counts[id(sem)] < val]
                msgs.append(f"{g}: {len(qs[g])} ops stuck; head waits {bad}")
            raise RuntimeError("DEADLOCK in schedule:\n" + "\n".join(msgs))

    def _semname(self, sem):
        for k, v in self.ctx.sems.items():
            if v is sem:
                return k
        return "?"

    def emit(self, block):
        self.check()
        for engname in ENGINES:
            items = [(f, w, i, k) for (g, f, w, i, k) in self.ops if g == engname]
            if not items:
                continue

            def mk(items):
                def run(e):
                    for (f, w, inc, k) in items:
                        for (sem, val) in w:
                            e.wait_ge(sem.h, val)
                        if f is None:
                            continue
                        ins = f(e)
                        if inc is not None:
                            assert ins is not None
                            ins.then_inc(inc.h, k)
                return run
            getattr(block, engname)(mk(items))


class Ctx:
    def __init__(self, nc, es, phases):
        self.nc = nc
        self.es = es
        self.p = Prog(self)
        self.phases = phases
        self.dram = {}
        self.sems = {}
        for name in ("ld", "ld2", "ld3", "ld4", "ld5", "st4", "st5", "t", "v", "s", "g", "bar"):
            self.sems[name] = Sem(nc, es, "sem_" + name)

    def sem(self, n):
        return self.sems[n]

    def sb(self, name, shape, dtype, stack=None):
        st = stack or getattr(self, "phase_stack", None) or self.es
        return st.enter_context(self.nc.sbuf_tensor(name, list(shape), dtype))

    def ps(self, name, shape, dtype=F32, stack=None):
        st = stack or getattr(self, "phase_stack", None) or self.es
        return st.enter_context(self.nc.psum_tensor(name, list(shape), dtype))


def declare_dram(ctx, group_phases):
    nc = ctx.nc
    phs = set(group_phases)
    needed_inputs = set()
    needed_consts = set()
    for ph in phs:
        needed_inputs.update(PHASE_INPUTS[ph])
        needed_consts.update(PHASE_CONSTS[ph])
    for name in sorted(needed_inputs):
        shape, dt = INPUT_SPECS[name]
        ctx.dram[name] = nc.dram_tensor(name, list(shape), dt, kind="ExternalInput").ap()
    for name in sorted(needed_consts):
        shape, dt = CONST_SPECS[name]
        ctx.dram[name] = nc.dram_tensor(name, list(shape), dt, kind="ExternalInput").ap()
    ext_in, ext_out = [], []
    for name, (shape, dt, producer, consumers) in SCRATCH.items():
        prod_here = producer in phs
        cons_here = any(c in phs for c in consumers)
        cons_later = any(c not in phs for c in consumers)
        if not prod_here and not cons_here:
            continue
        if prod_here and not cons_later:
            kind = "Internal"
        elif prod_here:
            kind = "ExternalOutput"
            ext_out.append(name)
        else:
            kind = "ExternalInput"
            ext_in.append(name)
        ctx.dram[name] = nc.dram_tensor(name, list(shape), dt, kind=kind).ap()
    if "c3" in phs:
        ctx.dram["out"] = nc.dram_tensor("out", [B_LOC, S, D], F32, kind="ExternalOutput").ap()
        ext_out.append("out")
    import os as _os
    if _os.environ.get("B2_DBG") and "b2" in phs:
        ctx.dram["dbg"] = nc.dram_tensor("dbg", [128, 16 * 2048], BF16, kind="ExternalOutput").ap()
        ext_out.append("dbg")
    return sorted(needed_inputs | needed_consts) + ext_in, ext_out



def ph_prep(ctx):
    nc, p, dr = ctx.nc, ctx.p, ctx.dram
    g = ctx.sem("g")
    ld = ctx.sem("ld")
    v = ctx.sem("v")
    t = ctx.sem("t")
    s = ctx.sem("s")

    ld2 = ctx.sem("ld2")
    tern_casts = [
        (dr["wfcbf"][0:4096], dr["L1_ffn_W_fc"][0:4096]),
        (dr["wfcbf"][4096:8192], dr["L1_ffn_W_fc"][4096:8192]),
        (dr["wprojbf"], dr["L1_ffn_W_proj"]),
    ]
    for (o, i) in tern_casts:
        p.add("gpsimd", lambda e, o=o, i=i: e.dma_start(out=o, in_=i), inc=ld2, inck=16)
    tern_cast_done = ld2.n
    casts = [
        (dr["xbf"][0], dr["x"][0]), (dr["xbf"][1], dr["x"][1]),
        (dr["wqbf"], dr["L0_attn_W_q"]), (dr["wkvbf"], dr["L0_attn_W_kv"]),
        (dr["wobf"], dr["L0_attn_W_o"]), (dr["wo1bf"], dr["L1_attn_W_o"]),
        (dr["wqkvbf"][0:3072], dr["L1_attn_W_qkv"][0:3072]),
        (dr["wqkvbf"][3072:6144], dr["L1_attn_W_qkv"][3072:6144]),
        (dr["dgn_s"][64:72], dr["L0_ffn_bias_w"]),
    ]
    for (o, i) in casts:
        p.add("gpsimd", lambda e, o=o, i=i: e.dma_start(out=o, in_=i), inc=g, inck=16)
    cast_done = g.n

    tern = ctx.sb("tern_in", (128, 2 * 8192), BF16)
    tw1 = ctx.sb("tw1", (128, 2 * 8192), BF16)
    tred = ctx.sb("tred", (128, 128), F32)
    tsc = ctx.sb("tsc", (128, 128), BF16)
    trec = ctx.sb("trec", (128, 128), F32)
    tscf = ctx.sb("tscf", (128, 128), F32)
    trat = ctx.sb("trat", (128, 8192), BF16)
    ternf = ctx.sb("ternf", (128, 8192), F32)

    jobs = []
    wfc3 = dr["wfcbf"].rearrange("(n p) d -> n p d", p=128)
    w1b3 = dr["w1bf"].rearrange("(n p) d -> n p d", p=128)
    for i in range(64):
        jobs.append((wfc3[i], w1b3[i], 2048))
    wpr3 = dr["wprojbf"].rearrange("(n p) d -> n p d", p=128)
    w2b3 = dr["w2bf"].rearrange("(n p) d -> n p d", p=128)
    for i in range(16):
        jobs.append((wpr3[i], w2b3[i], 8192))

    in_guard = {}
    st_guard = {}
    for i, (src, dst, F) in enumerate(jobs):
        bi = i % 2
        lsem = ctx.sem("ld4" if bi == 0 else "ld5")
        ssem = ctx.sem("st4")
        ti = tern[:, bi * 8192: bi * 8192 + F]
        to = tw1[:, bi * 8192: bi * 8192 + F]
        G = F // 64
        w_in = [(ld2, tern_cast_done), (v, in_guard.get(bi, 0))]
        p.add("sync", lambda e, ti=ti, src=src: e.dma_start(out=ti, in_=src),
              waits=w_in, inc=lsem, inck=16)
        tile_ld = lsem.n
        ti3 = ti.rearrange("p (gg o) -> p gg o", o=64)
        tf3 = ternf[:, 0:F].rearrange("p (gg o) -> p gg o", o=64)
        to3 = to.rearrange("p (gg o) -> p gg o", o=64)
        p.add("vector", lambda e, ti3=ti3, G=G: e.tensor_reduce(
            tred[:, 0:G].unsqueeze(2), ti3, AX.X, ALU.add,
            apply_absolute_value=True), waits=[(lsem, tile_ld)])
        p.add("vector", lambda e, G=G: e.tensor_scalar(
            tsc[:, 0:G], tred[:, 0:G], 1.0 / 64.0, 1e-8, ALU.mult, ALU.max))
        p.add("vector", lambda e, G=G: e.reciprocal(trec[:, 0:G], tsc[:, 0:G]))
        p.add("vector", lambda e, G=G: e.tensor_copy(tscf[:, 0:G], tsc[:, 0:G]))
        p.add("vector", lambda e, ti3=ti3, tf3=tf3, G=G: e.tensor_tensor(
            tf3, ti3, trec[:, 0:G].unsqueeze(2).broadcast_to([128, G, 64]),
            ALU.mult))
        in_guard[bi] = v.n
        p.add("vector", lambda e, F=F: e.tensor_copy(trat[:, 0:F], ternf[:, 0:F]))
        p.add("vector", lambda e, F=F: e.tensor_scalar(
            ternf[:, 0:F], trat[:, 0:F], MAGIC, MAGIC, ALU.add, ALU.subtract))
        p.add("vector", lambda e, F=F: e.tensor_scalar(
            ternf[:, 0:F], ternf[:, 0:F], 1.0, -1.0, ALU.min, ALU.max))
        p.add("vector", lambda e, tf3=tf3, to3=to3, G=G: e.tensor_tensor(
            to3, tf3, tscf[:, 0:G].unsqueeze(2).broadcast_to([128, G, 64]),
            ALU.mult), waits=[(ssem, st_guard.get(bi, 0))])
        out_done = v.n
        p.add("scalar", lambda e, to=to, dst=dst: e.dma_start(out=dst, in_=to),
              waits=[(v, out_done)], inc=ssem, inck=16)
        st_guard[bi] = ssem.n
    tern_v_end = v.n

    mu_f = ctx.sb("mu_f", (8, D), F32)
    deb_f = ctx.sb("deb_f", (8, D), F32)
    musq = ctx.sb("musq", (8, D), F32)
    mn2 = ctx.sb("mn2", (8, 1), F32)
    slp0 = ctx.sb("slp0", (1, 8), F32)
    slpc = ctx.sb("slpc", (1, 8), F32)
    p.add("sync", lambda e: e.dma_start(out=mu_f[:], in_=dr["L0_ffn_select_mean_diff"][:, :]), inc=ld, inck=16)
    p.add("sync", lambda e: e.dma_start(out=deb_f[:], in_=dr["L0_ffn_debias_w"][:, :]), inc=ld, inck=16)
    p.add("sync", lambda e: e.dma_start(out=slp0[:], in_=dr["L0_ffn_imp_slope"].unsqueeze(0)), inc=ld, inck=16)
    p.add("sync", lambda e: e.dma_start(out=slpc[:], in_=dr["L0_ffn_imp_center"].unsqueeze(0)), inc=ld, inck=16)
    small_ld = ld.n
    p.add("vector", lambda e: e.tensor_tensor(musq[:], mu_f[:], mu_f[:], ALU.mult),
          waits=[(ld, small_ld)], inc=v)
    p.add("vector", lambda e: e.tensor_reduce(mn2[:], musq[:], AX.X, ALU.add), inc=v)
    p.add("vector", lambda e: e.tensor_tensor(slpc[:], slp0[:], slpc[:], ALU.mult), inc=v)
    p.add("gpsimd", lambda e: e.dma_start(out=dr["gate_s"][0:1, :], in_=slp0[:]),
          waits=[(v, v.n)], inc=g, inck=16)
    p.add("gpsimd", lambda e: e.dma_start(out=dr["gate_s"][1:2, :], in_=slpc[:]),
          waits=[(v, v.n)], inc=g, inck=16)
    p.add("gpsimd", lambda e: e.dma_start(out=dr["mn2_s"][:, :], in_=mn2[:]),
          waits=[(v, v.n)], inc=g, inck=16)
    small_v = v.n

    ffnw = ctx.sb("ffnw", (128, 16 * 208), BF16)
    muhi = ctx.sb("muhi", (128, 16 * 8), BF16)
    debhi = ctx.sb("debhi", (128, 16 * 8), BF16)
    ident_f = ctx.sb("ident_f", (128, 128), F32)
    ident_b = ctx.sb("ident_b", (128, 128), BF16)
    p.add("sync", lambda e: e.dma_start(out=ident_f[:], in_=dr["c_identf"][:, :]), inc=ld, inck=16)
    p.add("sync", lambda e: e.dma_start(out=ident_b[:], in_=dr["c_ident"][:, :]), inc=ld, inck=16)
    ident_ld = ld.n
    ps_a = ctx.ps("pp_a", (128, 512))
    ps_b = ctx.ps("pp_b", (128, 512))
    mulo_t = ctx.sb("mulo_t", (128, 8), BF16)
    muf32 = ctx.sb("muf32", (128, 8), F32)
    hif32 = ctx.sb("hif32", (128, 8), F32)
    for which, srcf, hidst in (("mu", mu_f, muhi), ("deb", deb_f, debhi)):
        for dc in range(16):
            pst = ps_a[:, 0:8] if dc % 2 == 0 else ps_b[:, 0:8]
            p.add("tensor", lambda e, srcf=srcf, dc=dc, pst=pst: e.transpose(
                pst, srcf[:, dc * 128:(dc + 1) * 128], ident_f[0:8, 0:8]),
                waits=[(ld, ident_ld), (v, v.n)], inc=t)
            p.add("vector", lambda e, pst=pst: e.tensor_copy(muf32[:], pst),
                  waits=[(t, t.n)], inc=v)
            p.add("vector", lambda e, hidst=hidst, dc=dc: e.tensor_copy(
                hidst[:, dc * 8:(dc + 1) * 8], muf32[:]), inc=v)
            if which == "mu":
                p.add("vector", lambda e, dc=dc: e.tensor_copy(
                    hif32[:], muhi[:, dc * 8:(dc + 1) * 8]), inc=v)
                p.add("vector", lambda e: e.tensor_tensor(
                    mulo_t[:], muf32[:], hif32[:], ALU.subtract), inc=v)
                p.add("vector", lambda e, dc=dc: e.tensor_scalar_mul(
                    ffnw[:, dc * 208 + 64: dc * 208 + 72], muhi[:, dc * 8:(dc + 1) * 8], -2.0), inc=v)
                p.add("vector", lambda e, dc=dc: e.tensor_scalar_mul(
                    ffnw[:, dc * 208 + 200: dc * 208 + 208], mulo_t[:], -2.0), inc=v)
    mu_v = v.n

    wsl_t = ctx.sb("wsl_t", (128, 8), F32)
    for n in range(8):
        for dc in range(16):
            src = dr["L0_ffn_select_weight"][n, dc * 128:(dc + 1) * 128, :]
            p.add("sync", lambda e, src=src: e.dma_start(out=wsl_t[:], in_=src),
                  waits=[(v, v.n)], inc=ld, inck=16)
            p.add("vector", lambda e, dc=dc, n=n: e.tensor_copy(
                ffnw[:, dc * 208 + n * 8: dc * 208 + n * 8 + 8], wsl_t[:]),
                waits=[(ld, ld.n)], inc=v)
            p.add("vector", lambda e, dc=dc, n=n: e.tensor_copy(
                hif32[:], ffnw[:, dc * 208 + n * 8: dc * 208 + n * 8 + 8]), inc=v)
            p.add("vector", lambda e, dc=dc, n=n: e.tensor_tensor(
                ffnw[:, dc * 208 + 136 + n * 8: dc * 208 + 136 + n * 8 + 8], wsl_t[:],
                hif32[:], ALU.subtract), inc=v)
    for n in range(8):
        for dc in range(16):
            src = dr["L0_ffn_lora_update"][n, dc * 128:(dc + 1) * 128, :]
            p.add("sync", lambda e, src=src: e.dma_start(out=wsl_t[:], in_=src),
                  waits=[(v, v.n)], inc=ld, inck=16)
            p.add("vector", lambda e, dc=dc, n=n: e.tensor_copy(
                ffnw[:, dc * 208 + 72 + n * 8: dc * 208 + 72 + n * 8 + 8], wsl_t[:]),
                waits=[(ld, ld.n)], inc=v)
    ffnw_v = v.n
    p.add("gpsimd", lambda e: e.dma_start(out=dr["ffnw_s"][:, :], in_=ffnw[:]),
          waits=[(v, ffnw_v)], inc=g, inck=16)

    snneg = ctx.sb("snneg", (8, 8), F32)
    ccneg = ctx.sb("ccneg", (8, 8), F32)
    for dstbuf, colbase, vecbuf in ((snneg, 0, muhi), (ccneg, 72, debhi)):
        for n in range(8):
            for dc in range(16):
                lhsT = ffnw[:, dc * 208 + colbase + n * 8: dc * 208 + colbase + n * 8 + 8]
                rhs = vecbuf[:, dc * 8 + n: dc * 8 + n + 1]
                p.add("tensor", lambda e, lhsT=lhsT, rhs=rhs, dc=dc: e.matmul(
                    ps_a[0:8, 0:1], lhsT, rhs, start=(dc == 0), stop=(dc == 15)),
                    waits=[(v, ffnw_v)] if dc == 0 else (), inc=t if dc == 15 else None)
            p.add("vector", lambda e, dstbuf=dstbuf, n=n: e.tensor_scalar_mul(
                dstbuf[:, n:n + 1], ps_a[0:8, 0:1], -1.0), waits=[(t, t.n)], inc=v)
            p.add("tensor", lambda e: e.nop(), waits=[(v, v.n)])
    p.add("gpsimd", lambda e: e.dma_start(out=dr["sn_s"][:, :], in_=snneg[:]),
          waits=[(v, v.n)], inc=g, inck=16)
    p.add("gpsimd", lambda e: e.dma_start(out=dr["cc_s"][:, :], in_=ccneg[:]),
          waits=[(v, v.n)], inc=g, inck=16)
    sncc_g = g.n
    p.add("gpsimd", lambda e: e.dma_start(
        out=dr["k1_s"][0:1, 0:64].rearrange("o (n s) -> o n s", s=8),
        in_=dr["sn_s"].transpose([1, 0]).unsqueeze(0),
        allow_slow_non_contiguous=True),
        waits=[(g, sncc_g)], inc=g, inck=16)
    p.add("gpsimd", lambda e: e.dma_start(
        out=dr["k1_s"][0:1, 64:72], in_=dr["mn2_s"].rearrange("n o -> (n o)").unsqueeze(0)),
        inc=g, inck=16)
    p.add("gpsimd", lambda e: e.dma_start(
        out=dr["k1_s"][0:1, 72:136].rearrange("o (n h) -> o n h", h=8),
        in_=dr["cc_s"].transpose([1, 0]).unsqueeze(0),
        allow_slow_non_contiguous=True),
        inc=g, inck=16)

    st4 = ctx.sem("st4")
    p.add("gpsimd", lambda e, n4=st4.n, h4=st4.h: e.wait_ge(h4, n4))

    dgrow = ctx.sb("dgrow", (8, 16 * 128), BF16)
    ldg_f = ctx.sb("ldg_f", (128, 8), F32)
    ldg_b = ctx.sb("ldg_b", (128, 8), BF16)
    ps_tb = ctx.ps("pp_tb", (128, 1024), BF16)
    for n in range(8):
        for dc in range(16):
            src = dr["L0_ffn_lora_degen"][n, dc * 128:(dc + 1) * 128, :]
            p.add("sync", lambda e, src=src: e.dma_start(out=ldg_f[:], in_=src),
                  waits=[(v, v.n)], inc=ld, inck=16)
            p.add("vector", lambda e: e.tensor_copy(ldg_b[:], ldg_f[:]),
                  waits=[(ld, ld.n)], inc=v)
            pst_b = ps_tb[0:8, (dc % 2) * 512:(dc % 2) * 512 + 128]
            p.add("tensor", lambda e, pst_b=pst_b: e.transpose(pst_b, ldg_b[:], ident_b[:]),
                  waits=[(v, v.n)], inc=t)
            p.add("vector", lambda e, dc=dc, pst_b=pst_b: e.tensor_copy(
                dgrow[:, dc * 128:(dc + 1) * 128], pst_b), waits=[(t, t.n)], inc=v)
        p.add("gpsimd", lambda e, n=n: e.dma_start(
            out=dr["dgn_s"][n * 8:(n + 1) * 8, :], in_=dgrow[:]),
            waits=[(v, v.n)], inc=g, inck=16)
    p.add("gpsimd", lambda e, gn=g.n: e.wait_ge(g.h, gn))



def ph_a1(ctx):
    nc, p, dr = ctx.nc, ctx.p, ctx.dram
    ld, t, v, s, g = (ctx.sem(n) for n in ("ld", "t", "v", "s", "g"))

    wq = ctx.sb("a1_wq", (128, 16 * 2048), BF16)
    wkv = ctx.sb("a1_wkv", (128, 16 * 2048), BF16)
    xT = ctx.sb("a1_xT", (128, 16 * 512), BF16)
    qsb = ctx.sb("a1_q", (128, 16 * 512), BF16)
    kvsb = ctx.sb("a1_kv", (128, 16 * 512), BF16)
    sqb = ctx.sb("a1_sq", (128, 2 * 512), BF16)
    rstdq = ctx.sb("a1_rstdq", (16, 512), BF16)
    rstdkv = ctx.sb("a1_rstdkv", (16, 512), BF16)
    rtmp = ctx.sb("a1_rtmp", (16, 512), F32)
    qn = ctx.sb("a1_qn", (128, 512), BF16)
    kvn = ctx.sb("a1_kvn", (128, 512), F32)
    kvcs = ctx.sb("a1_kvcs", (128, 512), F32)
    kvcsb = ctx.sb("a1_kvcsb", (128, 512), BF16)
    carry = ctx.sb("a1_carry", (128, 16), F32)
    ffre = ctx.sb("a1_ffre", (128, 128), BF16)
    ffim = ctx.sb("a1_ffim", (128, 128), BF16)
    fire = ctx.sb("a1_fire", (128, 128), BF16)
    fiim = ctx.sb("a1_fiim", (128, 128), BF16)
    onescol = ctx.sb("a1_ones", (128, 256), BF16)
    hsel = ctx.sb("a1_hsel", (16, 2048), BF16)
    onesd = ctx.sb("a1_onesd", (128, 512), F32)
    bcb = ctx.sb("a1_bcb", (128, 512), BF16)
    qfre = ctx.sb("a1_qfre", (128, 512), BF16)
    qfim = ctx.sb("a1_qfim", (128, 512), BF16)
    kfre = ctx.sb("a1_kfre", (128, 512), BF16)
    kfim = ctx.sb("a1_kfim", (128, 512), BF16)
    qvre = ctx.sb("a1_qvre", (128, 512), BF16)
    qvim = ctx.sb("a1_qvim", (128, 512), BF16)
    tmpc = ctx.sb("a1_tmpc", (128, 512), F32)
    vhsb = ctx.sb("a1_vh", (128, 2 * 512), BF16)

    P = [ctx.ps(f"a1_ps{i}", (128, 512)) for i in range(8)]

    for dst, nmc in ((ffre, "c_fre"), (ffim, "c_fim"), (fire, "c_ifre"), (fiim, "c_ifim"),
                     (onescol, "c_onescol"), (hsel, "c_hsel"), (onesd, "c_onesf")):
        p.add("sync", lambda e, dst=dst, nmc=nmc: e.dma_start(out=dst[:], in_=dr[nmc][:, :]),
              inc=ld, inck=16)
    for wsb, wd in ((wq, "wqbf"), (wkv, "wkvbf")):
        for dc in range(16):
            for eb in range(4):
                src = dr[wd][eb * 512:(eb + 1) * 512, dc * 128:(dc + 1) * 128]
                dst = wsb[:, dc * 2048 + eb * 512: dc * 2048 + (eb + 1) * 512]
                p.add("sync", lambda e, dst=dst, src=src: e.dma_start_transpose(dst, src),
                      inc=ld, inck=16)
    const_ld = ld.n

    def xT_ap(buf_i, dc):
        return xT[:, dc * 512: (dc + 1) * 512]

    n_steps = B_LOC * NST
    evq_s = {}
    step_t_end = {}

    for step in range(n_steps):
        b, st = divmod(step, NST)
        bi = step % 2
        w_l = [(ld, const_ld)]
        if step >= 1:
            w_l.append((t, step_t_end[step - 1]))
        for dc in range(16):
            src = dr["xbf"][b, st * 512:(st + 1) * 512, dc * 128:(dc + 1) * 128]
            p.add("sync", lambda e, d=xT_ap(bi, dc), src=src: e.dma_start_transpose(d, src),
                  waits=w_l if dc == 0 else (), inc=ld, inck=16)
        ld_step = ld.n

        if st == 0:
            p.add("vector", lambda e: e.memset(carry[:], 0.0), inc=v)

        if step == 0:
            bank_evac = {}
            nsq_t = {}
            vh_store = {}
        for wi, (wsb, dstq, nsqbank) in enumerate(((wq, qsb, 4), (wkv, kvsb, 5))):
            for ec in range(16):
                bidx = wi * 2 + (ec % 2)
                bank = P[bidx]
                w_mm = [(ld, ld_step)]
                if bidx in bank_evac:
                    w_mm.append((s, bank_evac[bidx]))
                for dc in range(16):
                    lhsT = wsb[:, dc * 2048 + ec * 128: dc * 2048 + (ec + 1) * 128]
                    p.add("tensor", lambda e, bank=bank, lhsT=lhsT, dc=dc, bi=bi: e.matmul(
                        bank[:], lhsT, xT_ap(bi, dc), start=(dc == 0), stop=(dc == 15)),
                        waits=w_mm if dc == 0 else (), inc=t if dc == 15 else None)
                mm_done = t.n
                p.add("scalar", lambda e, bank=bank, dstq=dstq, ec=ec: e.copy(
                    dstq[:, ec * 512:(ec + 1) * 512], bank[:]), waits=[(t, mm_done)], inc=s)
                w_sq = [(t, nsq_t[ec % 2])] if (ec % 2) in nsq_t else []
                p.add("scalar", lambda e, bank=bank, ec=ec: e.activation(
                    sqb[:, (ec % 2) * 512:(ec % 2 + 1) * 512], bank[:], AF.Square),
                    waits=w_sq, inc=s)
                sq_done = s.n
                bank_evac[bidx] = sq_done
                p.add("tensor", lambda e, nb=P[nsqbank], ec=ec: e.matmul(
                    nb[0:16, :], onescol[:, ec * 16:(ec + 1) * 16],
                    sqb[:, (ec % 2) * 512:(ec % 2 + 1) * 512],
                    start=(ec == 0), stop=(ec == 15)),
                    waits=[(s, sq_done)], inc=t)
                nsq_t[ec % 2] = t.n
        nsq_done = t.n

        for nsqbank, rdst in ((4, rstdq), (5, rstdkv)):
            p.add("vector", lambda e, nb=P[nsqbank]: e.tensor_scalar_mul(
                rtmp[:], nb[0:16, :], 128.0), waits=[(t, nsq_done)], inc=v)
            p.add("scalar", lambda e: e.activation(rtmp[:], rtmp[:], AF.Ln),
                  waits=[(v, v.n)], inc=s)
            p.add("vector", lambda e: e.tensor_scalar_mul(rtmp[:], rtmp[:], -0.5),
                  waits=[(s, s.n)], inc=v)
            p.add("scalar", lambda e: e.activation(rtmp[:], rtmp[:], AF.Exp),
                  waits=[(v, v.n)], inc=s)
            p.add("vector", lambda e, rdst=rdst: e.tensor_copy(rdst[:], rtmp[:]),
                  waits=[(s, s.n)], inc=v)
        rstd_done = v.n

        for ec in range(16):
            bcp = P[0]
            p.add("tensor", lambda e, ec=ec: e.matmul(
                bcp[:], hsel[:, ec * 128:(ec + 1) * 128], rstdq[:], start=True, stop=True),
                waits=[(v, v.n)], inc=t)
            p.add("vector", lambda e: e.tensor_copy(bcb[:], bcp[:]), waits=[(t, t.n)])
            p.add("vector", lambda e, ec=ec: e.tensor_tensor(
                qn[:], qsb[:, ec * 512:(ec + 1) * 512], bcb[:], ALU.mult))
            qn_done = v.n
            p.add("tensor", lambda e: e.matmul(P[1][:], ffre[:], qn[:], start=True, stop=True),
                  waits=[(v, qn_done)], inc=t)
            p.add("tensor", lambda e: e.matmul(P[2][:], ffim[:], qn[:], start=True, stop=True),
                  inc=t)
            qf_mm = t.n
            p.add("scalar", lambda e: e.copy(qfre[:], P[1][:]), waits=[(t, qf_mm)], inc=s)
            p.add("scalar", lambda e: e.copy(qfim[:], P[2][:]), inc=s)
            qf_ev = s.n
            p.add("tensor", lambda e, ec=ec: e.matmul(
                bcp[:], hsel[:, ec * 128:(ec + 1) * 128], rstdkv[:], start=True, stop=True),
                waits=[(v, qn_done)], inc=t)
            p.add("vector", lambda e: e.tensor_copy(bcb[:], bcp[:]), waits=[(t, t.n)])
            p.add("vector", lambda e, ec=ec: e.tensor_tensor(
                kvn[:], kvsb[:, ec * 512:(ec + 1) * 512], bcb[:], ALU.mult))
            p.add("vector", lambda e, ec=ec: e.tensor_tensor_scan(
                kvcs[:], onesd[:], kvn[:], carry[:, ec:ec + 1], ALU.mult, ALU.add), inc=v)
            p.add("vector", lambda e, ec=ec: e.tensor_copy(carry[:, ec:ec + 1], kvcs[:, 511:512]), inc=v)
            p.add("vector", lambda e: e.tensor_copy(kvcsb[:], kvcs[:]), inc=v)
            kvcs_done = v.n
            p.add("tensor", lambda e: e.matmul(P[3][:], ffre[:], kvcsb[:], start=True, stop=True),
                  waits=[(v, kvcs_done)], inc=t)
            p.add("tensor", lambda e: e.matmul(P[4][:], ffim[:], kvcsb[:], start=True, stop=True),
                  inc=t)
            kf_mm = t.n
            p.add("scalar", lambda e: e.copy(kfre[:], P[3][:]), waits=[(t, kf_mm)], inc=s)
            p.add("scalar", lambda e: e.copy(kfim[:], P[4][:]), inc=s)
            kf_ev = s.n
            p.add("vector", lambda e: e.tensor_tensor(tmpc[:], qfre[:], kfre[:], ALU.mult),
                  waits=[(s, kf_ev)], inc=v)
            p.add("vector", lambda e: e.tensor_tensor(qvre[:], qfim[:], kfim[:], ALU.mult), inc=v)
            p.add("vector", lambda e: e.tensor_tensor(qvre[:], tmpc[:], qvre[:], ALU.subtract), inc=v)
            p.add("vector", lambda e: e.tensor_tensor(tmpc[:], qfre[:], kfim[:], ALU.mult), inc=v)
            p.add("vector", lambda e: e.tensor_tensor(qvim[:], qfim[:], kfre[:], ALU.mult), inc=v)
            p.add("vector", lambda e: e.tensor_tensor(qvim[:], tmpc[:], qvim[:], ALU.add), inc=v)
            qv_done = v.n
            p.add("tensor", lambda e: e.matmul(P[5][:], fire[:], qvre[:], start=True, stop=False),
                  waits=[(v, qv_done)], inc=None)
            p.add("tensor", lambda e: e.matmul(P[5][:], fiim[:], qvim[:], start=False, stop=True),
                  inc=t)
            vh_mm = t.n
            w_vh = [(t, vh_mm)]
            if (ec % 2) in vh_store:
                w_vh.append((g, vh_store[ec % 2]))
            p.add("scalar", lambda e, ec=ec: e.copy(
                vhsb[:, (ec % 2) * 512:(ec % 2 + 1) * 512], P[5][:]),
                waits=w_vh, inc=s)
            vh_ev = s.n
            dst = dr["vhT"][b, st, ec * 128:(ec + 1) * 128, :]
            p.add("gpsimd", lambda e, dst=dst, ec=ec: e.dma_start(
                out=dst, in_=vhsb[:, (ec % 2) * 512:(ec % 2 + 1) * 512]),
                waits=[(s, vh_ev)], inc=g, inck=16)
            vh_store[ec % 2] = g.n
        step_t_end[step] = t.n
    p.add("gpsimd", lambda e, gn=g.n: e.wait_ge(g.h, gn))



def _ln_stats(ctx, pre_ap, nfree, tl, eps):
    p, v, s = ctx.p, ctx.sem("v"), ctx.sem("s")
    p.add("scalar", lambda e: e.activation(
        tl["dump"][:, 0:nfree], pre_ap, AF.Identity, accum_out=tl["sumb"][:, 0:1]),
        waits=[(v, v.n)])
    p.add("vector", lambda e: e.tensor_scalar_mul(
        tl["negm"][:, 0:1], tl["sumb"][:, 0:1], -1.0 / nfree), waits=[(s, s.n)])
    p.add("scalar", lambda e: e.activation(
        tl["dump"][:, 0:nfree], pre_ap, AF.Square, bias=tl["negm"][:, 0:1],
        accum_out=tl["ssq"][:, 0:1]), waits=[(v, v.n)])
    p.add("vector", lambda e, eps=eps: e.tensor_scalar(
        tl["vpe"][:, 0:1], tl["ssq"][:, 0:1], 1.0 / nfree, eps, ALU.mult, ALU.add),
        waits=[(s, s.n)])
    p.add("scalar", lambda e: e.activation(
        tl["rstd"][:, 0:1], tl["vpe"][:, 0:1], AF.Ln), waits=[(v, v.n)])
    p.add("vector", lambda e: e.tensor_scalar_mul(
        tl["rstd"][:, 0:1], tl["rstd"][:, 0:1], -0.5), waits=[(s, s.n)])
    p.add("scalar", lambda e: e.activation(
        tl["rstd"][:, 0:1], tl["rstd"][:, 0:1], AF.Exp), waits=[(v, v.n)])
    p.add("vector", lambda e: e.tensor_scalar_mul(
        tl["mean"][:, 0:1], tl["sumb"][:, 0:1], 1.0 / nfree), waits=[(s, s.n)])


def ph_a2(ctx):
    nc, p, dr = ctx.nc, ctx.p, ctx.dram
    ld, t, v, s, g = (ctx.sem(n) for n in ("ld", "t", "v", "s", "g"))

    wo = ctx.sb("a2_wo", (128, 16 * 2048), BF16)
    vh = ctx.sb("a2_vh", (128, 16 * 512), BF16)
    ffnw = ctx.sb("a2_ffnw", (128, 16 * 208), BF16)
    dgn = ctx.sb("a2_dgn", (72, 2048), BF16)
    k1 = ctx.sb("a2_k1", (1, 136), BF16)
    g0 = ctx.sb("a2_g0", (1, 8), F32)
    g1 = ctx.sb("a2_g1", (1, 8), F32)
    slope_b = ctx.sb("a2_slb", (128, 8), F32)
    slc_b = ctx.sb("a2_slc", (128, 8), F32)
    onesr = ctx.sb("a2_onesr", (1, 128), BF16)
    onesrf = ctx.sb("a2_onesrf", (1, 128), F32)
    identf = ctx.sb("a2_idf", (128, 128), F32)
    identb = ctx.sb("a2_idb", (128, 128), BF16)
    onesf = ctx.sb("a2_onesf", (128, 8), F32)
    xf = ctx.sb("a2_xf", (128, 2 * 2048), F32)
    pre = ctx.sb("a2_pre", (128, 2048), F32)
    h1f = ctx.sb("a2_h1", (128, 2048), F32)
    dump = ctx.sb("a2_dump", (128, 2048), F32)
    xhi = ctx.sb("a2_xhi", (128, 2048), BF16)
    xlo = ctx.sb("a2_xlo", (128, 2048), BF16)
    tpf = ctx.sb("a2_tpf", (128, 128), F32)
    tpf2 = ctx.sb("a2_tpf2", (128, 128), F32)
    sel = ctx.sb("a2_sel", (128, 136), F32)
    h2 = ctx.sb("a2_h2", (128, 2048), F32)
    h2b = ctx.sb("a2_h2b", (128, 2048), BF16)
    hext = ctx.sb("a2_hext", (128, 72), BF16)
    hexT = ctx.sb("a2_hexT", (72, 128), BF16)
    sm = {k: ctx.sb("a2_" + k, (128, 8), F32) for k in
          ("sumb", "negm", "ssq", "vpe", "rstd", "mean", "sumx2", "r2", "rr",
           "s1", "score", "z", "expz", "gate", "gmax", "ge", "pfx", "eq1",
           "oh", "s2m", "macc")}
    sm["dump"] = dump
    d2 = ctx.sb("a2_d2", (128, 64), F32)
    tmp512 = ctx.sb("a2_tmp512", (128, 512), F32)

    P = [ctx.ps(f"a2_ps{i}", (128, 512)) for i in range(7)]
    ps_tb = ctx.ps("a2_pstb", (128, 1024), BF16)

    for dst, nmc in ((onesr, "c_onesr"), (onesrf, "c_onesrf"), (identf, "c_identf"),
                     (identb, "c_ident")):
        p.add("sync", lambda e, dst=dst, nmc=nmc: e.dma_start(out=dst[:], in_=dr[nmc][:, :]),
              inc=ld, inck=16)
    p.add("sync", lambda e: e.dma_start(out=onesf[:], in_=dr["c_onesf"][:, 0:8],
                                        allow_slow_non_contiguous=True), inc=ld, inck=16)
    p.add("sync", lambda e: e.dma_start(out=ffnw[:], in_=dr["ffnw_s"][:, :]), inc=ld, inck=16)
    p.add("sync", lambda e: e.dma_start(out=dgn[:], in_=dr["dgn_s"][:, :]), inc=ld, inck=16)
    p.add("sync", lambda e: e.dma_start(out=k1[:], in_=dr["k1_s"][:, :]), inc=ld, inck=16)
    p.add("sync", lambda e: e.dma_start(out=g0[:], in_=dr["gate_s"][0:1, :]), inc=ld, inck=16)
    p.add("sync", lambda e: e.dma_start(out=g1[:], in_=dr["gate_s"][1:2, :]), inc=ld, inck=16)
    for dc in range(16):
        for eb in range(4):
            src = dr["wobf"][eb * 512:(eb + 1) * 512, dc * 128:(dc + 1) * 128]
            dst = wo[:, dc * 2048 + eb * 512: dc * 2048 + (eb + 1) * 512]
            p.add("sync", lambda e, dst=dst, src=src: e.dma_start_transpose(dst, src),
                  inc=ld, inck=16)
    const_ld = ld.n
    p.add("tensor", lambda e: e.matmul(P[4][:, 0:8], onesrf[:], g0[:], start=True, stop=True),
          waits=[(ld, const_ld)], inc=t)
    p.add("tensor", lambda e: e.matmul(P[5][:, 0:8], onesrf[:], g1[:], start=True, stop=True),
          inc=t)
    p.add("vector", lambda e: e.tensor_copy(slope_b[:], P[4][:, 0:8]), waits=[(t, t.n)])
    p.add("vector", lambda e: e.tensor_copy(slc_b[:], P[5][:, 0:8]))
    gates_v = v.n

    vh_t_guard = 0
    prev_store = {}
    for step in range(B_LOC * NST):
        b, st = divmod(step, NST)
        w_l = [(ld, const_ld)]
        if step > 0:
            w_l.append((t, vh_t_guard))
        for ec in range(16):
            src = dr["vhT"][b, st, ec * 128:(ec + 1) * 128, :]
            p.add("sync", lambda e, ec=ec, src=src: e.dma_start(
                out=vh[:, ec * 512:(ec + 1) * 512], in_=src),
                waits=w_l if ec == 0 else (), inc=ld, inck=16)
        vh_ld = ld.n
        for tt in range(4):
            row0 = st * 512 + tt * 128
            w_x = [(v, v.n)] if step + tt > 0 else []
            p.add("sync", lambda e, b=b, row0=row0, tt=tt: e.dma_start(
                out=xf[:, (tt % 2) * 2048:(tt % 2 + 1) * 2048],
                in_=dr["x"][b, row0:row0 + 128, :]), waits=w_x, inc=ld, inck=16)
            x_ld = ld.n
            for dc in range(4):
                for ec in range(16):
                    lhsT = vh[:, ec * 512 + tt * 128: ec * 512 + tt * 128 + 128]
                    rhs = wo[:, ec * 2048 + dc * 512: ec * 2048 + (dc + 1) * 512]
                    p.add("tensor", lambda e, dc=dc, lhsT=lhsT, rhs=rhs, ec=ec: e.matmul(
                        P[dc][:], lhsT, rhs, start=(ec == 0), stop=(ec == 15)),
                        waits=[(ld, vh_ld), (v, v.n)] if ec == 0 else (),
                        inc=t if ec == 15 else None)
            wo_done = t.n
            vh_t_guard = t.n
            for dc in range(4):
                p.add("vector", lambda e, dc=dc, tt=tt: e.tensor_tensor(
                    pre[:, dc * 512:(dc + 1) * 512],
                    xf[:, (tt % 2) * 2048 + dc * 512:(tt % 2) * 2048 + (dc + 1) * 512],
                    P[dc][:], ALU.add), waits=[(t, wo_done), (ld, x_ld)])
            _ln_stats(ctx, pre[:], 2048, sm, EPS_LN)
            p.add("vector", lambda e: e.reciprocal(sm["sumx2"][:, 0:1], sm["vpe"][:, 0:1]))
            p.add("vector", lambda e: e.tensor_tensor(
                sm["sumx2"][:, 0:1], sm["ssq"][:, 0:1], sm["sumx2"][:, 0:1], ALU.mult))
            p.add("vector", lambda e: e.tensor_scalar(
                h1f[:], pre[:], sm["mean"][:, 0:1], sm["rstd"][:, 0:1],
                ALU.subtract, ALU.mult))
            h1_done = v.n
            for dc in range(16):
                bank = P[4 + dc % 2]
                p.add("tensor", lambda e, dc=dc, bank=bank: e.transpose(
                    bank[:, 0:128], h1f[:, dc * 128:(dc + 1) * 128], identf[:]),
                    waits=[(v, v.n)], inc=t)
                p.add("vector", lambda e, bank=bank: e.tensor_copy(tpf[:], bank[:, 0:128]),
                      waits=[(t, t.n)])
                p.add("vector", lambda e, dc=dc: e.tensor_copy(
                    xhi[:, dc * 128:(dc + 1) * 128], tpf[:]))
                p.add("vector", lambda e, dc=dc: e.tensor_copy(
                    tpf2[:], xhi[:, dc * 128:(dc + 1) * 128]))
                p.add("vector", lambda e, dc=dc: e.tensor_tensor(
                    xlo[:, dc * 128:(dc + 1) * 128], tpf[:], tpf2[:], ALU.subtract))
            xsplit_done = v.n
            nsel = 0
            for pas, (xsrc, c0, c1) in enumerate(
                    ((xhi, 0, 136), (xhi, 136, 208), (xlo, 0, 72))):
                for dc in range(16):
                    lhsT = xsrc[:, dc * 128:(dc + 1) * 128]
                    rhs = ffnw[:, dc * 208 + c0: dc * 208 + c1]
                    odst = P[6][:, 0:c1 - c0]
                    p.add("tensor", lambda e, lhsT=lhsT, rhs=rhs, odst=odst,
                          first=(nsel == 0): e.matmul(
                        odst, lhsT, rhs, start=first, stop=False),
                        waits=[(v, xsplit_done)] if nsel == 0 else ())
                    nsel += 1
            p.add("tensor", lambda e: e.matmul(
                P[6][:, 0:136], onesr[:], k1[:], start=False, stop=True), inc=t)
            sel_done = t.n
            p.add("vector", lambda e: e.tensor_copy(sel[:], P[6][:, 0:136]),
                  waits=[(t, sel_done)])
            p.add("vector", lambda e: e.tensor_scalar(
                sm["r2"][:], sel[:, 64:72], sm["sumx2"][:, 0:1], None, ALU.add))
            p.add("vector", lambda e: e.reciprocal(sm["rr"][:], sm["r2"][:]))
            p.add("vector", lambda e: e.tensor_tensor(d2[:], sel[:, 0:64], sel[:, 0:64], ALU.mult))
            p.add("vector", lambda e: e.tensor_reduce(
                sm["s1"][:].unsqueeze(2), d2[:].rearrange("p (n s_) -> p n s_", s_=8),
                AX.X, ALU.add))
            p.add("vector", lambda e: e.tensor_tensor(sm["score"][:], sm["s1"][:], sm["rr"][:], ALU.mult))
            p.add("vector", lambda e: e.tensor_tensor(sm["z"][:], sm["score"][:], slope_b[:], ALU.mult),
                  waits=[(v, gates_v)])
            p.add("vector", lambda e: e.tensor_tensor(sm["z"][:], sm["z"][:], slc_b[:], ALU.subtract))
            p.add("scalar", lambda e: e.activation(sm["expz"][:], sm["z"][:], AF.Exp,
                                                   bias=0.0, scale=-1.0), waits=[(v, v.n)])
            p.add("vector", lambda e: e.tensor_scalar_add(sm["gate"][:], sm["expz"][:], 1.0),
                  waits=[(s, s.n)])
            p.add("vector", lambda e: e.reciprocal(sm["gate"][:], sm["gate"][:]))
            p.add("vector", lambda e: e.reduce_max(sm["gmax"][:, 0:1], sm["gate"][:], AX.X))
            p.add("vector", lambda e: e.tensor_single_scalar(
                sm["ge"][:], sm["gate"][:], sm["gmax"][:, 0:1], ALU.is_ge))
            p.add("vector", lambda e: e.tensor_tensor_scan(
                sm["pfx"][:], onesf[:], sm["ge"][:], 0.0, ALU.mult, ALU.add))
            p.add("vector", lambda e: e.tensor_single_scalar(
                sm["eq1"][:], sm["pfx"][:], 1.0, ALU.is_equal))
            p.add("vector", lambda e: e.tensor_tensor(sm["oh"][:], sm["ge"][:], sm["eq1"][:], ALU.mult))
            p.add("vector", lambda e: e.tensor_scalar(
                sm["s2m"][:, 0:1], sm["gmax"][:, 0:1], -1.0, 2.0, ALU.mult, ALU.add))
            p.add("vector", lambda e: e.tensor_scalar(
                sm["macc"][:], sel[:, 72:80], sm["oh"][:, 0:1], None, ALU.mult))
            for n in range(1, 8):
                p.add("vector", lambda e, n=n: e.scalar_tensor_tensor(
                    sm["macc"][:], sel[:, 72 + 8 * n: 80 + 8 * n], sm["oh"][:, n:n + 1],
                    sm["macc"][:], ALU.mult, ALU.add))
            for n in range(8):
                p.add("vector", lambda e, n=n: e.tensor_scalar(
                    hext[:, 8 * n:8 * n + 8], sm["macc"][:], sm["oh"][:, n:n + 1],
                    None, ALU.mult))
            p.add("vector", lambda e: e.tensor_copy(hext[:, 64:72], sm["oh"][:]))
            hext_done = v.n
            p.add("tensor", lambda e: e.transpose(ps_tb[0:72, 0:128], hext[:], identb[:]),
                  waits=[(v, hext_done)], inc=t)
            p.add("vector", lambda e: e.tensor_copy(hexT[:], ps_tb[0:72, 0:128]),
                  waits=[(t, t.n)])
            hexT_done = v.n
            for dc in range(4):
                p.add("tensor", lambda e, dc=dc: e.matmul(
                    P[dc][:], hexT[:], dgn[:, dc * 512:(dc + 1) * 512],
                    start=True, stop=True),
                    waits=[(v, hexT_done)], inc=t)
            mod_done = t.n
            for dc in range(4):
                p.add("vector", lambda e, dc=dc: e.tensor_scalar(
                    tmp512[:], h1f[:, dc * 512:(dc + 1) * 512], sm["s2m"][:, 0:1],
                    None, ALU.mult), waits=[(t, mod_done)] if dc == 0 else ())
                p.add("vector", lambda e, dc=dc: e.scalar_tensor_tensor(
                    pre[:, dc * 512:(dc + 1) * 512], P[dc][:], sm["gmax"][:, 0:1],
                    tmp512[:], ALU.mult, ALU.add))
            _ln_stats(ctx, pre[:], 2048, sm, EPS_LN)
            p.add("vector", lambda e: e.tensor_scalar(
                h2[:], pre[:], sm["mean"][:, 0:1], sm["rstd"][:, 0:1],
                ALU.subtract, ALU.mult))
            p.add("vector", lambda e: e.tensor_copy(h2b[:], h2[:]))
            h2_done = v.n
            w_st = [(v, h2_done)]
            p.add("gpsimd", lambda e, b=b, row0=row0: e.dma_start(
                out=dr["h2f"][b, row0:row0 + 128, :], in_=h2[:]), waits=w_st, inc=g, inck=16)
            p.add("gpsimd", lambda e, b=b, row0=row0: e.dma_start(
                out=dr["h2bf"][b, row0:row0 + 128, :], in_=h2b[:]), waits=[(v, h2_done)],
                inc=g, inck=16)
            p.add("vector", lambda e, gn=g.n: e.nop(), waits=[(g, g.n)])
    p.add("gpsimd", lambda e, gn=g.n: e.wait_ge(g.h, gn))



def ph_b1(ctx):
    nc, p, dr = ctx.nc, ctx.p, ctx.dram
    ld, t, v, s, g = (ctx.sem(n) for n in ("ld", "t", "v", "s", "g"))
    wk = ctx.sb("b1_wk", (128, 16 * 3072), BF16)
    h2T = ctx.sb("b1_h2T", (128, 2 * 16 * 512), BF16)
    ev = ctx.sb("b1_ev", (128, 4 * 512), BF16)
    P = [ctx.ps(f"b1_ps{i}", (128, 512)) for i in range(4)]

    ev_guard = {}
    ev_g = {}
    slot_guard = {}
    for half in range(2):
        e0 = half * 3072
        w_l = [(t, t.n)]
        for dc in range(16):
            for eb in range(6):
                src = dr["wqkvbf"][e0 + eb * 512: e0 + (eb + 1) * 512,
                                   dc * 128:(dc + 1) * 128]
                dst = wk[:, dc * 3072 + eb * 512: dc * 3072 + (eb + 1) * 512]
                p.add("sync", lambda e, dst=dst, src=src: e.dma_start_transpose(dst, src),
                      waits=w_l if (dc == 0 and eb == 0) else (), inc=ld, inck=16)
        wk_ld = ld.n
        for step in range(B_LOC * NST):
            b, st = divmod(step, NST)
            idx = half * 8 + step
            sl = idx % 2
            lh = ctx.sem("ld2" if sl == 0 else "ld3")
            hsl = h2T[:, sl * 8192:(sl + 1) * 8192]
            w_h = [(t, slot_guard.get(sl, 0))]
            for dc in range(16):
                src = dr["h2bf"][b, st * 512:(st + 1) * 512, dc * 128:(dc + 1) * 128]
                p.add("sync", lambda e, dc=dc, src=src, hsl=hsl: e.dma_start_transpose(
                    hsl[:, dc * 512:(dc + 1) * 512], src),
                    waits=w_h if dc == 0 else (), inc=lh, inck=16)
            h2_ld = lh.n
            for tt in range(4):
                row0 = st * 512 + tt * 128
                for eb in range(6):
                    bank = P[eb % 4]
                    w_mm = [(lh, h2_ld), (ld, wk_ld)]
                    if (eb % 4) in ev_guard:
                        w_mm.append((s, ev_guard[eb % 4]))
                    for dc in range(16):
                        lhsT = hsl[:, dc * 512 + tt * 128: dc * 512 + tt * 128 + 128]
                        rhs = wk[:, dc * 3072 + eb * 512: dc * 3072 + (eb + 1) * 512]
                        p.add("tensor", lambda e, bank=bank, lhsT=lhsT, rhs=rhs, dc=dc: e.matmul(
                            bank[:], lhsT, rhs, start=(dc == 0), stop=(dc == 15)),
                            waits=w_mm if dc == 0 else (), inc=t if dc == 15 else None)
                    mm_done = t.n
                    w_ev = [(t, mm_done)]
                    p.add("scalar", lambda e, bank=bank, eb=eb: e.copy(
                        ev[:, (eb % 4) * 512:(eb % 4 + 1) * 512], bank[:]),
                        waits=w_ev + ([(g, ev_g[eb % 4])] if (eb % 4) in ev_g else []))
                    ev_guard[eb % 4] = s.n
                    dst = dr["qkvbf"][b, row0:row0 + 128,
                                      e0 + eb * 512: e0 + (eb + 1) * 512]
                    p.add("gpsimd", lambda e, dst=dst, eb=eb: e.dma_start(
                        out=dst, in_=ev[:, (eb % 4) * 512:(eb % 4 + 1) * 512]),
                        waits=[(s, s.n)], inc=g, inck=16)
                    ev_g[eb % 4] = g.n
            slot_guard[sl] = t.n
    p.add("gpsimd", lambda e, gn=g.n: e.wait_ge(g.h, gn))



def ph_b2(ctx):
    nc, p, dr = ctx.nc, ctx.p, ctx.dram
    ld, t, v, s, g = (ctx.sem(n) for n in ("ld", "t", "v", "s", "g"))
    SC = 1.0 / np.sqrt(128.0)

    wo = ctx.sb("b2_wo", (128, 16 * 2048), BF16)
    atn = ctx.sb("b2_atn", (128, 16 * 2048), BF16)
    qT = ctx.sb("b2_qT", (128, 2 * 2048), BF16)
    kT = ctx.sb("b2_kT", (128, 2 * 2048), BF16)
    vv = ctx.sb("b2_vv", (128, 2 * 2048), BF16)
    pt = ctx.sb("b2_pt", (128, 2 * 1024), BF16)
    rsum = ctx.sb("b2_rs", (1, 512), F32)
    rsumb = ctx.sb("b2_rsb", (1, 512), BF16)
    rbc = ctx.sb("b2_rbc", (128, 512), F32)
    onesr = ctx.sb("b2_onesr", (1, 128), BF16)
    onesc = ctx.sb("b2_onesc", (128, 1), BF16)
    h2t = ctx.sb("b2_h2t", (128, 2048), F32)
    pre = ctx.sb("b2_pre", (128, 2048), F32)
    dump = ctx.sb("b2_dump", (128, 2048), F32)
    h3 = ctx.sb("b2_h3", (128, 2048), F32)
    h3b = ctx.sb("b2_h3b", (128, 2048), BF16)
    sm = {k: ctx.sb("b2s_" + k, (128, 1), F32) for k in
          ("sumb", "negm", "ssq", "vpe", "rstd", "mean")}
    sm["dump"] = dump

    SC4 = [ctx.ps(f"b2_sc{i}", (128, 512)) for i in range(4)]
    SCB = [None, None]
    PVB = [ctx.ps(f"b2_pv{i}", (128, 512)) for i in range(2)]
    RS = ctx.ps("b2_rsp", (128, 512))
    BC = ctx.ps("b2_bcp", (128, 512))

    p.add("sync", lambda e: e.dma_start(out=onesr[:], in_=dr["c_onesr"][:, :]), inc=ld, inck=16)
    p.add("sync", lambda e: e.dma_start(out=onesc[:], in_=dr["c_ones"][:, 0:1],
                                        allow_slow_non_contiguous=True), inc=ld, inck=16)
    for dc in range(16):
        for eb in range(4):
            src = dr["wo1bf"][eb * 512:(eb + 1) * 512, dc * 128:(dc + 1) * 128]
            dst = wo[:, dc * 2048 + eb * 512: dc * 2048 + (eb + 1) * 512]
            p.add("sync", lambda e, dst=dst, src=src: e.dma_start_transpose(dst, src),
                  inc=ld, inck=16)
    const_ld = ld.n

    qkv_flat = dr["qkvbf"].rearrange("b s e -> b (s e)").rearrange(
        "b (th hh l i) -> b th hh l i", th=3, hh=16, l=2048)

    slot_guard = {}
    sc_guard = {}
    sc_guard_v = {}
    pt_guard = {}
    pv_guard = {}
    bc_guard = 0
    rs_guard = 0
    atn_guard = 0
    preadd_guard = 0
    for b in range(B_LOC):
        for h in range(16):
            bi = h % 2
            lq = ctx.sem("ld2" if bi == 0 else "ld3")
            w_l = [(ld, const_ld), (t, slot_guard.get(bi, 0))]
            p.add("sync", lambda e, b=b, h=h, bi=bi: e.dma_start_transpose(
                qT[:, bi * 2048:(bi + 1) * 2048], qkv_flat[b, 0, h]),
                waits=w_l, inc=lq, inck=16)
            p.add("sync", lambda e, b=b, h=h, bi=bi: e.dma_start_transpose(
                kT[:, bi * 2048:(bi + 1) * 2048], qkv_flat[b, 1, h]),
                inc=lq, inck=16)
            p.add("sync", lambda e, b=b, h=h, bi=bi: e.dma_start(
                out=vv[:, bi * 2048:(bi + 1) * 2048].rearrange("p (kc i) -> p kc i", i=128),
                in_=qkv_flat[b, 2, h].rearrange("(kc pp) i -> pp kc i", pp=128)),
                inc=lq, inck=16)
            qkv_ld = lq.n
            p.add("vector", lambda e, bi=bi: e.tensor_scalar_mul(
                qT[:, bi * 2048:(bi + 1) * 2048], qT[:, bi * 2048:(bi + 1) * 2048], SC),
                waits=[(lq, qkv_ld)], chain=False)
            q_ready = v.n

            for qg in range(4):
                pvb = PVB[qg % 2]
                qsl = qT[:, bi * 2048 + qg * 512: bi * 2048 + (qg + 1) * 512]

                def scores(kp):
                    scb = SCB[kp % 2]
                    w0 = [(s, sc_guard.get(kp % 2, 0)),
                          (v, sc_guard_v.get(kp % 2, 0))]
                    if qg == 0 and kp <= 1:
                        w0.append((v, q_ready))
                    w0.append((lq, qkv_ld))
                    for half in range(2):
                        kc = 2 * kp + half
                        sc4 = SC4[2 * (kp % 2) + half]
                        p.add("tensor", lambda e, kc=kc, sc4=sc4, bi=bi, qsl=qsl: e.matmul(
                            sc4[:],
                            kT[:, bi * 2048 + kc * 128: bi * 2048 + kc * 128 + 128],
                            qsl, start=True, stop=True),
                            waits=w0 if half == 0 else (), inc=t if half == 1 else None)
                    return t.n

                sc_done = {0: scores(0)}
                for kp in range(8):
                    if kp + 1 < 8:
                        sc_done[kp + 1] = scores(kp + 1)
                    w_exp = [(t, sc_done[kp]), (t, pt_guard.get(kp % 2, 0))]
                    p.add("scalar", lambda e, kp=kp: e.activation(
                        pt[:, (kp % 2) * 1024:(kp % 2) * 1024 + 512],
                        SC4[2 * (kp % 2)][:], AF.Exp), waits=w_exp)
                    p.add("scalar", lambda e, kp=kp: e.activation(
                        pt[:, (kp % 2) * 1024 + 512:(kp % 2 + 1) * 1024],
                        SC4[2 * (kp % 2) + 1][:], AF.Exp))
                    exp_done = s.n
                    sc_guard[kp % 2] = exp_done
                    for half in range(2):
                        kc = 2 * kp + half
                        w_pv = [(s, exp_done)]
                        if kp == 0 and half == 0:
                            w_pv.append((v, pv_guard.get(qg % 2, 0)))
                        p.add("tensor", lambda e, kc=kc, kp=kp, half=half, pvb=pvb, bi=bi: e.matmul(
                            pvb[:], vv[:, bi * 2048 + kc * 128: bi * 2048 + kc * 128 + 128],
                            pt[:, (kp % 2) * 1024 + half * 512:(kp % 2) * 1024 + (half + 1) * 512],
                            start=(kp == 0 and half == 0), stop=(kp == 7 and half == 1)),
                            waits=w_pv if half == 0 else (), inc=t if half == 1 else None)
                    for half in range(2):
                        w_rs = [(v, rs_guard)] if (kp == 0 and half == 0) else ()
                        p.add("tensor", lambda e, kp=kp, half=half: e.matmul(
                            RS[0:1, :], onesc[:],
                            pt[:, (kp % 2) * 1024 + half * 512:(kp % 2) * 1024 + (half + 1) * 512],
                            start=(kp == 0 and half == 0), stop=(kp == 7 and half == 1)),
                            waits=w_rs, inc=t if half == 1 else None)
                    pt_guard[kp % 2] = t.n
                pv_done = t.n
                p.add("vector", lambda e: e.reciprocal(rsum[:], RS[0:1, :]),
                      waits=[(t, pv_done)])
                rs_guard = v.n
                p.add("vector", lambda e: e.tensor_copy(rsumb[:], rsum[:]))
                rs_done = v.n
                p.add("tensor", lambda e: e.matmul(
                    BC[:], onesr[:], rsumb[:], start=True, stop=True),
                    waits=[(v, rs_done), (v, bc_guard)], inc=t)
                bc_done = t.n
                p.add("vector", lambda e: e.tensor_copy(rbc[:], BC[:]),
                      waits=[(t, bc_done)])
                bc_guard = v.n
                p.add("vector", lambda e, h=h, qg=qg, pvb=pvb: e.tensor_tensor(
                    atn[:, h * 2048 + qg * 512: h * 2048 + (qg + 1) * 512],
                    pvb[:], rbc[:], ALU.mult), waits=[(t, atn_guard)])
                pv_guard[qg % 2] = v.n
            slot_guard[bi] = t.n
        norm_all = v.n
        exp_all = s.n
        import os as _os
        if _os.environ.get("B2_DBG") and b == 0:
            p.add("gpsimd", lambda e: e.dma_start(out=dr["dbg"][:, :], in_=atn[:]),
                  waits=[(v, norm_all)], inc=g, inck=16)
        WOB = [SC4[0][:], SC4[1][:], SC4[2][:], SC4[3][:]]
        for tt in range(16):
            row0 = tt * 128
            p.add("sync", lambda e, b=b, row0=row0: e.dma_start(
                out=h2t[:], in_=dr["h2f"][b, row0:row0 + 128, :]),
                waits=[(v, preadd_guard), (ld, const_ld)],
                inc=ld, inck=16)
            h2_ld = ld.n
            import os as _os
            if _os.environ.get("B2_STRONG"):
                w_wo = [(v, v.n), (s, s.n)]
            elif tt == 0:
                w_wo = [(v, norm_all), (s, exp_all), (v, pv_guard.get(0, 0)),
                        (v, pv_guard.get(1, 0)), (v, rs_guard), (v, bc_guard)]
            else:
                w_wo = [(v, preadd_guard)]
            for dc in range(4):
                for ec in range(16):
                    lhsT = atn[:, ec * 2048 + tt * 128: ec * 2048 + tt * 128 + 128]
                    rhs = wo[:, ec * 2048 + dc * 512: ec * 2048 + (dc + 1) * 512]
                    p.add("tensor", lambda e, dc=dc, lhsT=lhsT, rhs=rhs, ec=ec: e.matmul(
                        WOB[dc], lhsT, rhs, start=(ec == 0), stop=(ec == 15)),
                        waits=w_wo if (ec == 0 and dc == 0) else (),
                        inc=t if ec == 15 else None)
            wo_done = t.n
            for dc in range(4):
                p.add("vector", lambda e, dc=dc: e.tensor_tensor(
                    pre[:, dc * 512:(dc + 1) * 512], h2t[:, dc * 512:(dc + 1) * 512],
                    WOB[dc], ALU.add), waits=[(t, wo_done), (ld, h2_ld)])
            preadd_guard = v.n
            _ln_stats(ctx, pre[:], 2048, sm, EPS_LN)
            p.add("vector", lambda e: e.tensor_scalar(
                h3[:], pre[:], sm["mean"][:, 0:1], sm["rstd"][:, 0:1],
                ALU.subtract, ALU.mult))
            p.add("vector", lambda e: e.tensor_copy(h3b[:], h3[:]))
            h3_done = v.n
            p.add("gpsimd", lambda e, b=b, row0=row0: e.dma_start(
                out=dr["h3f"][b, row0:row0 + 128, :], in_=h3[:]),
                waits=[(v, h3_done)], inc=g, inck=16)
            p.add("gpsimd", lambda e, b=b, row0=row0: e.dma_start(
                out=dr["h3bf"][b, row0:row0 + 128, :], in_=h3b[:]),
                waits=[(v, h3_done)], inc=g, inck=16)
            p.add("vector", lambda e: e.nop(), waits=[(g, g.n)])
        atn_guard = t.n
        for kk in (0, 1):
            pt_guard[kk] = t.n
            sc_guard_v[kk] = preadd_guard
    p.add("gpsimd", lambda e, gn=g.n: e.wait_ge(g.h, gn))



def ph_c2a(ctx):
    nc, p, dr = ctx.nc, ctx.p, ctx.dram
    ld, t, v, s, g = (ctx.sem(n) for n in ("ld", "t", "v", "s", "g"))
    h3T = ctx.sb("c2a_h3T", (128, 16 * 4096), BF16)
    w1T = ctx.sb("c2a_w1T", (128, 2 * 16 * 512), BF16)
    hrb = ctx.sb("c2a_hr", (128, 4 * 512), BF16)
    hsq = ctx.sb("c2a_hsq", (128, 512), F32)
    relu_t = ctx.sb("c2a_relu", (128, 512), BF16)
    sqf = ctx.sb("c2a_sqf", (128, 512), F32)
    acc = ctx.sb("c2a_acc", (128, 8 * 512), F32)
    ssr = ctx.sb("c2a_ssr", (1, 512), F32)
    rmsc = ctx.sb("c2a_rmsc", (128, 32), F32)
    identf = ctx.sb("c2a_idf", (128, 128), F32)
    onesc = ctx.sb("c2a_onesc", (128, 1), F32)
    P = [ctx.ps(f"c2a_ps{i}", (128, 512)) for i in range(6)]

    p.add("sync", lambda e: e.dma_start(out=identf[:], in_=dr["c_identf"][:, :]), inc=ld, inck=16)
    p.add("sync", lambda e: e.dma_start(out=onesc[:], in_=dr["c_onesf"][:, 0:1],
                                        allow_slow_non_contiguous=True), inc=ld, inck=16)
    for dc in range(16):
        for step in range(8):
            b, st = divmod(step, NST)
            src = dr["h3bf"][b, st * 512:(st + 1) * 512, dc * 128:(dc + 1) * 128]
            dst = h3T[:, dc * 4096 + step * 512: dc * 4096 + (step + 1) * 512]
            p.add("sync", lambda e, dst=dst, src=src: e.dma_start_transpose(dst, src),
                  inc=ld, inck=16)
    h3_ld = ld.n
    p.add("vector", lambda e: e.memset(acc[:], 0.0))
    acc_z = v.n

    hr_g = {}
    hr_st = {}
    for hb in range(16):
        wsl = w1T[:, (hb % 2) * 8192:(hb % 2 + 1) * 8192]
        w_l = [(t, t.n)] if hb >= 2 else []
        for dc in range(16):
            src = dr["w1bf"][hb * 512:(hb + 1) * 512, dc * 128:(dc + 1) * 128]
            p.add("sync", lambda e, wsl=wsl, dc=dc, src=src: e.dma_start_transpose(
                wsl[:, dc * 512:(dc + 1) * 512], src),
                waits=w_l if dc == 0 else (), inc=ld, inck=16)
        w_ld = ld.n
        for hc4 in range(4):
            hcg = hb * 4 + hc4
            for step in range(8):
                bank = P[step % 2]
                w_mm = [(ld, max(w_ld, h3_ld))]
                if (step % 2) in hr_g:
                    w_mm.append((s, hr_g[step % 2]))
                for dc in range(16):
                    lhsT = wsl[:, dc * 512 + hc4 * 128: dc * 512 + hc4 * 128 + 128]
                    rhs = h3T[:, dc * 4096 + step * 512: dc * 4096 + (step + 1) * 512]
                    p.add("tensor", lambda e, bank=bank, lhsT=lhsT, rhs=rhs, dc=dc: e.matmul(
                        bank[:], lhsT, rhs, start=(dc == 0), stop=(dc == 15)),
                        waits=w_mm if dc == 0 else (), inc=t if dc == 15 else None)
                mm_done = t.n
                bi4 = (hcg * 8 + step) % 4
                p.add("scalar", lambda e, bank=bank: e.activation(
                    hsq[:].bitcast(F32) if False else relu_t[:], bank[:], AF.Relu),
                    waits=[(t, mm_done)])
                hr_g[step % 2] = s.n
                relu_done = s.n
                w_sq = [(s, relu_done), (v, acc_z)]
                p.add("vector", lambda e: e.tensor_tensor(
                    sqf[:], relu_t[:], relu_t[:], ALU.mult), waits=w_sq)
                w_hr = [(g, hr_st[bi4])] if bi4 in hr_st else []
                p.add("vector", lambda e, bi4=bi4: e.tensor_copy(
                    hrb[:, bi4 * 512:(bi4 + 1) * 512], sqf[:]), waits=w_hr)
                sq_done = v.n
                dst = dr["hrs"][hcg, :, step * 512:(step + 1) * 512]
                p.add("gpsimd", lambda e, dst=dst, bi4=bi4: e.dma_start(
                    out=dst, in_=hrb[:, bi4 * 512:(bi4 + 1) * 512]),
                    waits=[(v, sq_done)], inc=g, inck=16)
                hr_st[bi4] = g.n
                p.add("vector", lambda e: e.tensor_tensor(
                    hsq[:], sqf[:], sqf[:], ALU.mult))
                p.add("vector", lambda e, step=step: e.tensor_tensor(
                    acc[:, step * 512:(step + 1) * 512],
                    acc[:, step * 512:(step + 1) * 512], hsq[:], ALU.add))
    acc_done = v.n
    for step in range(8):
        p.add("tensor", lambda e, step=step: e.matmul(
            P[2][0:1, :], onesc[:],
            acc[:, step * 512:(step + 1) * 512], start=True, stop=True),
            waits=[(v, max(acc_done, v.n))], inc=t)
        cs_done = t.n
        p.add("vector", lambda e: e.tensor_copy(ssr[:], P[2][0:1, :]),
              waits=[(t, cs_done)])
        ssr_done = v.n
        for q4 in range(4):
            p.add("tensor", lambda e, q4=q4: e.transpose(
                P[3][:, 0:1], ssr[0:1, q4 * 128:(q4 + 1) * 128], identf[0:1, 0:1]),
                waits=[(v, v.n)], inc=t)
            p.add("vector", lambda e, step=step, q4=q4: e.tensor_copy(
                rmsc[:, step * 4 + q4: step * 4 + q4 + 1], P[3][:, 0:1]),
                waits=[(t, t.n)])
    p.add("vector", lambda e: e.tensor_scalar(
        rmsc[:], rmsc[:], 1.0 / 8192.0, EPS_RMS, ALU.mult, ALU.add))
    p.add("scalar", lambda e: e.activation(rmsc[:], rmsc[:], AF.Ln), waits=[(v, v.n)])
    p.add("vector", lambda e: e.tensor_scalar_mul(rmsc[:], rmsc[:], -0.5), waits=[(s, s.n)])
    p.add("scalar", lambda e: e.activation(rmsc[:], rmsc[:], AF.Exp), waits=[(v, v.n)])
    p.add("gpsimd", lambda e: e.dma_start(out=dr["rms_s"][:, :], in_=rmsc[:]),
          waits=[(s, s.n)], inc=g, inck=16)
    p.add("gpsimd", lambda e, gn=g.n: e.wait_ge(g.h, gn))



def ph_c2b(ctx):
    nc, p, dr = ctx.nc, ctx.p, ctx.dram
    ld, t, v, s, g = (ctx.sem(n) for n in ("ld", "t", "v", "s", "g"))
    w2T = ctx.sb("c2b_w2T", (128, 128 * 512), BF16)
    hrt = ctx.sb("c2b_hrt", (128, 2 * 8192), BF16)
    rmsc = ctx.sb("c2b_rmsc", (128, 32), F32)
    yb = ctx.sb("c2b_y", (128, 4 * 512), BF16)
    P = [ctx.ps(f"c2b_ps{i}", (128, 512)) for i in range(4)]

    p.add("sync", lambda e: e.dma_start(out=rmsc[:], in_=dr["rms_s"][:, :]), inc=ld, inck=16)
    y_st = {}
    for half in range(2):
        d0 = half * 1024
        w_l = [(t, t.n)] if half > 0 else []
        for hc in range(64):
            for dc2 in range(2):
                src = dr["w2bf"][d0 + dc2 * 512: d0 + (dc2 + 1) * 512,
                                 hc * 128:(hc + 1) * 128]
                dst = w2T[:, (hc * 2 + dc2) * 512:(hc * 2 + dc2 + 1) * 512]
                p.add("sync", lambda e, dst=dst, src=src: e.dma_start_transpose(dst, src),
                      waits=w_l if (hc == 0 and dc2 == 0) else (), inc=ld, inck=16)
        w_ld = ld.n
        for tt in range(32):
            hb = hrt[:, (tt % 2) * 8192:(tt % 2 + 1) * 8192]
            w_h = [(ld, w_ld)]
            if tt >= 2:
                w_h.append((t, t.n))
            for hq in range(4):
                src = dr["hrs"][hq * 16:(hq + 1) * 16, :, tt * 128:(tt + 1) * 128]
                dst = hb[:, hq * 2048:(hq + 1) * 2048]
                p.add("sync", lambda e, dst=dst, src=src: e.dma_start(
                    out=dst.rearrange("p (hc i) -> p hc i", i=128),
                    in_=src.transpose([1, 0, 2])),
                    waits=w_h if hq == 0 else (), inc=ld, inck=16)
            hr_ld = ld.n
            for dc2 in range(2):
                bank = P[dc2 % 2]
                w_mm = [(ld, hr_ld)]
                if (dc2 % 2) in y_st:
                    w_mm.append((v, y_st[dc2 % 2][0]))
                for hc in range(64):
                    lhsT = hb[:, hc * 128:(hc + 1) * 128]
                    rhs = w2T[:, (hc * 2 + dc2) * 512:(hc * 2 + dc2 + 1) * 512]
                    p.add("tensor", lambda e, bank=bank, lhsT=lhsT, rhs=rhs, hc=hc: e.matmul(
                        bank[:], lhsT, rhs, start=(hc == 0), stop=(hc == 63)),
                        waits=w_mm if hc == 0 else (), inc=t if hc == 63 else None)
                mm_done = t.n
                w_e = [(t, mm_done)]
                if (dc2 % 2) in y_st:
                    w_e.append((g, y_st[dc2 % 2][1]))
                p.add("vector", lambda e, bank=bank, dc2=dc2, tt=tt: e.tensor_scalar(
                    yb[:, dc2 * 512:(dc2 + 1) * 512], bank[:],
                    rmsc[:, tt:tt + 1], None, ALU.mult), waits=w_e)
                ev_done = v.n
                dst = dr["ysb"][tt * 128:(tt + 1) * 128, d0 + dc2 * 512: d0 + (dc2 + 1) * 512]
                p.add("gpsimd", lambda e, dst=dst, dc2=dc2: e.dma_start(
                    out=dst, in_=yb[:, dc2 * 512:(dc2 + 1) * 512]),
                    waits=[(v, ev_done)], inc=g, inck=16)
                y_st[dc2 % 2] = (ev_done, g.n)
    p.add("gpsimd", lambda e, gn=g.n: e.wait_ge(g.h, gn))



def ph_c3(ctx):
    nc, p, dr = ctx.nc, ctx.p, ctx.dram
    ld, t, v, s, g = (ctx.sem(n) for n in ("ld", "t", "v", "s", "g"))
    yf = ctx.sb("c3_y", (128, 2 * 2048), BF16)
    h3t = ctx.sb("c3_h3", (128, 2 * 2048), F32)
    pre = ctx.sb("c3_pre", (128, 2048), F32)
    dump = ctx.sb("c3_dump", (128, 2048), F32)
    outt = ctx.sb("c3_out", (128, 2048), F32)
    sm = {k: ctx.sb("c3s_" + k, (128, 1), F32) for k in
          ("sumb", "negm", "ssq", "vpe", "rstd", "mean")}
    sm["dump"] = dump

    out_st = 0
    for tt in range(32):
        b, row0 = divmod(tt * 128, S)
        bi = tt % 2
        w_l = [(v, v.n)] if tt >= 2 else []
        p.add("sync", lambda e, tt=tt, bi=bi: e.dma_start(
            out=yf[:, bi * 2048:(bi + 1) * 2048],
            in_=dr["ysb"][tt * 128:(tt + 1) * 128, :]), waits=w_l, inc=ld, inck=16)
        p.add("sync", lambda e, b=b, row0=row0, bi=bi: e.dma_start(
            out=h3t[:, bi * 2048:(bi + 1) * 2048],
            in_=dr["h3f"][b, row0:row0 + 128, :]), inc=ld, inck=16)
        t_ld = ld.n
        p.add("vector", lambda e, bi=bi: e.tensor_copy(
            pre[:], yf[:, bi * 2048:(bi + 1) * 2048]), waits=[(ld, t_ld)])
        p.add("vector", lambda e, bi=bi: e.tensor_tensor(
            pre[:], pre[:], h3t[:, bi * 2048:(bi + 1) * 2048], ALU.add))
        _ln_stats(ctx, pre[:], 2048, sm, EPS_LN)
        p.add("vector", lambda e: e.tensor_scalar(
            outt[:], pre[:], sm["mean"][:, 0:1], sm["rstd"][:, 0:1],
            ALU.subtract, ALU.mult))
        o_done = v.n
        p.add("gpsimd", lambda e, b=b, row0=row0: e.dma_start(
            out=dr["out"][b, row0:row0 + 128, :], in_=outt[:]),
            waits=[(v, o_done)], inc=g, inck=16)
        p.add("vector", lambda e: e.nop(), waits=[(g, g.n)])
    p.add("gpsimd", lambda e, gn=g.n: e.wait_ge(g.h, gn))


def build(group_phases, for_run=True):
    nc = bass.Bass("TRN2", target_bir_lowering=False)
    es = ExitStack()
    ctx = Ctx(nc, es, group_phases)
    prog = ctx.p
    with es:
        in_names, out_names = declare_dram(ctx, group_phases)
        block = es.enter_context(nc.Block())
        first = True
        for ph in group_phases:
            if not first:
                prog.barrier(ctx.sem("bar"))
            first = False
            with ExitStack() as phstack:
                ctx.phase_stack = phstack
                globals()["ph_" + ph](ctx)
                ctx.phase_stack = None
        prog.emit(block)
    return nc, in_names, out_names



N_CORES = 8
GROUPS = [list(ALL_PHASES)]


def kernel(**inputs):
    consts = host_consts()
    xfull = np.ascontiguousarray(inputs["L0_attn_x"])
    per_core = []
    for c in range(N_CORES):
        d = {"x": np.ascontiguousarray(xfull[2 * c:2 * c + 2])}
        for k, v in inputs.items():
            if k != "L0_attn_x":
                d[k] = np.ascontiguousarray(v)
        d.update(consts)
        per_core.append(d)
    outs = None
    for grp in GROUPS:
        nc, in_names, out_names = build(grp)
        in_maps = [{n: per_core[c][n] for n in in_names} for c in range(N_CORES)]
        res = run_bass_kernel_spmd(nc, in_maps, core_ids=list(range(N_CORES)))
        for c in range(N_CORES):
            for n in out_names:
                per_core[c][n] = res.results[c][n]
        if "out" in out_names:
            outs = [res.results[c]["out"] for c in range(N_CORES)]
    return np.concatenate(outs, axis=0)
